# version 05
# baseline (speedup 1.0000x reference)
import numpy as np
import concourse.bass as bass
import concourse.mybir as mybir
from concourse.bass_utils import run_bass_kernel_spmd
from concourse.alu_op_type import AluOpType as A
from contextlib import ExitStack

F32 = mybir.dt.float32
BF16 = mybir.dt.bfloat16
AF = mybir.ActivationFunctionType

D = 1024
NB = D // 128
WINDOW = 128
CHUNK = 64
LOCAL_HEADS = 4
GLOBAL_HEADS = 12
DH = 64
HEADS_B = 8
DF = 128
HID = 4096
NH = HID // 128
MAGIC = float(np.float32(1.5 * 2.0 ** 23))
NEG = -1.0e30
EPS_LN = 1e-5
EPS_RMS = 1e-5
EPS_CTX = 1e-6
INVD = 1.0 / D


class ConstReg:
    def __init__(self):
        self.off = {}
        self.w = {}
        self.n = 0

    def reg(self, name, w):
        self.off[name] = self.n
        self.w[name] = w
        self.n += w


CR = ConstReg()
CR.reg("ident", 128)
CR.reg("identb", 64)
CR.reg("tri64", 128)
CR.reg("mask_self", 128)
CR.reg("mask_prev0", 128)
CR.reg("amask", 64)
CR.reg("ones", 128)
CR.reg("onesb", 8)
CR.reg("chind2", 2)
CR.reg("pind2", 256)
CR.reg("selcols", 64)
CR.reg("hsum12", 72)
CR.reg("hexp12", 384)
CR.reg("flags", 4)
CR.reg("cscal", 8)
SCALAR_CONSTS = {
    "eps": 1e-5,
    "negmagic": -MAGIC,
    "lnscale_b": float(-0.5 * np.log(128.0)),
    "epsctx": EPS_CTX,
    "one": 1.0,
}
CONST_COLS = ((CR.n + 15) // 16) * 16


def make_consts(is_half1):
    import ml_dtypes
    c = np.zeros((128, CONST_COLS), np.float32)

    def put(name, arr):
        o = CR.off[name]
        c[:, o:o + CR.w[name]] = arr

    put("ident", np.eye(128, dtype=np.float32))
    ibb = np.ascontiguousarray(np.eye(128, dtype=np.float32).astype(ml_dtypes.bfloat16))
    put("identb", ibb.view(np.float32))
    tri = np.zeros((128, 128), np.float32)
    for blk in range(2):
        for tt in range(64):
            tri[blk * 64:blk * 64 + tt + 1, blk * 64 + tt] = 1.0
    put("tri64", tri)
    ms = np.where(np.arange(128)[:, None] > np.arange(128)[None, :], NEG, 0.0)
    put("mask_self", ms.astype(np.float32))
    put("mask_prev0", np.zeros((128, 128), np.float32) if is_half1
        else np.full((128, 128), NEG, np.float32))
    am = np.zeros((128, 128), np.float32)
    for blk in range(2):
        for tt in range(64):
            am[blk * 64:blk * 64 + tt + 1, blk * 64 + tt] = 1.0
    put("amask", np.ascontiguousarray(am.astype(ml_dtypes.bfloat16)).view(np.float32))
    put("ones", np.ones((128, 128), np.float32))
    put("onesb", np.ascontiguousarray(
        np.ones((128, 16), np.float32).astype(ml_dtypes.bfloat16)).view(np.float32))
    ci = np.zeros((128, 2), np.float32)
    ci[0:64, 0] = 1.0
    ci[64:128, 1] = 1.0
    put("chind2", ci)
    pi = np.zeros((128, 256), np.float32)
    pi[0, 0:64] = 1.0
    pi[0, 128 + 64:256] = 1.0
    put("pind2", pi)
    sel = np.zeros((128, 64), np.float32)
    for j in range(8):
        sel[:, 8 * j + j] = 1.0
    put("selcols", sel)
    hs = np.zeros((128, 72), np.float32)
    he = np.zeros((128, 768), np.float32)
    for cb in range(6):
        hs[0:64, 12 * cb + 2 * cb] = 1.0
        hs[64:128, 12 * cb + 2 * cb + 1] = 1.0
        he[2 * cb, 128 * cb:128 * cb + 64] = 1.0
        he[2 * cb + 1, 128 * cb + 64:128 * cb + 128] = 1.0
    put("hsum12", hs)
    put("hexp12", np.ascontiguousarray(
        he.astype(ml_dtypes.bfloat16)).view(np.float32))
    fl = np.zeros((128, 4), np.float32)
    fl[:, 0] = 0.0 if is_half1 else 1.0
    fl[:, 1] = 1.0 if is_half1 else 0.0
    put("flags", fl)
    cs = np.zeros((128, 8), np.float32)
    for i, v in enumerate(SCALAR_CONSTS.values()):
        cs[:, i] = np.float32(v)
    put("cscal", cs)
    return c


class Op:
    __slots__ = ("eng", "fn", "deps", "dma", "needed", "idx", "sem", "val", "inc")

    def __init__(self, eng, fn, deps, dma, idx):
        self.eng, self.fn, self.deps, self.dma, self.idx = eng, fn, list(deps), dma, idx
        self.needed = dma
        self.sem = None
        self.val = 0
        self.inc = 16 if dma else 1


class Buf:
    __slots__ = ("w", "r")

    def __init__(self):
        self.w = None
        self.r = []


class Sched:
    NPOOL = 4

    def __init__(self, nc, stack, n_dma_sems=48):
        self.nc = nc
        self.ops = []
        self.eng_sem = {e: [stack.enter_context(nc.semaphore(f"es_{e}{i}"))
                            for i in range(self.NPOOL)]
                        for e in ("t", "v", "s", "g", "y")}
        self.dma_sems = [stack.enter_context(nc.semaphore(f"ds_{i}"))
                         for i in range(n_dma_sems)]
        self.dma_i = 0
        self._stack = stack
        self.gdma_sems = []
        self.bufs = {}
        self.last = {e: None for e in ("t", "v", "s", "g", "y")}
        self.dma_open = []

    def B(self, name):
        if name not in self.bufs:
            self.bufs[name] = Buf()
        return self.bufs[name]

    def op(self, eng, fn, reads=(), writes=(), dma=False, extra=()):
        psr = [b for b in reads if b.startswith("psb")]
        if psr:
            reads = [b for b in reads if not b.startswith("psb")]
            writes = list(writes) + [b for b in psr if b not in writes]
        deps = list(extra)
        for b in reads:
            bb = self.B(b)
            if bb.w is not None:
                deps.append(bb.w)
        for b in writes:
            bb = self.B(b)
            deps.extend(bb.r)
            if bb.w is not None:
                deps.append(bb.w)
        o = Op(eng, fn, [d for d in deps if d is not None], dma, len(self.ops))
        self.ops.append(o)
        self.last[eng] = o
        if dma:
            self.dma_open.append(o)
        for b in reads:
            self.B(b).r.append(o)
        for b in writes:
            bb = self.B(b)
            bb.w = o
            bb.r = []
        return o

    def fence(self):
        prev = [o for o in self.last.values() if o is not None] + self.dma_open
        self.dma_open = []
        for e in ("t", "v", "s", "g", "y"):
            self.op(e, lambda eng: eng.nop(), extra=prev)
        f = {e: self.last[e] for e in self.last}
        for b in self.bufs.values():
            if b.w is not None:
                b.w = f[b.w.eng]
            b.r = []

    def finalize(self):
        for op in self.ops:
            seen, dd = set(), []
            for d in op.deps:
                if id(d) in seen:
                    continue
                seen.add(id(d))
                dd.append(d)
                assert d.idx < op.idx
                d.needed = True
            op.deps = dd
        cnt = {e: 0 for e in self.eng_sem}
        use = {}
        for op in self.ops:
            if not op.needed:
                continue
            if op.dma:
                if op.eng == "g":
                    sem = self._stack.enter_context(
                        self.nc.semaphore(f"gs_{len(self.gdma_sems)}"))
                    self.gdma_sems.append(sem)
                    op.sem, op.val = sem, 16
                    continue
                sem = self.dma_sems[self.dma_i % len(self.dma_sems)]
                self.dma_i += 1
                use[id(sem)] = use.get(id(sem), 0) + 16
                op.sem, op.val = sem, use[id(sem)]
            else:
                sem = self.eng_sem[op.eng][cnt[op.eng] % self.NPOOL]
                cnt[op.eng] += 1
                use[id(sem)] = use.get(id(sem), 0) + 1
                op.sem, op.val = sem, use[id(sem)]

    def run(self, ekey, eng):
        mark = {}
        for op in self.ops:
            if op.eng != ekey:
                continue
            for d in op.deps:
                k = id(d.sem)
                if mark.get(k, 0) >= d.val:
                    continue
                eng.wait_ge(d.sem, d.val)
                mark[k] = d.val
            last = op.fn(eng)
            if op.needed:
                assert last is not None, f"op idx {op.idx} on {ekey} emitted nothing"
                last.then_inc(op.sem, op.inc)


ALL_PHASES = ("attn_a", "ffn1", "attn_b", "ffn2")


def build_nc(T, phases=ALL_PHASES, n_cores=8, debug=False, no_coll=False, aa_stop=None):
    assert T % 512 == 0
    NT128 = T // 128
    NT512 = T // 512
    nc = bass.Bass("TRN2", target_bir_lowering=False, debug=debug)
    stack = ExitStack()

    x_in = nc.dram_tensor("x_in", [T, D], F32, kind="ExternalInput").ap()
    halo_in = nc.dram_tensor("halo_in", [WINDOW, D], F32, kind="ExternalInput").ap()
    consts_in = nc.dram_tensor("consts", [128, CONST_COLS], F32, kind="ExternalInput").ap()
    y_out = nc.dram_tensor("y_out", [T, D], F32, kind="ExternalOutput").ap()

    Wd = {}
    for nm, shp in [
        ("aWq", (D, D)), ("aWk", (D, D)), ("aWv", (D, D)), ("aWo", (D, D)),
        ("abo", (1, D)),
        ("bWq", (D, D)), ("bWf", (D, D)), ("bWi", (D, D)), ("bWo", (D, D)),
        ("bgw", (1, D)),
        ("fw1", (HID, D)), ("fb1", (1, HID)), ("fw2", (D, HID)), ("fb2", (1, D)),
    ]:
        Wd[nm] = nc.dram_tensor(nm, list(shp), F32, kind="ExternalInput").ap()

    scr = {}
    scr["w1q"] = nc.dram_tensor("scr_w1q", [NH, NB, 128, 128], BF16, kind="Internal")
    scr["w2q"] = nc.dram_tensor("scr_w2q", [NB, NH, 128, 128], BF16, kind="Internal")
    for nm in ("bWq", "bWf", "bWi", "bWo"):
        scr[nm] = nc.dram_tensor("scr_" + nm, [NB, NB, 128, 128], BF16, kind="Internal")
    for nm in ("aWq", "aWk", "aWv", "aWo"):
        scr[nm] = nc.dram_tensor("scr_" + nm, [NB, NB, 128, 128], BF16, kind="Internal")

    scr["ctx"] = nc.dram_tensor("scr_ctx", [T // CHUNK, 64, 784], BF16,
                                kind="Internal")
    scr["og0"] = nc.dram_tensor("scr_og0", [128, 4096], BF16, kind="Internal")
    scr["qp0"] = nc.dram_tensor("scr_qp0", [8, 128, 512], BF16, kind="Internal")

    CTX_ROWS = 128
    cc1_in = nc.dram_tensor("cc1_in", [64, 792], F32, kind="Internal")
    cc1_out = nc.dram_tensor("cc1_out", [64, 792], F32, kind="Internal")
    cc2_in = nc.dram_tensor("cc2_in", [128, HEADS_B * 128], F32, kind="Internal")
    cc2_out = nc.dram_tensor("cc2_out", [128, HEADS_B * 128], F32, kind="Internal")

    sbt = {}

    def sbuf(name, shape, dtype):
        sbt[name] = stack.enter_context(nc.sbuf_tensor(name, list(shape), dtype))
        return sbt[name]

    x_sb = sbuf("x", [128, NB * T], F32)
    const_sb = sbuf("const", [128, CONST_COLS], F32)
    b1q16 = sbuf("b1q16", [128, NH], F32)
    b2q = sbuf("b2q", [128, NB], F32)
    abo_sb = sbuf("abo_pp", [128, NB], F32)
    bgw_sb = sbuf("bgw_pp", [128, NB], F32)
    S_state = sbuf("S_state", [128, HEADS_B * 128], BF16)
    ARENA_COLS = 18304
    arena = sbuf("arena", [128, ARENA_COLS], F32)

    ps = [stack.enter_context(nc.psum_tensor(f"ps{i}", [128, 512], F32))
          for i in range(8)]

    sc = Sched(nc, stack)
    xv = x_sb.ap().rearrange("p (c t) -> p c t", c=NB)

    def cview(name):
        o = CR.off[name]
        return const_sb[:, o:o + CR.w[name]]

    ident = cview("ident")
    identb = cview("identb").bitcast(BF16)
    ones = cview("ones")
    onesb = cview("onesb").bitcast(BF16)
    for i, (k, v) in enumerate(SCALAR_CONSTS.items()):
        capv = const_sb[:, CR.off["cscal"] + i:CR.off["cscal"] + i + 1]
        nc.const_aps.register(F32, float(v), capv)
        if float(np.float32(v)) != float(v):
            nc.const_aps.register(F32, float(np.float32(v)), capv)

    def AFr(off_kb, cols):
        o = off_kb * 256
        assert o + cols <= ARENA_COLS, (off_kb, cols)
        return arena[:, o:o + cols]

    def ABf(off_kb, cols_bf):
        o = off_kb * 256
        assert o + cols_bf // 2 <= ARENA_COLS
        return arena[:, o:o + cols_bf // 2].bitcast(BF16)

    class Arena:

        def __init__(self):
            self.spans = []

        def _reg(self, off, n, buf):
            assert off + n <= ARENA_COLS, (off, n)
            for (a, b, nm) in self.spans:
                if not (off + n <= a or off >= b):
                    assert nm == buf, f"arena overlap {buf} vs {nm} @{off}"
            self.spans.append((off, off + n, buf))

        def f32(self, off, n, buf):
            self._reg(off, n, buf)
            return arena[:, off:off + n]

        def bf16(self, off, nbf, buf):
            assert nbf % 2 == 0
            self._reg(off, nbf // 2, buf)
            return arena[:, off:off + nbf // 2].bitcast(BF16)

    def dma(queue, out, in_, reads=(), writes=(), extra=(), slow=False):
        return sc.op(queue, lambda e, out=out, in_=in_, slow=slow: e.dma_start(
            out=out, in_=in_, allow_slow_non_contiguous=slow),
            reads=reads, writes=writes, dma=True, extra=extra)

    def mm_group(items, reads=(), writes=(), extra=()):
        def fn(t):
            last = None
            for (o, l, r, st, sp) in items:
                last = t.matmul(o, l, r, start=st, stop=sp)
            return last
        return sc.op("t", fn, reads=reads, writes=writes, extra=extra)

    dma("y", const_sb[:], consts_in, writes=["const"])

    ldbuf = [AFr(0, 1024), AFr(4, 1024)]
    for i in range(NT128):
        tb = ldbuf[i % 2]
        tbn = f"ldbuf{i % 2}"
        dma("y", tb, x_in[i * 128:(i + 1) * 128, :], writes=[tbn])
        for h in range(2):
            bank = h
            sc.op("t", (lambda t, tb=tb, h=h, bank=bank: [
                t.transpose(ps[bank][:, c * 128:(c + 1) * 128],
                            tb[:, (4 * h + c) * 128:(4 * h + c + 1) * 128], ident)
                for c in range(4)][-1]),
                reads=[tbn, "const"], writes=[f"psb{bank}"])
            sc.op("v" if h == 0 else "s",
                  (lambda e, i=i, h=h, bank=bank, s=(h == 1):
                   (e.copy if s else e.tensor_copy)(
                       xv[:, 4 * h:4 * h + 4, i * 128:(i + 1) * 128],
                       ps[bank][:, 0:512])),
                  reads=[f"psb{bank}"], writes=[f"x_{i}"])
    sc.fence()

    def prep_transposed(src, scr_t, n_ob, n_ic, quant_scale=None, out_scale=1.0):
        lb = [AFr(0, 1024), AFr(4, 1024)]
        qb = [AFr(8, 1024), AFr(12, 1024)]
        ob_bf = [ABf(16, n_ic * 128), ABf(16 + (n_ic * 128 // 512), n_ic * 128)]
        for ob in range(n_ob):
            tb = lb[ob % 2]
            tbn = f"prep_lb{ob % 2}"
            for half in range(n_ic // 8):
                dma("y", tb, src[ob * 128:(ob + 1) * 128,
                                 half * 1024:(half + 1) * 1024], writes=[tbn])
                for h in range(2):
                    bank = 4 + h
                    sc.op("t", (lambda t, tb=tb, h=h, bank=bank: [
                        t.transpose(ps[bank][:, c * 128:(c + 1) * 128],
                                    tb[:, (4 * h + c) * 128:(4 * h + c + 1) * 128],
                                    ident)
                        for c in range(4)][-1]),
                        reads=[tbn, "const"], writes=[f"psb{bank}"])
                    q = qb[h]
                    qn = f"prep_qb{h}"
                    obf = ob_bf[ob % 2]
                    obn = f"prep_ob{ob % 2}"
                    if quant_scale is not None:
                        qs = float(quant_scale)
                        sc.op("v", (lambda v, bank=bank, q=q: v.tensor_scalar(
                            q[:, 0:512], ps[bank][:, 0:512], qs, -128.0,
                            A.mult, A.max)), reads=[f"psb{bank}"], writes=[qn])
                        sc.op("v", (lambda v, q=q: v.tensor_scalar(
                            q[:, 0:512], q[:, 0:512], 127.0, MAGIC,
                            A.min, A.add)), reads=[qn], writes=[qn])
                        osc = float(out_scale / quant_scale)
                        sc.op("v", (lambda v, q=q, obf=obf, h=h, half=half:
                                    v.tensor_scalar(
                                        obf[:, (half * 8 + 4 * h) * 128:
                                            (half * 8 + 4 * h + 4) * 128],
                                        q[:, 0:512], -MAGIC, osc, A.add, A.mult)),
                              reads=[qn], writes=[obn])
                    else:
                        sc.op("v", (lambda v, bank=bank, obf=obf, h=h, half=half:
                                    v.tensor_copy(
                                        obf[:, (half * 8 + 4 * h) * 128:
                                            (half * 8 + 4 * h + 4) * 128],
                                        ps[bank][:, 0:512])),
                              reads=[f"psb{bank}"], writes=[obn])
            obf = ob_bf[ob % 2]
            obn = f"prep_ob{ob % 2}"
            dma("y", scr_t.ap()[ob].transpose([1, 0, 2]),
                obf[:].rearrange("p (c o) -> p c o", c=n_ic), reads=[obn])
        sc.fence()

    def prep_natural(src, scr_t, n_ic, n_ob):
        cb = ABf(0, n_ob * 128)
        for ic in range(n_ic):
            dma("g", cb[:], src[ic * 128:(ic + 1) * 128, :], writes=["prep_cb"])
            dma("y", scr_t.ap()[:, ic].transpose([1, 0, 2]),
                cb[:].rearrange("p (o c) -> p o c", o=n_ob), reads=["prep_cb"])
        sc.fence()

    def prep_bias(src, dst, n, name, qscale=None, out_scale=1.0):
        dma("y", dst[:, 0:n], src.rearrange("a (o p) -> a p o", p=128),
            writes=[name], slow=True)
        if qscale is not None:
            qs = float(qscale)
            osc = float(out_scale / qscale)
            sc.op("v", lambda v: v.tensor_scalar(
                dst[:, 0:n], dst[:, 0:n], qs, -128.0, A.mult, A.max),
                reads=[name], writes=[name])
            sc.op("v", lambda v: v.tensor_scalar(
                dst[:, 0:n], dst[:, 0:n], 127.0, MAGIC, A.min, A.add),
                reads=[name], writes=[name])
            sc.op("v", lambda v: v.tensor_scalar(
                dst[:, 0:n], dst[:, 0:n], -MAGIC, osc, A.add, A.mult),
                reads=[name], writes=[name])

    if "ffn1" in phases or "ffn2" in phases:
        prep_transposed(Wd["fw1"], scr["w1q"], NH, NB, quant_scale=64.0)
        prep_transposed(Wd["fw2"], scr["w2q"], NB, NH, quant_scale=64.0,
                        out_scale=1.0 / 16.0)
        prep_bias(Wd["fb1"], b1q16, NH, "b1q16", qscale=64.0, out_scale=16.0)
        prep_bias(Wd["fb2"], b2q, NB, "b2q", qscale=64.0, out_scale=1.0)
        sc.fence()
    if "attn_b" in phases:
        for nm in ("bWq", "bWf", "bWi", "bWo"):
            prep_transposed(Wd[nm], scr[nm], NB, NB)
        prep_bias(Wd["bgw"], bgw_sb, NB, "bgw")
        sc.fence()
    if "attn_a" in phases:
        for nm in ("aWq", "aWk", "aWv", "aWo"):
            prep_natural(Wd[nm], scr[nm], NB, NB)
        prep_bias(Wd["abo"], abo_sb, NB, "abo")
        sc.fence()

    ST_OFF = 16384
    stats = arena[:, ST_OFF:ST_OFF + 1024]
    rowb = [arena[32:33, ST_OFF:ST_OFF + 1024],
            arena[64:65, ST_OFF:ST_OFF + 1024]]

    def xnames(t0, t1):
        return [f"x_{i}" for i in range(t0 // 128, t1 // 128)]

    def emit_ln_stats(out_scale):
        sqb = [AFr(48, 512), AFr(50, 512)]
        tmp = [AFr(40, 512), AFr(42, 512), AFr(44, 512)]
        n = NT512 * NB
        k = 0
        for j in range(NT512):
            sel = const_sb[:, CR.off["selcols"] + 8 * j:CR.off["selcols"] + 8 * j + 8]
            for c in range(NB):
                mm_sum = (lambda t, j=j, c=c, sel=sel, k=k: t.matmul(
                    ps[0][0:8, 0:512], sel,
                    xv[:, c, j * 512:(j + 1) * 512],
                    start=(k == 0), stop=(k == n - 1)))
                sc.op("t", mm_sum, reads=xnames(j * 512, (j + 1) * 512) + ["const"],
                      writes=["psb0"])
                sq = sqb[k % 2]
                sqn = f"sqb{k % 2}"
                sc.op("s", (lambda e, j=j, c=c, sq=sq: e.activation(
                    sq[:], xv[:, c, j * 512:(j + 1) * 512], AF.Square)),
                    reads=xnames(j * 512, (j + 1) * 512), writes=[sqn])
                sc.op("t", (lambda t, sq=sq, sel=sel, k=k: t.matmul(
                    ps[1][0:8, 0:512], sel, sq[:],
                    start=(k == 0), stop=(k == n - 1))),
                    reads=[sqn, "const"], writes=["psb1"])
                k += 1
        m, m2, v_ = tmp
        sc.op("v", lambda v: v.tensor_scalar(
            m[0:8, :], ps[0][0:8, 0:512], INVD, None, A.mult),
            reads=["psb0"], writes=["qtarea"])
        sc.op("v", lambda v: v.tensor_tensor(
            m2[0:8, :], m[0:8, :], m[0:8, :], A.mult),
            writes=["qtarea"])
        sc.op("v", lambda v: v.scalar_tensor_tensor(
            v_[0:8, :], ps[1][0:8, 0:512], INVD, m2[0:8, :],
            A.mult, A.subtract),
            reads=["psb1"], writes=["qtarea"])
        sc.op("s", lambda e: e.activation(
            m2[0:8, :], v_[0:8, :], AF.Ln, bias=EPS_LN),
            writes=["qtarea"])
        sc.op("s", lambda e: e.activation(
            v_[0:8, :], m2[0:8, :], AF.Exp, scale=-0.5),
            writes=["qtarea"])
        osf = float(out_scale)
        sc.op("v", lambda v: v.tensor_scalar(
            stats[0:8, 0:512], v_[0:8, :], osf, None, A.mult),
            reads=["qtarea"], writes=["stats"])
        sc.op("v", lambda v: v.scalar_tensor_tensor(
            stats[0:8, 512:1024], m[0:8, :], -osf, v_[0:8, :],
            A.mult, A.mult),
            reads=["qtarea"], writes=["stats"])
        sc.fence()

    RGROUPS = [[2 * b, 2 * b + 1] for b in range(n_cores // 2)]

    def emit_attn_a():
        NCH = T // CHUNK
        ar = Arena()
        ctxacc = ar.f32(17408, 768, "ctxacc")
        ksacc = ar.f32(18176, 12, "ksacc")
        emit_ln_stats(1.0)

        xn = ar.bf16(0, 4096, "xn")
        xnv = xn.rearrange("p (c t) -> p c t", c=NB)
        aqt = ar.f32(2048, 2048, "aqt")
        pmask = cview("chind2")

        def ln_apply_tile(tk):
            rb = rowb[tk % 2]
            rbn = f"rb{tk % 2}"
            dma("y", rb, stats[tk:tk + 1, 0:1024], reads=["stats"], writes=[rbn])
            obase = 32 * (1 + tk % 2)
            orow = ones[obase:obase + 1, 0:128]
            sc.op("t", (lambda t, rb=rb, orow=orow: t.matmul(
                ps[0][0:128, 0:512], orow, rb[0:1, 0:512],
                start=True, stop=True)), reads=[rbn, "const"], writes=["psb0"])
            sc.op("t", (lambda t, rb=rb, orow=orow: t.matmul(
                ps[1][0:128, 0:512], orow, rb[0:1, 512:1024],
                start=True, stop=True)), reads=[rbn, "const"], writes=["psb1"])
            t0 = tk * 512
            for th in range(2):
                xsrc = xv[:, :, t0 + th * 256:t0 + th * 256 + 256]
                sc.op("v", (lambda v, xsrc=xsrc, th=th: v.scalar_tensor_tensor(
                    aqt[:, 0:2048], xsrc, 1.0,
                    ps[0][:, th * 256:(th + 1) * 256].unsqueeze(1)
                    .broadcast_to([128, 8, 256]),
                    A.mult, A.mult)),
                    reads=xnames(t0, t0 + 512) + ["psb0"], writes=["aqt"])
                dst = xnv[:, :, th * 256:(th + 1) * 256]
                sc.op("v", (lambda v, th=th, dst=dst: v.scalar_tensor_tensor(
                    dst, aqt[:, 0:2048], 1.0,
                    ps[1][:, th * 256:(th + 1) * 256].unsqueeze(1)
                    .broadcast_to([128, 8, 256]),
                    A.mult, A.add)),
                    reads=["aqt", "psb1"], writes=["xn"])

        wkvg = ar.bf16(4096, 12288, "wkvg")
        wkvv = wkvg.rearrange("p (w c o) -> p w c o", w=2, c=NB)
        ccst = arena[:, 4096:4888]
        kT = ar.bf16(10240, 3072, "kT")
        kTv = kT.rearrange("p (s o) -> p s o", s=4)
        vT = ar.bf16(11776, 3072, "vT")
        vTv = vT.rearrange("p (s o) -> p s o", s=4)
        elu1 = ar.f32(13312, 768, "elu1")
        elu2 = ar.f32(14080, 768, "elu2")
        kmask = arena[:, 13312:13696].bitcast(BF16)
        ctxo = [ar.bf16(14848, 784, "ctxo0"), ar.bf16(15240, 784, "ctxo1")]

        sc.op("v", lambda v: v.memset(ctxacc[0:64, 0:768], 0.0),
              writes=["ctxacc"])
        sc.op("v", lambda v: v.memset(ksacc[0:64, 0:12], 0.0), writes=["ksacc"])
        sc.op("v", lambda v: v.memset(ctxo[0][0:64, 780:784], 0.0),
              writes=["ctxo0"])
        sc.op("v", lambda v: v.memset(ctxo[1][0:64, 780:784], 0.0),
              writes=["ctxo1"])
        if aa_stop == "none":
            sc.fence()
            return
        for tk in range(NT512 if aa_stop != "prep" else 0):
            ln_apply_tile(tk)
            if aa_stop == "ln":
                continue
            for w, nm in ((0, "aWk"), (1, "aWv")):
                for ob in range(6):
                    dma("y", wkvv[:, w, :, ob * 128:(ob + 1) * 128],
                        scr[nm].ap()[2 + ob].transpose([1, 0, 2]),
                        writes=["wkvg"])
            for t128 in range(4):
                for w, (bankA, bankB) in ((0, (2, 3)), (1, (4, 5))):
                    def mmkv(t, w=w, t128=t128, bankA=bankA, bankB=bankB):
                        last = None
                        for sp, bank, o0, o1 in ((0, bankA, 0, 512),
                                                 (1, bankB, 512, 768)):
                            for c in range(NB):
                                lhsT = xnv[:, c, t128 * 128:(t128 + 1) * 128]
                                last = t.matmul(
                                    ps[bank][:, 0:o1 - o0], lhsT,
                                    wkvv[:, w, c, o0:o1],
                                    start=(c == 0), stop=(c == NB - 1))
                        return last
                    sc.op("t", mmkv, reads=["xn", "wkvg"],
                          writes=[f"psb{bankA}", f"psb{bankB}"])
                for sp, bank, o0, o1 in ((0, 2, 0, 512), (1, 3, 512, 768)):
                    n = o1 - o0
                    sc.op("v", (lambda v, bank=bank, o0=o0, n=n: v.tensor_scalar(
                        elu1[:, o0:o0 + n], ps[bank][:, 0:n], 0.0, None, A.min)),
                        reads=[f"psb{bank}"], writes=["elu1"])
                    sc.op("s", (lambda e, o0=o0, n=n: e.activation(
                        elu2[:, o0:o0 + n], elu1[:, o0:o0 + n], AF.Exp)),
                        reads=["elu1"], writes=["elu2"])
                    sc.op("v", (lambda v, bank=bank, o0=o0, n=n, t128=t128:
                                v.scalar_tensor_tensor(
                                    kTv[:, t128, o0:o0 + n], ps[bank][:, 0:n],
                                    0.0, elu2[:, o0:o0 + n], A.max, A.add)),
                          reads=[f"psb{bank}", "elu2"], writes=["kT"])
                    eng = "v" if sp == 0 else "s"
                    sc.op(eng, (lambda e, bank=bank, o0=o0, n=n, t128=t128,
                                sv=(sp == 1): (e.copy if sv else e.tensor_copy)(
                                    vTv[:, t128, o0:o0 + n],
                                    ps[bank + 2][:, 0:n])),
                          reads=[f"psb{bank + 2}"], writes=["vT"])
                for cpar in range(2 if aa_stop != "kv" else 0):
                    cu = tk * 8 + t128 * 2 + cpar
                    sc.op("v", (lambda v, t128=t128, cpar=cpar: v.tensor_scalar(
                        kmask[:, 0:768], kTv[:, t128, :],
                        pmask[:, cpar:cpar + 1], None, A.mult)),
                        reads=["kT", "const"], writes=["elu1"])
                    def mmctx(t, t128=t128):
                        last = None
                        for gh in range(GLOBAL_HEADS):
                            bank, col = (6, gh * 64) if gh < 8 else \
                                        (7, (gh - 8) * 64)
                            last = t.matmul(
                                ps[bank][0:64, col:col + 64],
                                kmask[:, gh * 64:gh * 64 + 64],
                                vTv[:, t128, gh * 64:gh * 64 + 64],
                                start=True, stop=True)
                        for gh in range(GLOBAL_HEADS):
                            last = t.matmul(
                                ps[7][0:64, 384 + gh:385 + gh],
                                kmask[:, gh * 64:gh * 64 + 64],
                                onesb[:, 0:1],
                                start=True, stop=True)
                        return last
                    sc.op("t", mmctx, reads=["elu1", "vT", "const"],
                          writes=["psb6", "psb7"])
                    if aa_stop == "ctx1":
                        continue
                    cb = ctxo[cu % 2]
                    cbn = f"ctxo{cu % 2}"
                    sc.op("v", (lambda v, cb=cb: v.tensor_copy(
                        cb[0:64, 0:512], ps[6][0:64, 0:512])),
                        reads=["psb6"], writes=[cbn])
                    sc.op("s", (lambda e, cb=cb: e.copy(
                        cb[0:64, 512:768], ps[7][0:64, 0:256])),
                        reads=["psb7"], writes=[cbn])
                    sc.op("s", (lambda e, cb=cb: e.copy(
                        cb[0:64, 768:780], ps[7][0:64, 384:396])),
                        reads=["psb7"], writes=[cbn])
                    sc.op("v", lambda v: v.tensor_tensor(
                        ctxacc[0:64, 0:512], ctxacc[0:64, 0:512],
                        ps[6][0:64, 0:512], A.add),
                        reads=["psb6"], writes=["ctxacc"])
                    sc.op("v", lambda v: v.tensor_tensor(
                        ctxacc[0:64, 512:768], ctxacc[0:64, 512:768],
                        ps[7][0:64, 0:256], A.add),
                        reads=["psb7"], writes=["ctxacc"])
                    sc.op("v", lambda v: v.tensor_tensor(
                        ksacc[0:64, 0:12], ksacc[0:64, 0:12],
                        ps[7][0:64, 384:396], A.add),
                        reads=["psb7"], writes=["ksacc"])
                    if aa_stop != "ctx2":
                        dma("y", scr["ctx"].ap()[cu], cb[0:64, 0:784],
                            reads=[cbn], writes=["scrctx"])

        if aa_stop in ("prep", "1a", "ln", "kv", "ctx1", "ctx2"):
            sc.fence()
            return
        flagC = cview("flags")[0:64, 0:1]
        flagU = cview("flags")[0:64, 1:2]
        sc.op("v", lambda v: v.memset(ccst[0:64, 780:792], 0.0),
              writes=["wkvg"])
        sc.op("v", lambda v: v.tensor_scalar(
            ccst[0:64, 0:768], ctxacc[0:64, 0:768], flagC, None, A.mult),
            reads=["ctxacc", "const"], writes=["wkvg"])
        sc.op("v", lambda v: v.tensor_scalar(
            ccst[0:64, 768:780], ksacc[0:64, 0:12], flagC, None, A.mult),
            reads=["ksacc", "const"], writes=["wkvg"])
        dma("y", cc1_in.ap(), ccst[0:64, 0:792], reads=["wkvg"], writes=["cc1i"])
        if no_coll:
            dma("y", cc1_out.ap(), cc1_in.ap(), reads=["cc1i"], writes=["cc1o"])
        else:
            sc.op("g", lambda g: g.collective_compute(
                "AllReduce", A.add, replica_groups=RGROUPS,
                ins=[cc1_in.ap().opt()], outs=[cc1_out.ap().opt()]),
                reads=["cc1i"], writes=["cc1o"])
        dma("y", ccst[0:64, 0:792], cc1_out.ap(), reads=["cc1o"], writes=["wkvg"])
        sc.op("v", lambda v: v.tensor_scalar(
            ctxacc[0:64, 0:768], ccst[0:64, 0:768], flagU, None, A.mult),
            reads=["wkvg", "const"], writes=["ctxacc"])
        sc.op("v", lambda v: v.tensor_scalar(
            ksacc[0:64, 0:12], ccst[0:64, 768:780], flagU, None, A.mult),
            reads=["wkvg", "const"], writes=["ksacc"])
        sc.fence()
        if aa_stop == "coll":
            return

        ar2 = Arena()
        ar2.f32(17408, 768, "ctxacc")
        ar2.f32(18176, 12, "ksacc")
        xn2 = ar2.bf16(0, 4096, "xn")
        xnv2 = xn2.rearrange("p (c t) -> p c t", c=NB)
        attnb = ar2.bf16(0, 4096, "xn")
        atv = attnb.rearrange("p (c t) -> p c t", c=NB)
        ar2.f32(2048, 2048, "aqt")
        qbf = ar2.bf16(4096, 4096, "qbf")
        qbv = qbf.rearrange("p (c t) -> p c t", c=NB)
        qe = ar2.f32(6144, 1536, "qearea")
        qev = qe.rearrange("p (c t) -> p c t", c=6)
        qsmR = arena[:, 6144:9216].bitcast(BF16)
        qsRv = qsmR.rearrange("p (h t) -> p h t", h=12)
        wob = [ar2.bf16(6144, 1024, "qearea"), ar2.bf16(6656, 1024, "qearea")]
        wovs = [w.rearrange("p (c o) -> p c o", c=NB) for w in wob]
        qsm = ar2.bf16(7680, 3072, "qsm")
        qsv = qsm.rearrange("p (c t) -> p c t", c=6)
        gstrip = arena[:, 2432:3968].bitcast(BF16)
        gsv = gstrip.rearrange("p (h t) -> p h t", h=12)
        qrec = ar2.bf16(9216, 1024, "qrec")
        kloc = [ar2.bf16(9728, 1024, "kloc0"), ar2.bf16(10240, 1024, "kloc1")]
        klv = [k.rearrange("p (c t) -> p c t", c=2) for k in kloc]
        vloc = [ar2.bf16(10752, 1024, "vloc0"), ar2.bf16(11264, 1024, "vloc1")]
        vlv = [v.rearrange("p (c t) -> p c t", c=4) for v in vloc]
        hkv = ar2.bf16(11776, 512, "halokv")
        hkvv = hkv.rearrange("p (c t) -> p c t", c=4)
        wklb = ar2.bf16(12032, 2048, "wklarea")
        wklv = wklb.rearrange("p (o c t) -> p o c t", o=2, c=NB)
        scp = ar2.bf16(12032, 512, "wklarea")
        scpv = scp.rearrange("p (h t) -> p h t", h=4)
        scs = ar2.bf16(12288, 512, "wklarea")
        scsv = scs.rearrange("p (h t) -> p h t", h=4)
        recrow = ar2.f32(12544, 512, "wklarea")
        wvlb = ar2.bf16(13056, 2048, "wvlarea")
        wvlv = wvlb.rearrange("p (c t) -> p c t", c=NB)
        ctxrb = [ar2.bf16(13056, 784, "wvlarea"), ar2.bf16(13448, 784, "wvlarea")]
        wqb = [ar2.bf16(14080, 1024, "wqb0"), ar2.bf16(14592, 1024, "wqb1")]
        wqvs = [w.rearrange("p (c o) -> p c o", c=NB) for w in wqb]
        hfeat = ar2.f32(15104, 1024, "halof")
        hfv = hfeat.rearrange("p (c t) -> p c t", c=NB)
        rtile = ar2.f32(15104, 512, "halof")
        qm = [ar2.bf16(15616, 256, "halof"), ar2.bf16(15744, 256, "halof")]
        qmv = [q.rearrange("p (c t) -> p c t", c=2) for q in qm]
        ltile = ar2.bf16(15872, 512, "halof")
        hst = ar2.f32(16128, 256, "halost")
        hxn = ar2.bf16(3072, 1024, "aqt")
        hxv = hxn.rearrange("p (c t) -> p c t", c=NB)
        ctxs_bf = arena[:, 2048:2432].bitcast(BF16)

        for tk in range(NT512):
            ln_apply_tile(tk)
            for ob in range(2):
                dma("y", wklv[:, ob], scr["aWk"].ap()[ob].transpose([1, 0, 2]),
                    writes=["wklarea"])
                dma("y", wvlv[:, :, ob * 128:(ob + 1) * 128],
                    scr["aWv"].ap()[ob].transpose([1, 0, 2]),
                    writes=["wvlarea"])
            for o in range(NB):
                wvv = wqvs[o % 2]
                wn = f"wqb{o % 2}"
                dma("y", wvv, scr["aWq"].ap()[o].transpose([1, 0, 2]),
                    writes=[wn])
                bank = 2 + o % 2
                sc.op("t", (lambda t, wvv=wvv, bank=bank: [
                    t.matmul(ps[bank][:, 0:512], wvv[:, c, :], xnv2[:, c, :],
                             start=(c == 0), stop=(c == NB - 1))
                    for c in range(NB)][-1]),
                    reads=[wn, "xn"], writes=[f"psb{bank}"])
                eng = "v" if o % 2 == 0 else "s"
                sc.op(eng, (lambda e, bank=bank, o=o, sv=(o % 2 == 1):
                            (e.copy if sv else e.tensor_copy)(
                                qbv[:, o, :], ps[bank][:, 0:512])),
                      reads=[f"psb{bank}"], writes=["qbf"])
            for ob in range(2):
                bank = 2 + ob
                sc.op("t", (lambda t, ob=ob, bank=bank: [
                    t.matmul(ps[bank][:, 0:512], wklv[:, ob, c, :],
                             xnv2[:, c, :], start=(c == 0), stop=(c == NB - 1))
                    for c in range(NB)][-1]),
                    reads=["wklarea", "xn"], writes=[f"psb{bank}"])
                sc.op("v", (lambda v, bank=bank, ob=ob, tk=tk: v.tensor_copy(
                    klv[tk % 2][:, ob, :], ps[bank][:, 0:512])),
                    reads=[f"psb{bank}"], writes=[f"kloc{tk % 2}"])
            for t128 in range(4):
                bank = 2 + t128 % 2
                sc.op("t", (lambda t, t128=t128, bank=bank: [
                    t.matmul(ps[bank][:, 0:256],
                             xnv2[:, c, t128 * 128:(t128 + 1) * 128],
                             wvlv[:, c, 0:256],
                             start=(c == 0), stop=(c == NB - 1))
                    for c in range(NB)][-1]),
                    reads=["wvlarea", "xn"], writes=[f"psb{bank}"])
                sc.op("v", (lambda v, bank=bank, t128=t128, tk=tk: v.tensor_copy(
                    vlv[tk % 2][:, t128, :], ps[bank][:, 0:256])),
                    reads=[f"psb{bank}"], writes=[f"vloc{tk % 2}"])
            if tk == 0:
                dma("y", hfeat[:, 0:1024].rearrange("p (a b) -> p a b", a=2),
                    halo_in.rearrange("(a t) d -> a t d", a=2), slow=True,
                    writes=["halof"])
                sc.op("t", (lambda t: [
                    t.transpose(ps[4][:, c * 128:(c + 1) * 128],
                                hfeat[:, c * 128:(c + 1) * 128], ident)
                    for c in range(4)][-1]),
                    reads=["halof", "const"], writes=["psb4"])
                sc.op("t", (lambda t: [
                    t.transpose(ps[5][:, (c - 4) * 128:(c - 3) * 128],
                                hfeat[:, c * 128:(c + 1) * 128], ident)
                    for c in range(4, 8)][-1]),
                    reads=["halof", "const"], writes=["psb5"])
                sc.op("v", lambda v: v.tensor_copy(hfeat[:, 0:512],
                                                   ps[4][:, 0:512]),
                      reads=["psb4"], writes=["halof"])
                sc.op("v", lambda v: v.tensor_copy(hfeat[:, 512:1024],
                                                   ps[5][:, 0:512]),
                      reads=["psb5"], writes=["halof"])
                sc.op("t", (lambda t: [
                    t.matmul(ps[4][0:1, 0:128], ones[:, 0:1],
                             hfv[:, c, :], start=(c == 0), stop=(c == NB - 1))
                    for c in range(NB)][-1]),
                    reads=["halof", "const"], writes=["psb4"])
                sc.op("s", (lambda e: [
                    e.activation(aqt[:, c * 128:(c + 1) * 128], hfv[:, c, :],
                                 AF.Square) for c in range(NB)][-1]),
                    reads=["halof"], writes=["aqt"])
                sc.op("t", (lambda t: [
                    t.matmul(ps[5][0:1, 0:128], ones[:, 0:1],
                             aqt[:, c * 128:(c + 1) * 128],
                             start=(c == 0), stop=(c == NB - 1))
                    for c in range(NB)][-1]),
                    reads=["aqt", "const"], writes=["psb5"])
                sc.op("v", lambda v: v.tensor_scalar(
                    hst[0:1, 0:128], ps[4][0:1, 0:128], INVD, None, A.mult),
                    reads=["psb4"], writes=["halost"])
                sc.op("v", lambda v: v.tensor_tensor(
                    hst[0:1, 128:256], hst[0:1, 0:128], hst[0:1, 0:128],
                    A.mult), reads=["halost"], writes=["halost"])
                sc.op("v", lambda v: v.scalar_tensor_tensor(
                    hst[0:1, 128:256], ps[5][0:1, 0:128], INVD,
                    hst[0:1, 128:256], A.mult, A.subtract),
                    reads=["psb5", "halost"], writes=["halost"])
                sc.op("s", lambda e: e.activation(
                    hst[0:1, 128:256], hst[0:1, 128:256], AF.Ln, bias=EPS_LN),
                    reads=["halost"], writes=["halost"])
                sc.op("s", lambda e: e.activation(
                    hst[0:1, 128:256], hst[0:1, 128:256], AF.Exp, scale=-0.5),
                    reads=["halost"], writes=["halost"])
                sc.op("v", lambda v: v.scalar_tensor_tensor(
                    hst[0:1, 0:128], hst[0:1, 0:128], -1.0,
                    hst[0:1, 128:256], A.mult, A.mult),
                    reads=["halost"], writes=["halost"])
                sc.op("t", lambda t: t.matmul(
                    ps[4][0:128, 0:128], ones[0:1, 0:128], hst[0:1, 128:256],
                    start=True, stop=True),
                    reads=["halost", "const"], writes=["psb4"])
                sc.op("t", lambda t: t.matmul(
                    ps[5][0:128, 0:128], ones[0:1, 0:128], hst[0:1, 0:128],
                    start=True, stop=True),
                    reads=["halost", "const"], writes=["psb5"])
                sc.op("v", lambda v: v.scalar_tensor_tensor(
                    aqt[:, 0:1024], hfeat[:, 0:1024], 1.0,
                    ps[4][:, 0:128].unsqueeze(1).broadcast_to([128, 8, 128]),
                    A.mult, A.mult), reads=["halof", "psb4"], writes=["aqt"])
                sc.op("v", lambda v: v.scalar_tensor_tensor(
                    hxn[:, 0:1024], aqt[:, 0:1024], 1.0,
                    ps[5][:, 0:128].unsqueeze(1).broadcast_to([128, 8, 128]),
                    A.mult, A.add), reads=["aqt", "psb5"], writes=["aqt"])
                for ob in range(2):
                    bank = 4 + ob
                    sc.op("t", (lambda t, ob=ob, bank=bank: [
                        t.matmul(ps[bank][:, 0:128], wklv[:, ob, c, :],
                                 hxv[:, c, :], start=(c == 0),
                                 stop=(c == NB - 1)) for c in range(NB)][-1]),
                        reads=["wklarea", "aqt"], writes=[f"psb{bank}"])
                    sc.op("v", (lambda v, bank=bank, ob=ob: v.tensor_copy(
                        hkvv[:, ob, :], ps[bank][:, 0:128])),
                        reads=[f"psb{bank}"], writes=["halokv"])
                sc.op("t", (lambda t: [
                    t.matmul(ps[4][:, 0:256], hxv[:, c, :], wvlv[:, c, 0:256],
                             start=(c == 0), stop=(c == NB - 1))
                    for c in range(NB)][-1]),
                    reads=["wvlarea", "aqt"], writes=["psb4"])
                sc.op("v", (lambda v: v.tensor_copy(
                    hkv[:, 256:512], ps[4][:, 0:256])),
                    reads=["psb4"], writes=["halokv"])
            for th in range(2):
                c0 = th * 256
                sc.op("s", (lambda e, c0=c0: e.activation(
                    qe[:, 0:1536], qbv[:, 2:8, c0:c0 + 256], AF.Exp)),
                    reads=["qbf"], writes=["qearea"])
                sc.op("t", (lambda t, c0=c0, th=th: [
                    t.matmul(ps[4][0:12, th * 256:th * 256 + 256],
                             const_sb[:, CR.off["hsum12"] + 12 * cb:
                                      CR.off["hsum12"] + 12 * cb + 12],
                             qev[:, cb, :], start=(cb == 0), stop=(cb == 5))
                    for cb in range(6)][-1]),
                    reads=["qearea", "const"], writes=["psb4"])
                def _recip(v, th=th):
                    with nc.allow_low_precision("bf16 softmax recip"):
                        return v.reciprocal(
                            qrec[0:12, th * 256:th * 256 + 256],
                            ps[4][0:12, th * 256:th * 256 + 256])
                sc.op("v", _recip, reads=["psb4"], writes=["qrec"])
                for cb in range(6):
                    sc.op("t", (lambda t, cb=cb, th=th: t.matmul(
                        ps[5][:, 0:256],
                        const_sb[0:12, CR.off["hexp12"]:CR.off["hexp12"] + 384]
                        .bitcast(BF16)[:, 128 * cb:128 * cb + 128],
                        qrec[0:12, th * 256:th * 256 + 256],
                        start=True, stop=True)),
                        reads=["qrec", "const"], writes=["psb5"])
                    sc.op("v", (lambda v, cb=cb, th=th, c0=c0: v.tensor_tensor(
                        qsv[:, cb, c0:c0 + 256], qev[:, cb, :],
                        ps[5][:, 0:256], A.mult)),
                        reads=["qearea", "psb5"], writes=["qsm"])
            qsR4 = qsmR.rearrange("p (h q t) -> p h q t", h=6, q=2)
            dma("y", qsR4[0:64, :, 0, :], qsm[0:64, 0:3072]
                .rearrange("p (c t) -> p c t", c=6), reads=["qsm"],
                writes=["qearea", "qsm"])
            dma("y", qsR4[0:64, :, 1, :], qsm[64:128, 0:3072]
                .rearrange("p (c t) -> p c t", c=6), reads=["qsm"],
                writes=["qearea", "qsm"])
            for b in range(4):
                first = (tk == 0 and b == 0)
                if b == 0:
                    if tk == 0:
                        kprev = hkvv[:, 0:2, :]
                        vprev = hkv[:, 256:512]
                        pn = ["halokv"]
                    else:
                        kprev = klv[(tk - 1) % 2][:, :, 384:512]
                        vprev = vloc[(tk - 1) % 2][:, 3 * 256:4 * 256]
                        pn = [f"kloc{(tk - 1) % 2}", f"vloc{(tk - 1) % 2}"]
                else:
                    kprev = klv[tk % 2][:, :, (b - 1) * 128:b * 128]
                    vprev = vloc[tk % 2][:, (b - 1) * 256:b * 256]
                    pn = [f"kloc{tk % 2}", f"vloc{tk % 2}"]
                kself = klv[tk % 2][:, :, b * 128:(b + 1) * 128]
                vself = vloc[tk % 2][:, b * 256:(b + 1) * 256]
                sn = [f"kloc{tk % 2}", f"vloc{tk % 2}"]
                for par in range(2):
                    sc.op("v", (lambda v, par=par, b=b: v.tensor_scalar(
                        qmv[par][:, :, :], qbv[:, 0:2, b * 128:(b + 1) * 128],
                        pmask[:, par:par + 1], None, A.mult)),
                        reads=["qbf", "const"], writes=["halof"])
                def mmdots(t, kprev=kprev, kself=kself, b=b):
                    last = None
                    for h in range(4):
                        par, ob = h % 2, h // 2
                        if kprev.ndim == 3:
                            kp = kprev[:, ob, :]
                        else:
                            kp = kprev.rearrange("p (o t) -> p o t", o=2)[:, ob, :]
                        last = t.matmul(
                            ps[2][:, h * 128:(h + 1) * 128], kp,
                            qmv[par][:, ob, :], start=True, stop=True)
                        last = t.matmul(
                            ps[3][:, h * 128:(h + 1) * 128],
                            kself[:, ob, :],
                            qmv[par][:, ob, :], start=True, stop=True)
                    return last
                sc.op("t", mmdots, reads=pn + sn + ["halof"],
                      writes=["psb2", "psb3"])
                if first:
                    mprev = cview("mask_prev0").unsqueeze(1).broadcast_to(
                        [128, 4, 128])
                    sc.op("v", (lambda v, mprev=mprev: v.scalar_tensor_tensor(
                        ps[2][:, 0:512].rearrange("p (h t) -> p h t", h=4),
                        ps[2][:, 0:512].rearrange("p (h t) -> p h t", h=4),
                        0.125, mprev, A.mult, A.add)),
                        reads=["psb2", "const"], writes=["psb2"])
                else:
                    sc.op("v", lambda v: v.tensor_scalar(
                        ps[2][:, 0:512], ps[2][:, 0:512], 0.125, None, A.mult),
                        reads=["psb2"], writes=["psb2"])
                msf = cview("mask_self").unsqueeze(1).broadcast_to([128, 4, 128])
                sc.op("v", (lambda v, msf=msf: v.scalar_tensor_tensor(
                    ps[3][:, 0:512].rearrange("p (h t) -> p h t", h=4),
                    ps[3][:, 0:512].rearrange("p (h t) -> p h t", h=4),
                    0.125, msf, A.mult, A.add)),
                    reads=["psb3", "const"], writes=["psb3"])
                sc.op("s", lambda e: e.activation(
                    scp[:, 0:512], ps[2][:, 0:512], AF.Exp),
                    reads=["psb2"], writes=["wklarea"])
                sc.op("s", lambda e: e.activation(
                    scs[:, 0:512], ps[3][:, 0:512], AF.Exp),
                    reads=["psb3"], writes=["wklarea"])
                sc.op("t", lambda t: [
                    t.matmul(ps[5][0:1, 0:512], onesb[:, 0:1], scp[:, 0:512],
                             start=True, stop=False),
                    t.matmul(ps[5][0:1, 0:512], onesb[:, 0:1], scs[:, 0:512],
                             start=False, stop=True)][-1],
                    reads=["wklarea", "const"], writes=["psb5"])
                sc.op("v", lambda v: v.reciprocal(
                    recrow[0:1, 0:512], ps[5][0:1, 0:512]),
                    reads=["psb5"], writes=["wklarea"])
                def mmav(t, vprev=vprev, vself=vself):
                    last = None
                    for h in range(4):
                        out = ps[4][0:64, h * 128:(h + 1) * 128]
                        last = t.matmul(out, vprev[:, h * 64:(h + 1) * 64],
                                        scpv[:, h, :], start=True, stop=False)
                        last = t.matmul(out, vself[:, h * 64:(h + 1) * 64],
                                        scsv[:, h, :], start=False, stop=True)
                    return last
                sc.op("t", mmav, reads=pn + sn + ["wklarea"], writes=["psb4"])
                def mmrec(t):
                    last = None
                    for h in range(4):
                        last = t.matmul(
                            ps[5][0:64, h * 128:(h + 1) * 128],
                            ones[0:1, 0:64],
                            recrow[0:1, h * 128:(h + 1) * 128],
                            start=True, stop=True)
                    return last
                sc.op("t", mmrec, reads=["wklarea", "const"], writes=["psb5"])
                sc.op("s", lambda e: e.copy(rtile[0:64, 0:512],
                                            ps[5][0:64, 0:512]),
                      reads=["psb5"], writes=["halof"])
                sc.op("v", (lambda v: v.tensor_tensor(
                    ltile[0:64, 0:512], ps[4][0:64, 0:512],
                    rtile[0:64, 0:512], A.mult)),
                    reads=["psb4", "halof"], writes=["halof"])
                lt4 = ltile[0:64, 0:512].rearrange("p (h q t) -> p h q t",
                                                    h=2, q=2)
                dma("y", atv[0:64, 0:2, b * 128:(b + 1) * 128],
                    lt4[:, :, 0, :], reads=["halof"], writes=["xn"])
                dma("y", atv[64:128, 0:2, b * 128:(b + 1) * 128],
                    lt4[:, :, 1, :], reads=["halof"], writes=["xn"])
            for cl in range(8):
                cu = tk * 8 + cl
                crb = ctxrb[cu % 2]
                crn = "wvlarea"
                dma("y", crb[0:64, 0:784], scr["ctx"].ap()[cu],
                    reads=["scrctx"], writes=[crn])
                sc.op("v", lambda v: v.tensor_scalar(
                    arena[0:64, 18188:18200], ksacc[0:64, 0:12], EPS_CTX,
                    None, A.add), reads=["ksacc"], writes=["kstmp"])
                sc.op("v", lambda v: v.reciprocal(
                    arena[0:64, 18188:18200], arena[0:64, 18188:18200]),
                    reads=["kstmp"], writes=["kstmp"])
                sc.op("v", lambda v: v.tensor_tensor(
                    ctxs_bf[0:64, 0:768].rearrange("p (h e) -> p h e", h=12),
                    ctxacc[0:64, 0:768].rearrange("p (h e) -> p h e", h=12),
                    arena[0:64, 18188:18200].unsqueeze(2)
                    .broadcast_to([64, 12, 64]), A.mult),
                    reads=["ctxacc", "kstmp"], writes=["aqt"])
                def mmglob(t, cl=cl):
                    last = None
                    for gh in range(GLOBAL_HEADS):
                        bank, col = (4, gh * 64) if gh < 8 else \
                                    (5, (gh - 8) * 64)
                        last = t.matmul(
                            ps[bank][0:64, col:col + 64],
                            ctxs_bf[0:64, gh * 64:gh * 64 + 64],
                            qsRv[0:64, gh, cl * 64:(cl + 1) * 64],
                            start=True, stop=True)
                    return last
                sc.op("t", mmglob, reads=["aqt", "qearea", "qsm"],
                      writes=["psb4", "psb5"])
                ch = cl % 4
                sc.op("v", (lambda v, ch=ch: v.tensor_copy(
                    gsv[0:64, 0:8, ch * 64:(ch + 1) * 64],
                    ps[4][0:64, 0:512].rearrange("p (h e) -> p h e", h=8))),
                    reads=["psb4"], writes=["aqt"])
                sc.op("s", (lambda e, ch=ch: e.copy(
                    gsv[0:64, 8:12, ch * 64:(ch + 1) * 64],
                    ps[5][0:64, 0:256].rearrange("p (h e) -> p h e", h=4))),
                    reads=["psb5"], writes=["aqt"])
                sc.op("v", (lambda v, crb=crb: v.tensor_tensor(
                    ksacc[0:64, 0:12], ksacc[0:64, 0:12],
                    crb[0:64, 768:780], A.add)),
                    reads=[crn], writes=["ksacc"])
                sc.op("v", (lambda v, crb=crb: v.tensor_tensor(
                    ctxacc[0:64, 0:768], ctxacc[0:64, 0:768],
                    crb[0:64, 0:768], A.add)),
                    reads=[crn], writes=["ctxacc"])
                if cl % 4 == 3:
                    th0 = (cl // 4) * 256
                    gs4 = gstrip.rearrange("p (h q t) -> p h q t", h=6, q=2)
                    dma("y", atv[0:64, 2:8, th0:th0 + 256], gs4[0:64, :, 0, :],
                        reads=["aqt"], writes=["xn"])
                    dma("y", atv[64:128, 2:8, th0:th0 + 256], gs4[0:64, :, 1, :],
                        reads=["aqt"], writes=["xn"])
            t0 = tk * 512
            for o in range(NB):
                wb = wovs[o % 2]
                wn = "qearea"
                dma("y", wb, scr["aWo"].ap()[o].transpose([1, 0, 2]),
                    writes=[wn])
                bank = 2 + o % 2
                sc.op("t", (lambda t, wb=wb, bank=bank: [
                    t.matmul(ps[bank][:, 0:512], wb[:, c, :], atv[:, c, :],
                             start=(c == 0), stop=(c == NB - 1))
                    for c in range(NB)][-1]),
                    reads=[wn, "xn"], writes=[f"psb{bank}"])
                sc.op("v", (lambda v, bank=bank, o=o, t0=t0:
                            v.scalar_tensor_tensor(
                                xv[:, o, t0:t0 + 512], ps[bank][:, 0:512],
                                abo_sb[:, o:o + 1], xv[:, o, t0:t0 + 512],
                                A.add, A.add)),
                      reads=[f"psb{bank}", "abo"] + xnames(t0, t0 + 512),
                      writes=xnames(t0, t0 + 512))
        sc.fence()

    def emit_attn_b():
        arb = Arena()
        zc = arb.f32(0, 2048, "bz")
        zv = zc.rearrange("p (s f) -> p s f", s=2)
        ngT = zv
        ccs2 = arena[:, 0:1024]
        spc = arb.f32(2048, 2048, "bsp")
        spv = spc.rearrange("p (s f) -> p s f", s=2)
        aqt3 = spc
        xn3 = arb.bf16(4096, 2048, "xn")
        xnv3 = xn3.rearrange("p (c t) -> p c t", c=NB)
        bq = arb.bf16(5120, 2048, "bq")
        bqv = bq.rearrange("p (c t) -> p c t", c=NB)
        bkt = arb.bf16(6144, 2048, "bkt")
        bktv = bkt.rearrange("p (s f) -> p s f", s=2)
        bent = arb.bf16(7168, 2048, "bent")
        bentv = bent.rearrange("p (s f) -> p s f", s=2)
        bepv = bent.rearrange("p (c t) -> p c t", c=NB)
        bkf = arb.bf16(8192, 2048, "bkf")
        bkfv = bkf.rearrange("p (c t) -> p c t", c=NB)
        bvt = arb.bf16(9216, 2048, "bvt")
        bvtv = bvt.rearrange("p (s d) -> p s d", s=2)
        bog = arb.bf16(10240, 4096, "bog")
        bogv = bog.rearrange("p (c t) -> p c t", c=NB)
        ngFs = zc.rearrange("p (s c t) -> p s c t", s=2, c=NB)
        ngtots = arena[:, 16048:16080]
        ngtv = ngtots.rearrange("p (c u) -> p c u", c=NB)
        bqps = [arb.bf16(12288, 512, "bqps0"), arb.bf16(12544, 512, "bqps1")]
        bsA = arb.bf16(12800, 1024, "bsA")
        bw = [arb.bf16(13312, 2048, "bw0"), arb.bf16(14336, 2048, "bw1")]
        bwv = [w.rearrange("p (c o) -> p c o", c=NB) for w in bw]
        bsq = arb.f32(15360, 512, "bsq")
        rmsrow = arena[32:33, 15360:15872]
        ebuf = arena[0:2, 15360:15872]
        bD = arb.f32(15872, 32, "bD")
        bDv = bD.rearrange("p (c u) -> p c u", c=NB)
        bngc = arb.f32(15904, 8, "bngc")
        bngp = arb.f32(15912, 8, "bngp")
        bA = [arb.bf16(15920, 128, "bA0"), arb.bf16(15984, 128, "bA1")]
        pmask = cview("chind2")
        NT256 = T // 256
        emit_ln_stats(1.0)

        def ln_apply(t0, w, dstv, rbslot):
            j, c0 = t0 // 512, t0 % 512
            rb = rowb[rbslot % 2]
            rbn = f"rb{rbslot % 2}"
            src = stats[j:j + 1, :].rearrange("p (q c) -> p q c", q=2)[:, :, c0:c0 + w]
            dma("y", rb[0:1, 0:2 * w].rearrange("p (q c) -> p q c", q=2), src,
                reads=["stats"], writes=[rbn])
            obase = 32 * (1 + rbslot % 2)
            orow = ones[obase:obase + 1, 0:128]
            sc.op("t", (lambda t, rb=rb, orow=orow, w=w: t.matmul(
                ps[6][0:128, 0:w], orow, rb[0:1, 0:w],
                start=True, stop=True)), reads=[rbn, "const"], writes=["psb6"])
            sc.op("t", (lambda t, rb=rb, orow=orow, w=w: t.matmul(
                ps[7][0:128, 0:w], orow, rb[0:1, w:2 * w],
                start=True, stop=True)), reads=[rbn, "const"], writes=["psb7"])
            sc.op("v", (lambda v, t0=t0, w=w: v.scalar_tensor_tensor(
                aqt3[:, 0:8 * w], xv[:, :, t0:t0 + w], 1.0,
                ps[6][:, 0:w].unsqueeze(1).broadcast_to([128, 8, w]),
                A.mult, A.mult)),
                reads=xnames(t0, t0 + w) + ["psb6"], writes=["bsp"])
            sc.op("v", (lambda v, dstv=dstv, w=w: v.scalar_tensor_tensor(
                dstv[:, :, 0:w], aqt3[:, 0:8 * w], 1.0,
                ps[7][:, 0:w].unsqueeze(1).broadcast_to([128, 8, w]),
                A.mult, A.add)),
                reads=["bsp", "psb7"], writes=["xn"])

        def emit_group_out(g0):
            for c in range(NB):
                sc.op("s", (lambda e, c=c: e.activation(
                    bsq[:, 0:512], bogv[:, c, :], AF.Square)),
                    reads=["bog"], writes=["bsq"])
                sc.op("t", (lambda t, c=c: t.matmul(
                    ps[6][0:1, 0:512], ones[:, 0:1], bsq[:, 0:512],
                    start=(c == 0), stop=(c == NB - 1))),
                    reads=["bsq", "const"], writes=["psb6"])
            sc.op("v", lambda v: v.tensor_scalar(
                rmsrow[0:1, 0:512], ps[6][0:1, 0:512], INVD, None, A.mult),
                reads=["psb6"], writes=["bsq"])
            sc.op("s", lambda e: e.activation(
                rmsrow[0:1, 0:512], rmsrow[0:1, 0:512], AF.Ln, bias=EPS_RMS),
                reads=["bsq"], writes=["bsq"])
            sc.op("s", lambda e: e.activation(
                rmsrow[0:1, 0:512], rmsrow[0:1, 0:512], AF.Exp, scale=-0.5),
                reads=["bsq"], writes=["bsq"])
            sc.op("t", lambda t: t.matmul(
                ps[7][0:128, 0:512], ones[32:33, 0:128], rmsrow[0:1, 0:512],
                start=True, stop=True), reads=["bsq", "const"], writes=["psb7"])
            for c in range(NB):
                sc.op("v", (lambda v, c=c: v.scalar_tensor_tensor(
                    bogv[:, c, :], bogv[:, c, :], bgw_sb[:, c:c + 1],
                    ps[7][:, 0:512], A.mult, A.mult)),
                    reads=["psb7", "bgw"], writes=["bog"])
            for o in range(NB):
                wv, wn = wload("bWo", [o])
                bank = 4 + o % 2
                sc.op("t", (lambda t, wv=wv, bank=bank: [
                    t.matmul(ps[bank][:, 0:512], wv[:, c, 0:128], bogv[:, c, :],
                             start=(c == 0), stop=(c == NB - 1))
                    for c in range(NB)][-1]),
                    reads=[wn, "bog"], writes=[f"psb{bank}"])
                sc.op("v", (lambda v, bank=bank, o=o, g0=g0: v.tensor_tensor(
                    xv[:, o, g0:g0 + 512], ps[bank][:, 0:512],
                    xv[:, o, g0:g0 + 512], A.add)),
                    reads=[f"psb{bank}"] + xnames(g0, g0 + 512),
                    writes=xnames(g0, g0 + 512))

        sc.op("v", lambda v: v.memset(S_state[:, :], 0.0), writes=["S"])
        sc.op("v", lambda v: v.memset(bngc[:, 0:8], 0.0), writes=["bngc"])
        wcnt = [0]

        def wload(nm, obs):
            i = wcnt[0] % 2
            wcnt[0] += 1
            wn = f"bw{i}"
            for k, ob in enumerate(obs):
                dma("y", bwv[i][:, :, k * 128:(k + 1) * 128],
                    scr[nm].ap()[ob].transpose([1, 0, 2]), writes=[wn])
            return bwv[i][:, :, 0:128 * len(obs)], wn

        for tk in range(NT256):
            t0 = tk * 256
            ln_apply(t0, 256, xnv3, tk)
            for nm, which in (("bWf", "z"), ("bWi", "v")):
                for og in range(4):
                    wv, wn = wload(nm, [og * 2, og * 2 + 1])
                    for slot in range(2):
                        bank = 2 + slot
                        def mmz(t, wv=wv, slot=slot, bank=bank):
                            last = None
                            for c in range(NB):
                                last = t.matmul(
                                    ps[bank][:, 0:256],
                                    xnv3[:, c, slot * 128:(slot + 1) * 128],
                                    wv[:, c, :],
                                    start=(c == 0), stop=(c == NB - 1))
                            return last
                        dstv_ = zv if which == "z" else bvtv
                        dn = "bz" if which == "z" else "bvt"
                        sc.op("t", mmz, reads=["xn", wn], writes=[f"psb{bank}"])
                        sc.op("v" if slot == 0 else "s",
                              (lambda e, slot=slot, og=og, bank=bank,
                               dstv_=dstv_, sv=(slot == 1):
                               (e.copy if sv else e.tensor_copy)(
                                  dstv_[:, slot, og * 256:(og + 1) * 256],
                                  ps[bank][:, 0:256])),
                              reads=[f"psb{bank}"], writes=[dn])
            for slot in range(2):
                sc.op("s", (lambda e, slot=slot: e.activation(
                    spv[:, slot, :], zv[:, slot, :], AF.Exp, scale=-1.0)),
                    reads=["bz"], writes=["bsp"])
                sc.op("s", (lambda e, slot=slot: e.activation(
                    spv[:, slot, :], spv[:, slot, :], AF.Ln, bias=1.0)),
                    reads=["bsp"], writes=["bsp"])
                sc.op("s", (lambda e, slot=slot: e.activation(
                    bentv[:, slot, :], spv[:, slot, :], AF.Exp, scale=-1.0)),
                    reads=["bsp"], writes=["bent"])
                sc.op("v", (lambda v, slot=slot: v.tensor_scalar(
                    bktv[:, slot, :], bentv[:, slot, :], -1.0, 1.0,
                    A.mult, A.add)),
                    reads=["bent"], writes=["bkt"])
            kubuf = bsA
            for slot in range(2):
                sc.op("t", (lambda t, slot=slot: [
                    t.matmul(ps[4 + fh][:, 0:512], cview("tri64"),
                             spv[:, slot, fh * 512:(fh + 1) * 512],
                             start=True, stop=True)
                    for fh in range(2)][-1]),
                    reads=["bsp", "const"], writes=["psb4", "psb5"])
                for fh in range(2):
                    sc.op("s", (lambda e, slot=slot, fh=fh: e.activation(
                        bentv[:, slot, fh * 512:(fh + 1) * 512],
                        ps[4 + fh][:, 0:512], AF.Exp)),
                        reads=[f"psb{4 + fh}"], writes=["bent"])
                    sc.op("v", (lambda v, slot=slot, fh=fh: v.tensor_copy(
                        ngT[:, slot, fh * 512:(fh + 1) * 512],
                        ps[4 + fh][:, 0:512])),
                        reads=[f"psb{4 + fh}"], writes=["bz"])
                sc.op("v", (lambda v, slot=slot: v.tensor_tensor(
                    bktv[:, slot, :], bktv[:, slot, :], bentv[:, slot, :],
                    A.mult)), reads=["bent"], writes=["bkt"])
                for half in range(2):
                    bank = 4 + half
                    sc.op("t", (lambda t, slot=slot, half=half, bank=bank: [
                        t.transpose(ps[bank][:, i * 64:(i + 1) * 64]
                                    .bitcast(BF16),
                                    bktv[:, slot, (half * 4 + i) * 128:
                                         (half * 4 + i + 1) * 128], identb)
                        for i in range(4)][-1]),
                        reads=["bkt", "const"], writes=[f"psb{bank}"])
                    sc.op("v", (lambda v, slot=slot, half=half, bank=bank:
                                v.tensor_copy(
                                    bkfv[:, half * 4:half * 4 + 4,
                                         slot * 128:(slot + 1) * 128],
                                    ps[bank][:, 0:256].bitcast(BF16)
                                    .rearrange("p (c t) -> p c t", c=4))),
                          reads=[f"psb{bank}"], writes=["bkf"])

            for slot in range(2):
                for half in range(2):
                    bank = 4 + half
                    sc.op("t", (lambda t, slot=slot, half=half, bank=bank: [
                        t.transpose(ps[bank][:, i * 128:(i + 1) * 128],
                                    ngT[:, slot, (half * 4 + i) * 128:
                                        (half * 4 + i + 1) * 128], ident)
                        for i in range(4)][-1]),
                        reads=["bz", "const"], writes=[f"psb{bank}"])
                    sc.op("v", (lambda v, slot=slot, half=half, bank=bank:
                                v.tensor_copy(
                                    ngFs[:, slot, half * 4:half * 4 + 4, :],
                                    ps[bank][:, 0:512]
                                    .rearrange("p (c t) -> p c t", c=4))),
                          reads=[f"psb{bank}"], writes=["bz"])
            ngF5 = zc.rearrange("p (s c q t) -> p s c q t", s=2, c=NB, q=2)
            sc.op("v", lambda v: v.tensor_copy(
                ngtots.rearrange("p (c s q) -> p c s q", c=NB, s=2),
                ngF5[:, :, :, :, 63:64].transpose([0, 2, 1, 3, 4])
                .rearrange("p c s q one -> p c s (q one)")),
                reads=["bz"], writes=["bngt"])
            sc.op("s", lambda e: e.activation(
                bD[:, 0:32], ngtots[:, 0:32], AF.Exp, scale=-1.0),
                reads=["bngt"], writes=["bD"])
            sc.op("s", lambda e: e.activation(
                bent[:, 0:2048].rearrange("p (c s t) -> p s c t", s=2, c=NB),
                ngFs[:, :, :, :], AF.Exp, scale=-1.0,
                bias=float(SCALAR_CONSTS["lnscale_b"])),
                reads=["bz"], writes=["bent"])
            for o in range(NB):
                wv, wn = wload("bWq", [o])
                bank = 2 + o % 2
                sc.op("t", (lambda t, wv=wv, bank=bank: [
                    t.matmul(ps[bank][:, 0:256], wv[:, c, 0:128],
                             xnv3[:, c, :], start=(c == 0), stop=(c == NB - 1))
                    for c in range(NB)][-1]),
                    reads=[wn, "xn"], writes=[f"psb{bank}"])
                sc.op("s", (lambda e, bank=bank, o=o: e.activation(
                    bqv[:, o, :], ps[bank][:, 0:256], AF.Silu)),
                    reads=[f"psb{bank}"], writes=["bq"])
            sc.op("v", lambda v: v.tensor_tensor(
                bq[:, 0:2048], bq[:, 0:2048], bent[:, 0:2048], A.mult),
                reads=["bent"], writes=["bq"])
            if tk < 2:
                for cl in range(4):
                    cu = tk * 4 + cl
                    sc.op("s", lambda e: e.activation(
                        bngp[:, 0:8], bngc[:, 0:8], AF.Exp, scale=-1.0),
                        reads=["bngc"], writes=["bngp"])
                    qp = bqps[cu % 2]
                    qn = f"bqps{cu % 2}"
                    sc.op("v", (lambda v, cl=cl, qp=qp: v.tensor_tensor(
                        qp[:, 0:512].rearrange("p (c t) -> p c t", c=NB),
                        bqv[:, :, cl * 64:(cl + 1) * 64],
                        bngp[:, 0:8].unsqueeze(2).broadcast_to([128, 8, 64]),
                        A.mult)), reads=["bq", "bngp"], writes=[qn])
                    dma("y", scr["qp0"].ap()[cu], qp[:, 0:512], reads=[qn], writes=["scrqp0"])
                    sc.op("v", (lambda v, cl=cl: v.tensor_tensor(
                        bngc[:, 0:8], bngc[:, 0:8],
                        ngtv[:, :, cl:cl + 1], A.add)),
                        reads=["bngt"], writes=["bngc"])
            for slot in range(2):
                obank = [2 * slot, 2 * slot + 1]
                for h in range(HEADS_B):
                    pa = 4 + h % 2
                    sc.op("t", (lambda t, slot=slot, h=h, pa=pa: t.matmul(
                        ps[pa][:, 0:128], bkfv[:, h, slot * 128:(slot + 1) * 128],
                        bqv[:, h, slot * 128:(slot + 1) * 128],
                        start=True, stop=True)),
                        reads=["bkf", "bq"], writes=[f"psb{pa}"])
                    sc.op("v", (lambda t_, h=h, pa=pa: t_.tensor_tensor(
                        bA[h % 2][:, 0:128], ps[pa][:, 0:128],
                        cview("amask").bitcast(BF16), A.mult)),
                        reads=[f"psb{pa}", "const"], writes=[f"bA{h % 2}"])
                    ob = obank[h // 4]
                    sc.op("t", (lambda t, slot=slot, h=h, ob=ob: t.matmul(
                        ps[ob][:, (h % 4) * 128:(h % 4) * 128 + 128],
                        bvtv[:, slot, h * 128:(h + 1) * 128],
                        bA[h % 2][:, 0:128], start=(h % 4 == 0), stop=False)),
                        reads=[f"bA{h % 2}", "bvt"], writes=[f"psb{ob}"])
                for cpar in range(2):
                    cl = slot * 2 + cpar
                    cu = tk * 4 + cl
                    sb = cpar * 64
                    def mminter(t, slot=slot, cpar=cpar, sb=sb, obank=tuple(obank)):
                        last = None
                        for h in range(HEADS_B):
                            ob = obank[h // 4]
                            last = t.matmul(
                                ps[ob][:, (h % 4) * 128 + sb:
                                       (h % 4) * 128 + sb + 64],
                                S_state[:, h * 128:(h + 1) * 128].bitcast(BF16)
                                if False else
                                S_state[:, h * 128:(h + 1) * 128],
                                bqv[:, h, slot * 128 + sb:slot * 128 + sb + 64],
                                start=False, stop=(cpar == 1 and h % 4 == 3))
                        return last
                    sc.op("t", mminter, reads=["S", "bq"],
                          writes=[f"psb{obank[0]}", f"psb{obank[1]}"])
                    for fh in range(2):
                        sc.op("t", (lambda t, slot=slot, cpar=cpar, fh=fh:
                                    t.matmul(
                                        ps[6 + fh][0:1, 0:512],
                                        pmask[:, cpar:cpar + 1],
                                        spv[:, slot, fh * 512:(fh + 1) * 512],
                                        start=True, stop=True)),
                              reads=["bsp", "const"], writes=[f"psb{6 + fh}"])
                        sc.op("s", (lambda e, fh=fh: e.activation(
                            ebuf[0:1, 0:512], ps[6 + fh][0:1, 0:512], AF.Exp,
                            scale=-1.0)),
                            reads=[f"psb{6 + fh}"], writes=["bsq"])
                        sc.op("t", (lambda t, cpar=cpar, fh=fh: t.matmul(
                            ps[6 + fh][0:128, 0:512],
                            cview("pind2")[0:1, cpar * 128:cpar * 128 + 128],
                            ebuf[0:1, 0:512], start=True, stop=True)),
                            reads=["bsq", "const"], writes=[f"psb{6 + fh}"])
                        sc.op("v", (lambda v, slot=slot, fh=fh:
                                    v.tensor_tensor(
                                        kubuf[:, fh * 512:(fh + 1) * 512],
                                        bktv[:, slot, fh * 512:(fh + 1) * 512],
                                        ps[6 + fh][:, 0:512], A.mult)),
                              reads=[f"psb{6 + fh}", "bkt"], writes=["bsA"])
                    def mmu(t, slot=slot):
                        last = None
                        for h in range(HEADS_B):
                            bank = 6 + h // 4
                            last = t.matmul(
                                ps[bank][:, (h % 4) * 128:(h % 4) * 128 + 128],
                                kubuf[:, h * 128:(h + 1) * 128],
                                bvtv[:, slot, h * 128:(h + 1) * 128],
                                start=True, stop=True)
                        return last
                    sc.op("t", mmu, reads=["bsA", "bvt"],
                          writes=["psb6", "psb7"])
                    def scan(v, cl=cl):
                        last = None
                        for h in range(HEADS_B):
                            bank = 6 + h // 4
                            last = v.scalar_tensor_tensor(
                                S_state[:, h * 128:(h + 1) * 128],
                                S_state[:, h * 128:(h + 1) * 128],
                                bD[:, h * 4 + cl:h * 4 + cl + 1],
                                ps[bank][:, (h % 4) * 128:(h % 4) * 128 + 128],
                                A.mult, A.add)
                        return last
                    sc.op("v", scan, reads=["psb6", "psb7", "bD"],
                          writes=["S"])
                for bi in range(2):
                    ob = obank[bi]
                    sc.op("v" if bi == 0 else "s",
                          (lambda e, ob=ob, bi=bi, slot=slot, tk=tk,
                           sv=(bi == 1): (e.copy if sv else e.tensor_copy)(
                              bogv[:, bi * 4:bi * 4 + 4,
                                   (tk % 2) * 256 + slot * 128:
                                   (tk % 2) * 256 + slot * 128 + 128],
                              ps[ob][:, 0:512]
                              .rearrange("p (c t) -> p c t", c=4))),
                          reads=[f"psb{ob}"], writes=["bog"])
            if tk % 2 == 1:
                g0 = (tk - 1) * 256
                if g0 == 0:
                    dma("y", scr["og0"].ap()[:, :], bog[:, 0:4096], reads=["bog"], writes=["scrog0"])
                else:
                    emit_group_out(g0)
        flagC = cview("flags")[:, 0:1]
        flagU = cview("flags")[:, 1:2]
        sc.op("v", lambda v: v.tensor_scalar(
            ccs2[:, 0:1024], S_state[:, 0:1024], flagC, None, A.mult),
            reads=["S", "const"], writes=["bz"])
        dma("y", cc2_in.ap(), ccs2[:, 0:1024], reads=["bz"], writes=["cc2i"])
        if no_coll:
            dma("y", cc2_out.ap(), cc2_in.ap(), reads=["cc2i"], writes=["cc2o"])
        else:
            sc.op("g", lambda g: g.collective_compute(
                "AllReduce", A.add, replica_groups=RGROUPS,
                ins=[cc2_in.ap().opt()], outs=[cc2_out.ap().opt()]),
                reads=["cc2i"], writes=["cc2o"])
        dma("y", ccs2[:, 0:1024], cc2_out.ap(), reads=["cc2o"], writes=["bz"])
        sc.op("v", lambda v: v.tensor_scalar(
            bsA[:, 0:1024], ccs2[:, 0:1024], flagU, None, A.mult),
            reads=["bz", "const"], writes=["bsA"])
        qpb = arena[:, 8192:10240].bitcast(BF16)
        qpv = qpb.rearrange("p (u c t) -> p u c t", u=8, c=NB)
        dma("y", bog[:, 0:4096], scr["og0"].ap()[:, :], reads=["scrog0"], writes=["bog"])
        for cu in range(8):
            dma("y", qpb[:, cu * 512:(cu + 1) * 512], scr["qp0"].ap()[cu],
                reads=["scrqp0"], writes=["bkf", "bvt"])
        for h in range(HEADS_B):
            sc.op("t", (lambda t, h=h: t.matmul(
                ps[6][:, 0:512], bsA[:, h * 128:(h + 1) * 128],
                qpv[:, :, h, :], start=True, stop=True)),
                reads=["bsA", "bkf", "bvt"], writes=["psb6"])
            sc.op("v", (lambda v, h=h: v.tensor_tensor(
                bogv[:, h, :], bogv[:, h, :], ps[6][:, 0:512], A.add)),
                reads=["psb6"], writes=["bog"])
        emit_group_out(0)
        sc.fence()

    def emit_ffn():
        h_bf = ABf(0, 16384)
        hvb = h_bf.rearrange("p (c t) -> p c t", c=NH)
        xi = ABf(32, 4096)
        xivb = xi.rearrange("p (c t) -> p c t", c=NB)
        qt = AFr(40, 2048)
        qtmp = [AFr(48, 512), AFr(50, 512)]
        wb1 = [ABf(52, 1024), ABf(54, 1024)]
        wb2 = [ABf(56, 2048), ABf(60, 2048)]
        emit_ln_stats(16.0)
        for tk in range(NT512):
            pS = ps[0 if tk % 2 == 0 else 6]
            pB = ps[1 if tk % 2 == 0 else 7]
            nS = "psb0" if tk % 2 == 0 else "psb6"
            nB = "psb1" if tk % 2 == 0 else "psb7"
            rb = rowb[tk % 2]
            rbn = f"rb{tk % 2}"
            dma("y", rb, stats[tk:tk + 1, 0:1024], reads=["stats"], writes=[rbn])
            obase = 32 * (1 + tk % 2)
            orow = ones[obase:obase + 1, 0:128]
            sc.op("t", (lambda t, pS=pS, rb=rb, orow=orow: t.matmul(
                pS[0:128, 0:512], orow, rb[0:1, 0:512],
                start=True, stop=True)), reads=[rbn, "const"], writes=[nS])
            sc.op("t", (lambda t, pB=pB, rb=rb, orow=orow: t.matmul(
                pB[0:128, 0:512], orow, rb[0:1, 512:1024],
                start=True, stop=True)), reads=[rbn, "const"], writes=[nB])
            t0 = tk * 512
            for th in range(2):
                sl = slice(t0 + th * 256, t0 + th * 256 + 256)
                xsrc = xv[:, :, sl]
                sc.op("v", (lambda v, xsrc=xsrc, pS=pS, th=th: v.scalar_tensor_tensor(
                    qt[:, 0:2048], xsrc, 1.0,
                    pS[:, th * 256:(th + 1) * 256].unsqueeze(1)
                    .broadcast_to([128, 8, 256]),
                    A.mult, A.mult)),
                    reads=xnames(t0, t0 + 512) + [nS], writes=["qtarea"])
                sc.op("v", (lambda v, pB=pB, th=th: v.scalar_tensor_tensor(
                    qt[:, 0:2048], qt[:, 0:2048], 1.0,
                    pB[:, th * 256:(th + 1) * 256].unsqueeze(1)
                    .broadcast_to([128, 8, 256]),
                    A.mult, A.add)),
                    reads=[nB], writes=["qtarea"])
                sc.op("g", lambda g: g.tensor_scalar(
                    qt[:, 0:2048], qt[:, 0:2048], -128.0, 127.0, A.max, A.min),
                    reads=["qtarea"], writes=["qtarea"])
                xdst = xivb[:, :, th * 256:(th + 1) * 256]
                sc.op("v", (lambda v, xdst=xdst: v.tensor_scalar(
                    xdst, qt[:, 0:2048], MAGIC, -MAGIC, A.add, A.add)),
                    reads=["qtarea"], writes=["xi"])
            for o1 in range(NH):
                wb = wb1[o1 % 2]
                wn = f"wb1_{o1 % 2}"
                wvb = wb.rearrange("p (c o) -> p c o", c=NB)
                dma("y", wvb, scr["w1q"].ap()[o1].transpose([1, 0, 2]),
                    writes=[wn])
                bank = 2 + o1 % 2
                bn = f"psb{bank}"
                sc.op("t", (lambda t, wvb=wvb, bank=bank: [
                    t.matmul(ps[bank][:, 0:512], wvb[:, c, :], xivb[:, c, :],
                             start=(c == 0), stop=(c == NB - 1))
                    for c in range(NB)][-1]),
                    reads=[wn, "xi"], writes=[bn])
                qm = qtmp[o1 % 2]
                qmn = f"qtmp{o1 % 2}"
                e1 = "v" if o1 % 2 == 0 else "s"
                if e1 == "v":
                    sc.op("v", (lambda v, bank=bank, qm=qm, o1=o1: v.tensor_scalar(
                        qm[:], ps[bank][:, 0:512], b1q16[:, o1:o1 + 1], 0.0,
                        A.add, A.max)), reads=[bn, "b1q16"], writes=[qmn])
                else:
                    sc.op("s", (lambda e, bank=bank, qm=qm, o1=o1: e.activation(
                        qm[:], ps[bank][:, 0:512], AF.Relu,
                        bias=b1q16[:, o1:o1 + 1])), reads=[bn, "b1q16"],
                        writes=[qmn])
                sc.op("g", (lambda g, qm=qm: g.tensor_scalar(
                    qm[:], qm[:], 127.0, MAGIC, A.min, A.add)),
                    reads=[qmn], writes=[qmn])
                e3 = "s" if o1 % 2 == 0 else "v"
                if e3 == "s":
                    sc.op("s", (lambda e, qm=qm, o1=o1: e.activation(
                        hvb[:, o1, :], qm[:], AF.Copy, bias=-MAGIC)),
                        reads=[qmn], writes=[f"h{o1}"])
                else:
                    sc.op("v", (lambda v, qm=qm, o1=o1: v.tensor_scalar(
                        hvb[:, o1, :], qm[:], -MAGIC, None, A.add)),
                        reads=[qmn], writes=[f"h{o1}"])
            for o2 in range(NB):
                bank = 4 + o2 % 2
                bn = f"psb{bank}"
                for hh in range(2):
                    wb = wb2[(o2 * 2 + hh) % 2]
                    wn = f"wb2_{(o2 * 2 + hh) % 2}"
                    wvb = wb.rearrange("p (c o) -> p c o", c=16)
                    dma("y", wvb,
                        scr["w2q"].ap()[o2, hh * 16:(hh + 1) * 16]
                        .transpose([1, 0, 2]), writes=[wn])
                    sc.op("t", (lambda t, wvb=wvb, bank=bank, hh=hh: [
                        t.matmul(ps[bank][:, 0:512], wvb[:, c, :],
                                 hvb[:, hh * 16 + c, :],
                                 start=(hh == 0 and c == 0),
                                 stop=(hh == 1 and c == 15))
                        for c in range(16)][-1]),
                        reads=[wn] + [f"h{hh * 16 + c}" for c in range(16)],
                        writes=[bn])
                sc.op("v", (lambda v, bank=bank, o2=o2, t0=t0: v.scalar_tensor_tensor(
                    xv[:, o2, t0:t0 + 512], ps[bank][:, 0:512],
                    b2q[:, o2:o2 + 1], xv[:, o2, t0:t0 + 512],
                    A.add, A.add)),
                    reads=[bn, "b2q"] + xnames(t0, t0 + 512),
                    writes=xnames(t0, t0 + 512))
        sc.fence()

    if "attn_a" in phases:
        emit_attn_a()
    if "ffn1" in phases:
        emit_ffn()
    if "attn_b" in phases:
        emit_attn_b()
    if "ffn2" in phases:
        emit_ffn()

    sbuf_store = [AFr(0, 1024), AFr(4, 1024)]
    store_ops = []
    for i in range(NT128):
        tb = sbuf_store[i % 2]
        tbn = f"stb{i % 2}"
        for h in range(2):
            bank = 2 + h
            sc.op("t", (lambda t, i=i, h=h, bank=bank: [
                t.transpose(ps[bank][:, c * 128:(c + 1) * 128],
                            xv[:, 4 * h + c, i * 128:(i + 1) * 128], ident)
                for c in range(4)][-1]),
                reads=[f"x_{i}", "const"], writes=[f"psb{bank}"])
            sc.op("v" if h == 0 else "s",
                  (lambda e, h=h, bank=bank, tb=tb, s=(h == 1):
                   (e.copy if s else e.tensor_copy)(
                       tb[:, h * 512:(h + 1) * 512], ps[bank][:, 0:512])),
                  reads=[f"psb{bank}"], writes=[tbn])
        store_ops.append(
            dma("y", y_out[i * 128:(i + 1) * 128, :], tb[:], reads=[tbn]))

    sc.op("g", lambda g: g.nop(), extra=store_ops)

    sc.finalize()
    with nc.Block() as block:
        @block.tensor
        def _(t):
            sc.run("t", t)

        @block.vector
        def _(v):
            sc.run("v", v)

        @block.scalar
        def _(s):
            sc.run("s", s)

        @block.gpsimd
        def _(g):
            sc.run("g", g)

        @block.sync
        def _(y):
            sc.run("y", y)

    return nc


_NC_CACHE = {}


def get_nc(T, n_cores=8):
    key = (T, n_cores)
    if key not in _NC_CACHE:
        _NC_CACHE[key] = build_nc(T, n_cores=n_cores)
    return _NC_CACHE[key]


def make_wmap(inputs):
    wmap = {
        "aWq": inputs["L0_attn_a_Wq"], "aWk": inputs["L0_attn_a_Wk"],
        "aWv": inputs["L0_attn_a_Wv"], "aWo": inputs["L0_attn_a_Wo"],
        "abo": np.asarray(inputs["L0_attn_a_bo"], np.float32).reshape(1, D),
        "bWq": inputs["L0_attn_b_Wq"], "bWf": inputs["L0_attn_b_Wf"],
        "bWi": inputs["L0_attn_b_Wi"], "bWo": inputs["L0_attn_b_Wo"],
        "bgw": np.asarray(inputs["L0_attn_b_g_weight"], np.float32).reshape(1, D),
        "fw1": inputs["L0_ffn_w1"],
        "fb1": np.asarray(inputs["L0_ffn_b1"], np.float32).reshape(1, HID),
        "fw2": inputs["L0_ffn_w2"],
        "fb2": np.asarray(inputs["L0_ffn_b2"], np.float32).reshape(1, D),
    }
    return {k: np.ascontiguousarray(np.asarray(v, np.float32))
            for k, v in wmap.items()}


def kernel(**inputs):
    x = np.asarray(inputs["L0_attn_a_x"], np.float32)
    Bz, S, _ = x.shape
    T = S // 2
    nc = get_nc(T, 2 * Bz)
    wmap = make_wmap(inputs)
    consts = [make_consts(False), make_consts(True)]
    zero_halo = np.zeros((WINDOW, D), np.float32)
    in_maps = []
    for b in range(Bz):
        for h in range(2):
            m = dict(wmap)
            m["x_in"] = np.ascontiguousarray(x[b, h * T:(h + 1) * T, :])
            m["halo_in"] = (np.ascontiguousarray(x[b, T - WINDOW:T, :]) if h == 1
                            else zero_halo)
            m["consts"] = consts[h]
            in_maps.append(m)
    res = run_bass_kernel_spmd(nc, in_maps, core_ids=list(range(2 * Bz)))
    out = np.empty((Bz, S, D), np.float32)
    for b in range(Bz):
        for h in range(2):
            out[b, h * T:(h + 1) * T, :] = res.results[2 * b + h]["y_out"]
    return out


# version 06
# speedup vs baseline: 1.3429x; 1.3429x over previous
import numpy as np
import concourse.bass as bass
import concourse.mybir as mybir
from concourse.bass_utils import run_bass_kernel_spmd
from concourse.alu_op_type import AluOpType as A
from contextlib import ExitStack

F32 = mybir.dt.float32
BF16 = mybir.dt.bfloat16
AF = mybir.ActivationFunctionType

D = 1024
NB = D // 128
WINDOW = 128
CHUNK = 64
LOCAL_HEADS = 4
GLOBAL_HEADS = 12
DH = 64
HEADS_B = 8
DF = 128
HID = 4096
NH = HID // 128
MAGIC = float(np.float32(1.5 * 2.0 ** 23))
NEG = -1.0e30
EPS_LN = 1e-5
EPS_RMS = 1e-5
EPS_CTX = 1e-6
INVD = 1.0 / D


class ConstReg:
    def __init__(self):
        self.off = {}
        self.w = {}
        self.n = 0

    def reg(self, name, w):
        self.off[name] = self.n
        self.w[name] = w
        self.n += w


CR = ConstReg()
CR.reg("ident", 128)
CR.reg("identb", 64)
CR.reg("tri64", 128)
CR.reg("mask_self", 128)
CR.reg("mask_prev0", 128)
CR.reg("amask", 64)
CR.reg("ones", 128)
CR.reg("onesb", 8)
CR.reg("chind2", 2)
CR.reg("pind2", 256)
CR.reg("selcols", 64)
CR.reg("hsum12", 72)
CR.reg("hexp12", 384)
CR.reg("flags", 4)
CR.reg("cscal", 8)
SCALAR_CONSTS = {
    "eps": 1e-5,
    "negmagic": -MAGIC,
    "lnscale_b": float(-0.5 * np.log(128.0)),
    "epsctx": EPS_CTX,
    "one": 1.0,
}
CONST_COLS = ((CR.n + 15) // 16) * 16


def make_consts(is_half1):
    import ml_dtypes
    c = np.zeros((128, CONST_COLS), np.float32)

    def put(name, arr):
        o = CR.off[name]
        c[:, o:o + CR.w[name]] = arr

    put("ident", np.eye(128, dtype=np.float32))
    ibb = np.ascontiguousarray(np.eye(128, dtype=np.float32).astype(ml_dtypes.bfloat16))
    put("identb", ibb.view(np.float32))
    tri = np.zeros((128, 128), np.float32)
    for blk in range(2):
        for tt in range(64):
            tri[blk * 64:blk * 64 + tt + 1, blk * 64 + tt] = 1.0
    put("tri64", tri)
    ms = np.where(np.arange(128)[:, None] > np.arange(128)[None, :], NEG, 0.0)
    put("mask_self", ms.astype(np.float32))
    put("mask_prev0", np.zeros((128, 128), np.float32) if is_half1
        else np.full((128, 128), NEG, np.float32))
    am = np.zeros((128, 128), np.float32)
    for blk in range(2):
        for tt in range(64):
            am[blk * 64:blk * 64 + tt + 1, blk * 64 + tt] = 1.0
    put("amask", np.ascontiguousarray(am.astype(ml_dtypes.bfloat16)).view(np.float32))
    put("ones", np.ones((128, 128), np.float32))
    put("onesb", np.ascontiguousarray(
        np.ones((128, 16), np.float32).astype(ml_dtypes.bfloat16)).view(np.float32))
    ci = np.zeros((128, 2), np.float32)
    ci[0:64, 0] = 1.0
    ci[64:128, 1] = 1.0
    put("chind2", ci)
    pi = np.zeros((128, 256), np.float32)
    pi[0, 0:64] = 1.0
    pi[0, 128 + 64:256] = 1.0
    put("pind2", pi)
    sel = np.zeros((128, 64), np.float32)
    for j in range(8):
        sel[:, 8 * j + j] = 1.0
    put("selcols", sel)
    hs = np.zeros((128, 72), np.float32)
    he = np.zeros((128, 768), np.float32)
    for cb in range(6):
        hs[0:64, 12 * cb + 2 * cb] = 1.0
        hs[64:128, 12 * cb + 2 * cb + 1] = 1.0
        he[2 * cb, 128 * cb:128 * cb + 64] = 1.0
        he[2 * cb + 1, 128 * cb + 64:128 * cb + 128] = 1.0
    put("hsum12", hs)
    put("hexp12", np.ascontiguousarray(
        he.astype(ml_dtypes.bfloat16)).view(np.float32))
    fl = np.zeros((128, 4), np.float32)
    fl[:, 0] = 0.0 if is_half1 else 1.0
    fl[:, 1] = 1.0 if is_half1 else 0.0
    put("flags", fl)
    cs = np.zeros((128, 8), np.float32)
    for i, v in enumerate(SCALAR_CONSTS.values()):
        cs[:, i] = np.float32(v)
    put("cscal", cs)
    return c


class Op:
    __slots__ = ("eng", "fn", "deps", "dma", "needed", "idx", "sem", "val", "inc")

    def __init__(self, eng, fn, deps, dma, idx):
        self.eng, self.fn, self.deps, self.dma, self.idx = eng, fn, list(deps), dma, idx
        self.needed = dma
        self.sem = None
        self.val = 0
        self.inc = 16 if dma else 1


class Buf:
    __slots__ = ("w", "r")

    def __init__(self):
        self.w = None
        self.r = []


class Sched:
    NPOOL = 4

    def __init__(self, nc, stack, n_dma_sems=48):
        self.nc = nc
        self.ops = []
        self.eng_sem = {e: [stack.enter_context(nc.semaphore(f"es_{e}{i}"))
                            for i in range(self.NPOOL)]
                        for e in ("t", "v", "s", "g", "y")}
        self.dma_sems = [stack.enter_context(nc.semaphore(f"ds_{i}"))
                         for i in range(n_dma_sems)]
        self.dma_i = 0
        self._stack = stack
        self.gdma_sems = []
        self.bufs = {}
        self.last = {e: None for e in ("t", "v", "s", "g", "y")}
        self.dma_open = []

    def B(self, name):
        if name not in self.bufs:
            self.bufs[name] = Buf()
        return self.bufs[name]

    def op(self, eng, fn, reads=(), writes=(), dma=False, extra=()):
        psr = [b for b in reads if b.startswith("psb")]
        if psr:
            reads = [b for b in reads if not b.startswith("psb")]
            writes = list(writes) + [b for b in psr if b not in writes]
        deps = list(extra)
        for b in reads:
            bb = self.B(b)
            if bb.w is not None:
                deps.append(bb.w)
        for b in writes:
            bb = self.B(b)
            deps.extend(bb.r)
            if bb.w is not None:
                deps.append(bb.w)
        o = Op(eng, fn, [d for d in deps if d is not None], dma, len(self.ops))
        self.ops.append(o)
        self.last[eng] = o
        if dma:
            self.dma_open.append(o)
        for b in reads:
            self.B(b).r.append(o)
        for b in writes:
            bb = self.B(b)
            bb.w = o
            bb.r = []
        return o

    def fence(self):
        prev = [o for o in self.last.values() if o is not None] + self.dma_open
        self.dma_open = []
        for e in ("t", "v", "s", "g", "y"):
            self.op(e, lambda eng: eng.nop(), extra=prev)
        f = {e: self.last[e] for e in self.last}
        for b in self.bufs.values():
            if b.w is not None:
                b.w = f[b.w.eng]
            b.r = []

    def finalize(self):
        for op in self.ops:
            seen, dd = set(), []
            for d in op.deps:
                if id(d) in seen:
                    continue
                seen.add(id(d))
                dd.append(d)
                assert d.idx < op.idx
                d.needed = True
            op.deps = dd
        cnt = {e: 0 for e in self.eng_sem}
        use = {}
        for op in self.ops:
            if not op.needed:
                continue
            if op.dma:
                if op.eng == "g":
                    sem = self._stack.enter_context(
                        self.nc.semaphore(f"gs_{len(self.gdma_sems)}"))
                    self.gdma_sems.append(sem)
                    op.sem, op.val = sem, 16
                    continue
                sem = self.dma_sems[self.dma_i % len(self.dma_sems)]
                self.dma_i += 1
                use[id(sem)] = use.get(id(sem), 0) + 16
                op.sem, op.val = sem, use[id(sem)]
            else:
                sem = self.eng_sem[op.eng][cnt[op.eng] % self.NPOOL]
                cnt[op.eng] += 1
                use[id(sem)] = use.get(id(sem), 0) + 1
                op.sem, op.val = sem, use[id(sem)]

    def run(self, ekey, eng):
        mark = {}
        for op in self.ops:
            if op.eng != ekey:
                continue
            for d in op.deps:
                k = id(d.sem)
                if mark.get(k, 0) >= d.val:
                    continue
                eng.wait_ge(d.sem, d.val)
                mark[k] = d.val
            last = op.fn(eng)
            if op.needed:
                assert last is not None, f"op idx {op.idx} on {ekey} emitted nothing"
                last.then_inc(op.sem, op.inc)


ALL_PHASES = ("attn_a", "ffn1", "attn_b", "ffn2")


def build_nc(T, phases=ALL_PHASES, n_cores=8, debug=False, no_coll=False, aa_stop=None):
    assert T % 512 == 0
    NT128 = T // 128
    NT512 = T // 512
    nc = bass.Bass("TRN2", target_bir_lowering=False, debug=debug)
    stack = ExitStack()

    x_in = nc.dram_tensor("x_in", [T, D], F32, kind="ExternalInput").ap()
    halo_in = nc.dram_tensor("halo_in", [WINDOW, D], F32, kind="ExternalInput").ap()
    consts_in = nc.dram_tensor("consts", [128, CONST_COLS], F32, kind="ExternalInput").ap()
    y_out = nc.dram_tensor("y_out", [T, D], F32, kind="ExternalOutput").ap()

    Wd = {}
    for nm, shp in [
        ("aWq", (D, D)), ("aWk", (D, D)), ("aWv", (D, D)), ("aWo", (D, D)),
        ("abo", (1, D)),
        ("bWq", (D, D)), ("bWf", (D, D)), ("bWi", (D, D)), ("bWo", (D, D)),
        ("bgw", (1, D)),
        ("fw1", (HID, D)), ("fb1", (1, HID)), ("fw2", (D, HID)), ("fb2", (1, D)),
    ]:
        Wd[nm] = nc.dram_tensor(nm, list(shp), F32, kind="ExternalInput").ap()

    scr = {}
    scr["w1q"] = nc.dram_tensor("scr_w1q", [NH, NB, 128, 128], BF16, kind="Internal")
    scr["w2q"] = nc.dram_tensor("scr_w2q", [NB, NH, 128, 128], BF16, kind="Internal")
    for nm in ("bWq", "bWf", "bWi", "bWo"):
        scr[nm] = nc.dram_tensor("scr_" + nm, [NB, NB, 128, 128], BF16, kind="Internal")
    for nm in ("aWq", "aWk", "aWv", "aWo"):
        scr[nm] = nc.dram_tensor("scr_" + nm, [NB, NB, 128, 128], BF16, kind="Internal")

    scr["ctx"] = nc.dram_tensor("scr_ctx", [T // CHUNK, 64, 784], BF16,
                                kind="Internal")
    scr["og0"] = nc.dram_tensor("scr_og0", [128, 4096], BF16, kind="Internal")
    scr["qp0"] = nc.dram_tensor("scr_qp0", [8, 128, 512], BF16, kind="Internal")

    CTX_ROWS = 128
    cc1_in = nc.dram_tensor("cc1_in", [64, 792], F32, kind="Internal")
    cc1_out = nc.dram_tensor("cc1_out", [64, 792], F32, kind="Internal")
    cc2_in = nc.dram_tensor("cc2_in", [128, HEADS_B * 128], F32, kind="Internal")
    cc2_out = nc.dram_tensor("cc2_out", [128, HEADS_B * 128], F32, kind="Internal")

    sbt = {}

    def sbuf(name, shape, dtype):
        sbt[name] = stack.enter_context(nc.sbuf_tensor(name, list(shape), dtype))
        return sbt[name]

    x_sb = sbuf("x", [128, NB * T], F32)
    const_sb = sbuf("const", [128, CONST_COLS], F32)
    b1q16 = sbuf("b1q16", [128, NH], F32)
    b2q = sbuf("b2q", [128, NB], F32)
    abo_sb = sbuf("abo_pp", [128, NB], F32)
    bgw_sb = sbuf("bgw_pp", [128, NB], F32)
    S_state = sbuf("S_state", [128, HEADS_B * 128], BF16)
    ARENA_COLS = 18304
    arena = sbuf("arena", [128, ARENA_COLS], F32)

    ps = [stack.enter_context(nc.psum_tensor(f"ps{i}", [128, 512], F32))
          for i in range(8)]

    sc = Sched(nc, stack)
    xv = x_sb.ap().rearrange("p (c t) -> p c t", c=NB)

    def cview(name):
        o = CR.off[name]
        return const_sb[:, o:o + CR.w[name]]

    ident = cview("ident")
    identb = cview("identb").bitcast(BF16)
    ones = cview("ones")
    onesb = cview("onesb").bitcast(BF16)
    for i, (k, v) in enumerate(SCALAR_CONSTS.items()):
        capv = const_sb[:, CR.off["cscal"] + i:CR.off["cscal"] + i + 1]
        nc.const_aps.register(F32, float(v), capv)
        if float(np.float32(v)) != float(v):
            nc.const_aps.register(F32, float(np.float32(v)), capv)

    def AFr(off_kb, cols):
        o = off_kb * 256
        assert o + cols <= ARENA_COLS, (off_kb, cols)
        return arena[:, o:o + cols]

    def ABf(off_kb, cols_bf):
        o = off_kb * 256
        assert o + cols_bf // 2 <= ARENA_COLS
        return arena[:, o:o + cols_bf // 2].bitcast(BF16)

    class Arena:

        def __init__(self):
            self.spans = []

        def _reg(self, off, n, buf):
            assert off + n <= ARENA_COLS, (off, n)
            for (a, b, nm) in self.spans:
                if not (off + n <= a or off >= b):
                    assert nm == buf, f"arena overlap {buf} vs {nm} @{off}"
            self.spans.append((off, off + n, buf))

        def f32(self, off, n, buf):
            self._reg(off, n, buf)
            return arena[:, off:off + n]

        def bf16(self, off, nbf, buf):
            assert nbf % 2 == 0
            self._reg(off, nbf // 2, buf)
            return arena[:, off:off + nbf // 2].bitcast(BF16)

    def dma(queue, out, in_, reads=(), writes=(), extra=(), slow=False):
        return sc.op(queue, lambda e, out=out, in_=in_, slow=slow: e.dma_start(
            out=out, in_=in_, allow_slow_non_contiguous=slow),
            reads=reads, writes=writes, dma=True, extra=extra)

    def mm_group(items, reads=(), writes=(), extra=()):
        def fn(t):
            last = None
            for (o, l, r, st, sp) in items:
                last = t.matmul(o, l, r, start=st, stop=sp)
            return last
        return sc.op("t", fn, reads=reads, writes=writes, extra=extra)

    dma("y", const_sb[:], consts_in, writes=["const"])

    ldbuf = [AFr(0, 1024), AFr(4, 1024)]
    for i in range(NT128):
        tb = ldbuf[i % 2]
        tbn = f"ldbuf{i % 2}"
        dma("y", tb, x_in[i * 128:(i + 1) * 128, :], writes=[tbn])
        for h in range(2):
            bank = h
            sc.op("t", (lambda t, tb=tb, h=h, bank=bank: [
                t.transpose(ps[bank][:, c * 128:(c + 1) * 128],
                            tb[:, (4 * h + c) * 128:(4 * h + c + 1) * 128], ident)
                for c in range(4)][-1]),
                reads=[tbn, "const"], writes=[f"psb{bank}"])
            sc.op("v" if h == 0 else "s",
                  (lambda e, i=i, h=h, bank=bank, s=(h == 1):
                   (e.copy if s else e.tensor_copy)(
                       xv[:, 4 * h:4 * h + 4, i * 128:(i + 1) * 128],
                       ps[bank][:, 0:512])),
                  reads=[f"psb{bank}"], writes=[f"x_{i}"])
    sc.fence()

    def prep_transposed(src, scr_t, n_ob, n_ic, quant_scale=None, out_scale=1.0):
        lb = [AFr(0, 1024), AFr(4, 1024)]
        qb = [AFr(8, 1024), AFr(12, 1024)]
        ob_bf = [ABf(16, n_ic * 128), ABf(16 + (n_ic * 128 // 512), n_ic * 128)]
        for ob in range(n_ob):
            tb = lb[ob % 2]
            tbn = f"prep_lb{ob % 2}"
            for half in range(n_ic // 8):
                dma("y", tb, src[ob * 128:(ob + 1) * 128,
                                 half * 1024:(half + 1) * 1024], writes=[tbn])
                for h in range(2):
                    bank = 4 + h
                    sc.op("t", (lambda t, tb=tb, h=h, bank=bank: [
                        t.transpose(ps[bank][:, c * 128:(c + 1) * 128],
                                    tb[:, (4 * h + c) * 128:(4 * h + c + 1) * 128],
                                    ident)
                        for c in range(4)][-1]),
                        reads=[tbn, "const"], writes=[f"psb{bank}"])
                    q = qb[h]
                    qn = f"prep_qb{h}"
                    obf = ob_bf[ob % 2]
                    obn = f"prep_ob{ob % 2}"
                    if quant_scale is not None:
                        qs = float(quant_scale)
                        sc.op("v", (lambda v, bank=bank, q=q: v.tensor_scalar(
                            q[:, 0:512], ps[bank][:, 0:512], qs, -128.0,
                            A.mult, A.max)), reads=[f"psb{bank}"], writes=[qn])
                        sc.op("v", (lambda v, q=q: v.tensor_scalar(
                            q[:, 0:512], q[:, 0:512], 127.0, MAGIC,
                            A.min, A.add)), reads=[qn], writes=[qn])
                        osc = float(out_scale / quant_scale)
                        sc.op("v", (lambda v, q=q, obf=obf, h=h, half=half:
                                    v.tensor_scalar(
                                        obf[:, (half * 8 + 4 * h) * 128:
                                            (half * 8 + 4 * h + 4) * 128],
                                        q[:, 0:512], -MAGIC, osc, A.add, A.mult)),
                              reads=[qn], writes=[obn])
                    else:
                        sc.op("v", (lambda v, bank=bank, obf=obf, h=h, half=half:
                                    v.tensor_copy(
                                        obf[:, (half * 8 + 4 * h) * 128:
                                            (half * 8 + 4 * h + 4) * 128],
                                        ps[bank][:, 0:512])),
                              reads=[f"psb{bank}"], writes=[obn])
            obf = ob_bf[ob % 2]
            obn = f"prep_ob{ob % 2}"
            dma("y", scr_t.ap()[ob].transpose([1, 0, 2]),
                obf[:].rearrange("p (c o) -> p c o", c=n_ic), reads=[obn])
        sc.fence()

    def prep_natural(src, scr_t, n_ic, n_ob):
        cb = ABf(0, n_ob * 128)
        for ic in range(n_ic):
            dma("g", cb[:], src[ic * 128:(ic + 1) * 128, :], writes=["prep_cb"])
            dma("y", scr_t.ap()[:, ic].transpose([1, 0, 2]),
                cb[:].rearrange("p (o c) -> p o c", o=n_ob), reads=["prep_cb"])
        sc.fence()

    def prep_bias(src, dst, n, name, qscale=None, out_scale=1.0):
        dma("y", dst[:, 0:n], src.rearrange("a (o p) -> a p o", p=128),
            writes=[name], slow=True)
        if qscale is not None:
            qs = float(qscale)
            osc = float(out_scale / qscale)
            sc.op("v", lambda v: v.tensor_scalar(
                dst[:, 0:n], dst[:, 0:n], qs, -128.0, A.mult, A.max),
                reads=[name], writes=[name])
            sc.op("v", lambda v: v.tensor_scalar(
                dst[:, 0:n], dst[:, 0:n], 127.0, MAGIC, A.min, A.add),
                reads=[name], writes=[name])
            sc.op("v", lambda v: v.tensor_scalar(
                dst[:, 0:n], dst[:, 0:n], -MAGIC, osc, A.add, A.mult),
                reads=[name], writes=[name])

    if "ffn1" in phases or "ffn2" in phases:
        prep_transposed(Wd["fw1"], scr["w1q"], NH, NB, quant_scale=64.0)
        prep_transposed(Wd["fw2"], scr["w2q"], NB, NH, quant_scale=64.0,
                        out_scale=1.0 / 16.0)
        prep_bias(Wd["fb1"], b1q16, NH, "b1q16", qscale=64.0, out_scale=16.0)
        prep_bias(Wd["fb2"], b2q, NB, "b2q", qscale=64.0, out_scale=1.0)
        sc.fence()
    if "attn_b" in phases:
        for nm in ("bWq", "bWf", "bWi", "bWo"):
            prep_transposed(Wd[nm], scr[nm], NB, NB)
        prep_bias(Wd["bgw"], bgw_sb, NB, "bgw")
        sc.fence()
    if "attn_a" in phases:
        for nm in ("aWq", "aWk", "aWv", "aWo"):
            prep_natural(Wd[nm], scr[nm], NB, NB)
        prep_bias(Wd["abo"], abo_sb, NB, "abo")
        sc.fence()

    ST_OFF = 16384
    stats = arena[:, ST_OFF:ST_OFF + 1024]
    rowb = [arena[32:33, ST_OFF:ST_OFF + 1024],
            arena[64:65, ST_OFF:ST_OFF + 1024]]

    def xnames(t0, t1):
        return [f"x_{i}" for i in range(t0 // 128, t1 // 128)]

    def emit_ln_stats(out_scale):
        sqb = [AFr(48, 512), AFr(50, 512)]
        tmp = [AFr(40, 512), AFr(42, 512), AFr(44, 512)]
        n = NT512 * NB
        k = 0
        for j in range(NT512):
            sel = const_sb[:, CR.off["selcols"] + 8 * j:CR.off["selcols"] + 8 * j + 8]
            for c in range(NB):
                mm_sum = (lambda t, j=j, c=c, sel=sel, k=k: t.matmul(
                    ps[0][0:8, 0:512], sel,
                    xv[:, c, j * 512:(j + 1) * 512],
                    start=(k == 0), stop=(k == n - 1)))
                sc.op("t", mm_sum, reads=xnames(j * 512, (j + 1) * 512) + ["const"],
                      writes=["psb0"])
                sq = sqb[k % 2]
                sqn = f"sqb{k % 2}"
                sc.op("s", (lambda e, j=j, c=c, sq=sq: e.activation(
                    sq[:], xv[:, c, j * 512:(j + 1) * 512], AF.Square)),
                    reads=xnames(j * 512, (j + 1) * 512), writes=[sqn])
                sc.op("t", (lambda t, sq=sq, sel=sel, k=k: t.matmul(
                    ps[1][0:8, 0:512], sel, sq[:],
                    start=(k == 0), stop=(k == n - 1))),
                    reads=[sqn, "const"], writes=["psb1"])
                k += 1
        m, m2, v_ = tmp
        sc.op("v", lambda v: v.tensor_scalar(
            m[0:8, :], ps[0][0:8, 0:512], INVD, None, A.mult),
            reads=["psb0"], writes=["qtarea"])
        sc.op("v", lambda v: v.tensor_tensor(
            m2[0:8, :], m[0:8, :], m[0:8, :], A.mult),
            writes=["qtarea"])
        sc.op("v", lambda v: v.scalar_tensor_tensor(
            v_[0:8, :], ps[1][0:8, 0:512], INVD, m2[0:8, :],
            A.mult, A.subtract),
            reads=["psb1"], writes=["qtarea"])
        sc.op("s", lambda e: e.activation(
            m2[0:8, :], v_[0:8, :], AF.Ln, bias=EPS_LN),
            writes=["qtarea"])
        sc.op("s", lambda e: e.activation(
            v_[0:8, :], m2[0:8, :], AF.Exp, scale=-0.5),
            writes=["qtarea"])
        osf = float(out_scale)
        sc.op("v", lambda v: v.tensor_scalar(
            stats[0:8, 0:512], v_[0:8, :], osf, None, A.mult),
            reads=["qtarea"], writes=["stats"])
        sc.op("v", lambda v: v.scalar_tensor_tensor(
            stats[0:8, 512:1024], m[0:8, :], -osf, v_[0:8, :],
            A.mult, A.mult),
            reads=["qtarea"], writes=["stats"])
        sc.fence()

    RGROUPS = [[2 * b, 2 * b + 1] for b in range(n_cores // 2)]

    def emit_attn_a():
        NCH = T // CHUNK
        ar = Arena()
        ctxacc = ar.f32(17408, 768, "ctxacc")
        ksacc = ar.f32(18176, 12, "ksacc")
        emit_ln_stats(1.0)

        xn = ar.bf16(0, 4096, "xn")
        xnv = xn.rearrange("p (c t) -> p c t", c=NB)
        aqt = ar.f32(2048, 2048, "aqt")
        pmask = cview("chind2")

        def ln_apply_tile(tk):
            rb = rowb[tk % 2]
            rbn = f"rb{tk % 2}"
            dma("y", rb, stats[tk:tk + 1, 0:1024], reads=["stats"], writes=[rbn])
            obase = 32 * (1 + tk % 2)
            orow = ones[obase:obase + 1, 0:128]
            sc.op("t", (lambda t, rb=rb, orow=orow: t.matmul(
                ps[0][0:128, 0:512], orow, rb[0:1, 0:512],
                start=True, stop=True)), reads=[rbn, "const"], writes=["psb0"])
            sc.op("t", (lambda t, rb=rb, orow=orow: t.matmul(
                ps[1][0:128, 0:512], orow, rb[0:1, 512:1024],
                start=True, stop=True)), reads=[rbn, "const"], writes=["psb1"])
            t0 = tk * 512
            for th in range(2):
                xsrc = xv[:, :, t0 + th * 256:t0 + th * 256 + 256]
                sc.op("v", (lambda v, xsrc=xsrc, th=th: v.scalar_tensor_tensor(
                    aqt[:, 0:2048], xsrc, 1.0,
                    ps[0][:, th * 256:(th + 1) * 256].unsqueeze(1)
                    .broadcast_to([128, 8, 256]),
                    A.mult, A.mult)),
                    reads=xnames(t0, t0 + 512) + ["psb0"], writes=["aqt"])
                dst = xnv[:, :, th * 256:(th + 1) * 256]
                sc.op("v", (lambda v, th=th, dst=dst: v.scalar_tensor_tensor(
                    dst, aqt[:, 0:2048], 1.0,
                    ps[1][:, th * 256:(th + 1) * 256].unsqueeze(1)
                    .broadcast_to([128, 8, 256]),
                    A.mult, A.add)),
                    reads=["aqt", "psb1"], writes=["xn"])

        wkvg = ar.bf16(4096, 12288, "wkvg")
        wkvv = wkvg.rearrange("p (w c o) -> p w c o", w=2, c=NB)
        ccst = arena[:, 4096:4888]
        kT = ar.bf16(10240, 3072, "kT")
        kTv = kT.rearrange("p (s o) -> p s o", s=4)
        vT = ar.bf16(11776, 3072, "vT")
        vTv = vT.rearrange("p (s o) -> p s o", s=4)
        elu1 = ar.f32(13312, 768, "elu1")
        elu2 = ar.f32(14080, 768, "elu2")
        kmask = arena[:, 13312:13696].bitcast(BF16)
        ctxo = [ar.bf16(14848, 784, "ctxo0"), ar.bf16(15240, 784, "ctxo1")]

        sc.op("v", lambda v: v.memset(ctxacc[0:64, 0:768], 0.0),
              writes=["ctxacc"])
        sc.op("v", lambda v: v.memset(ksacc[0:64, 0:12], 0.0), writes=["ksacc"])
        sc.op("v", lambda v: v.memset(ctxo[0][0:64, 780:784], 0.0),
              writes=["ctxo0"])
        sc.op("v", lambda v: v.memset(ctxo[1][0:64, 780:784], 0.0),
              writes=["ctxo1"])
        if aa_stop == "none":
            sc.fence()
            return
        for tk in range(NT512 if aa_stop != "prep" else 0):
            ln_apply_tile(tk)
            if aa_stop == "ln":
                continue
            for w, nm in ((0, "aWk"), (1, "aWv")):
                for ob in range(6):
                    dma("y", wkvv[:, w, :, ob * 128:(ob + 1) * 128],
                        scr[nm].ap()[2 + ob].transpose([1, 0, 2]),
                        writes=["wkvg"])
            for t128 in range(4):
                for w, (bankA, bankB) in ((0, (2, 3)), (1, (4, 5))):
                    def mmkv(t, w=w, t128=t128, bankA=bankA, bankB=bankB):
                        last = None
                        for sp, bank, o0, o1 in ((0, bankA, 0, 512),
                                                 (1, bankB, 512, 768)):
                            for c in range(NB):
                                lhsT = xnv[:, c, t128 * 128:(t128 + 1) * 128]
                                last = t.matmul(
                                    ps[bank][:, 0:o1 - o0], lhsT,
                                    wkvv[:, w, c, o0:o1],
                                    start=(c == 0), stop=(c == NB - 1))
                        return last
                    sc.op("t", mmkv, reads=["xn", "wkvg"],
                          writes=[f"psb{bankA}", f"psb{bankB}"])
                for sp, bank, o0, o1 in ((0, 2, 0, 512), (1, 3, 512, 768)):
                    n = o1 - o0
                    sc.op("v", (lambda v, bank=bank, o0=o0, n=n: v.tensor_scalar(
                        elu1[:, o0:o0 + n], ps[bank][:, 0:n], 0.0, None, A.min)),
                        reads=[f"psb{bank}"], writes=["elu1"])
                    sc.op("s", (lambda e, o0=o0, n=n: e.activation(
                        elu2[:, o0:o0 + n], elu1[:, o0:o0 + n], AF.Exp)),
                        reads=["elu1"], writes=["elu2"])
                    sc.op("v", (lambda v, bank=bank, o0=o0, n=n, t128=t128:
                                v.scalar_tensor_tensor(
                                    kTv[:, t128, o0:o0 + n], ps[bank][:, 0:n],
                                    0.0, elu2[:, o0:o0 + n], A.max, A.add)),
                          reads=[f"psb{bank}", "elu2"], writes=["kT"])
                    eng = "v" if sp == 0 else "s"
                    sc.op(eng, (lambda e, bank=bank, o0=o0, n=n, t128=t128,
                                sv=(sp == 1): (e.copy if sv else e.tensor_copy)(
                                    vTv[:, t128, o0:o0 + n],
                                    ps[bank + 2][:, 0:n])),
                          reads=[f"psb{bank + 2}"], writes=["vT"])
                for cpar in range(2 if aa_stop != "kv" else 0):
                    cu = tk * 8 + t128 * 2 + cpar
                    sc.op("v", (lambda v, t128=t128, cpar=cpar: v.tensor_scalar(
                        kmask[:, 0:768], kTv[:, t128, :],
                        pmask[:, cpar:cpar + 1], None, A.mult)),
                        reads=["kT", "const"], writes=["elu1"])
                    def mmctx(t, t128=t128):
                        last = None
                        for gh in range(GLOBAL_HEADS):
                            bank, col = (6, gh * 64) if gh < 8 else \
                                        (7, (gh - 8) * 64)
                            last = t.matmul(
                                ps[bank][0:64, col:col + 64],
                                kmask[:, gh * 64:gh * 64 + 64],
                                vTv[:, t128, gh * 64:gh * 64 + 64],
                                start=True, stop=True)
                        for gh in range(GLOBAL_HEADS):
                            last = t.matmul(
                                ps[7][0:64, 384 + gh:385 + gh],
                                kmask[:, gh * 64:gh * 64 + 64],
                                onesb[:, 0:1],
                                start=True, stop=True)
                        return last
                    sc.op("t", mmctx, reads=["elu1", "vT", "const"],
                          writes=["psb6", "psb7"])
                    if aa_stop == "ctx1":
                        continue
                    cb = ctxo[cu % 2]
                    cbn = f"ctxo{cu % 2}"
                    sc.op("v", (lambda v, cb=cb: v.tensor_copy(
                        cb[0:64, 0:512], ps[6][0:64, 0:512])),
                        reads=["psb6"], writes=[cbn])
                    sc.op("s", (lambda e, cb=cb: e.copy(
                        cb[0:64, 512:768], ps[7][0:64, 0:256])),
                        reads=["psb7"], writes=[cbn])
                    sc.op("s", (lambda e, cb=cb: e.copy(
                        cb[0:64, 768:780], ps[7][0:64, 384:396])),
                        reads=["psb7"], writes=[cbn])
                    sc.op("v", lambda v: v.tensor_tensor(
                        ctxacc[0:64, 0:512], ctxacc[0:64, 0:512],
                        ps[6][0:64, 0:512], A.add),
                        reads=["psb6"], writes=["ctxacc"])
                    sc.op("v", lambda v: v.tensor_tensor(
                        ctxacc[0:64, 512:768], ctxacc[0:64, 512:768],
                        ps[7][0:64, 0:256], A.add),
                        reads=["psb7"], writes=["ctxacc"])
                    sc.op("v", lambda v: v.tensor_tensor(
                        ksacc[0:64, 0:12], ksacc[0:64, 0:12],
                        ps[7][0:64, 384:396], A.add),
                        reads=["psb7"], writes=["ksacc"])
                    if aa_stop != "ctx2":
                        dma("y", scr["ctx"].ap()[cu], cb[0:64, 0:784],
                            reads=[cbn], writes=["scrctx"])

        if aa_stop in ("prep", "1a", "ln", "kv", "ctx1", "ctx2"):
            sc.fence()
            return
        flagC = cview("flags")[0:64, 0:1]
        flagU = cview("flags")[0:64, 1:2]
        sc.op("v", lambda v: v.memset(ccst[0:64, 780:792], 0.0),
              writes=["wkvg"])
        sc.op("v", lambda v: v.tensor_scalar(
            ccst[0:64, 0:768], ctxacc[0:64, 0:768], flagC, None, A.mult),
            reads=["ctxacc", "const"], writes=["wkvg"])
        sc.op("v", lambda v: v.tensor_scalar(
            ccst[0:64, 768:780], ksacc[0:64, 0:12], flagC, None, A.mult),
            reads=["ksacc", "const"], writes=["wkvg"])
        dma("y", cc1_in.ap(), ccst[0:64, 0:792], reads=["wkvg"], writes=["cc1i"])
        if no_coll:
            dma("y", cc1_out.ap(), cc1_in.ap(), reads=["cc1i"], writes=["cc1o"])
        else:
            sc.op("g", lambda g: g.collective_compute(
                "AllReduce", A.add, replica_groups=RGROUPS,
                ins=[cc1_in.ap().opt()], outs=[cc1_out.ap().opt()]),
                reads=["cc1i"], writes=["cc1o"])
        dma("y", ccst[0:64, 0:792], cc1_out.ap(), reads=["cc1o"], writes=["wkvg"])
        sc.op("v", lambda v: v.tensor_scalar(
            ctxacc[0:64, 0:768], ccst[0:64, 0:768], flagU, None, A.mult),
            reads=["wkvg", "const"], writes=["ctxacc"])
        sc.op("v", lambda v: v.tensor_scalar(
            ksacc[0:64, 0:12], ccst[0:64, 768:780], flagU, None, A.mult),
            reads=["wkvg", "const"], writes=["ksacc"])
        sc.fence()
        if aa_stop == "coll":
            return

        ar2 = Arena()
        ar2.f32(17408, 768, "ctxacc")
        ar2.f32(18176, 12, "ksacc")
        xn2 = ar2.bf16(0, 4096, "xn")
        xnv2 = xn2.rearrange("p (c t) -> p c t", c=NB)
        attnb = ar2.bf16(0, 4096, "xn")
        atv = attnb.rearrange("p (c t) -> p c t", c=NB)
        ar2.f32(2048, 2048, "aqt")
        qbf = ar2.bf16(4096, 4096, "qbf")
        qbv = qbf.rearrange("p (c t) -> p c t", c=NB)
        qe = ar2.f32(6144, 1536, "qearea")
        qev = qe.rearrange("p (c t) -> p c t", c=6)
        qsmR = arena[:, 6144:9216].bitcast(BF16)
        qsRv = qsmR.rearrange("p (h t) -> p h t", h=12)
        wob = [ar2.bf16(6144, 1024, "qearea"), ar2.bf16(6656, 1024, "qearea")]
        wovs = [w.rearrange("p (c o) -> p c o", c=NB) for w in wob]
        qsm = ar2.bf16(7680, 3072, "qsm")
        qsv = qsm.rearrange("p (c t) -> p c t", c=6)
        gstrip = arena[:, 2432:3968].bitcast(BF16)
        gsv = gstrip.rearrange("p (h t) -> p h t", h=12)
        qrec = ar2.bf16(9216, 1024, "qrec")
        kloc = [ar2.bf16(9728, 1024, "kloc0"), ar2.bf16(10240, 1024, "kloc1")]
        klv = [k.rearrange("p (c t) -> p c t", c=2) for k in kloc]
        vloc = [ar2.bf16(10752, 1024, "vloc0"), ar2.bf16(11264, 1024, "vloc1")]
        vlv = [v.rearrange("p (c t) -> p c t", c=4) for v in vloc]
        hkv = ar2.bf16(11776, 512, "halokv")
        hkvv = hkv.rearrange("p (c t) -> p c t", c=4)
        wklb = ar2.bf16(12032, 2048, "wklarea")
        wklv = wklb.rearrange("p (o c t) -> p o c t", o=2, c=NB)
        scp = ar2.bf16(12032, 512, "wklarea")
        scpv = scp.rearrange("p (h t) -> p h t", h=4)
        scs = ar2.bf16(12288, 512, "wklarea")
        scsv = scs.rearrange("p (h t) -> p h t", h=4)
        recrow = ar2.f32(12544, 512, "wklarea")
        wvlb = ar2.bf16(13056, 2048, "wvlarea")
        wvlv = wvlb.rearrange("p (c t) -> p c t", c=NB)
        ctxrb = [ar2.bf16(13056, 784, "wvlarea"), ar2.bf16(13448, 784, "wvlarea")]
        wqb = [ar2.bf16(14080, 1024, "wqb0"), ar2.bf16(14592, 1024, "wqb1")]
        wqvs = [w.rearrange("p (c o) -> p c o", c=NB) for w in wqb]
        hfeat = ar2.f32(15104, 1024, "halof")
        hfv = hfeat.rearrange("p (c t) -> p c t", c=NB)
        rtile = ar2.f32(15104, 512, "halof")
        qm = [ar2.bf16(15616, 256, "halof"), ar2.bf16(15744, 256, "halof")]
        qmv = [q.rearrange("p (c t) -> p c t", c=2) for q in qm]
        ltile = ar2.bf16(15872, 512, "halof")
        hst = ar2.f32(16128, 256, "halost")
        hxn = ar2.bf16(3072, 1024, "aqt")
        hxv = hxn.rearrange("p (c t) -> p c t", c=NB)
        ctxs_bf = arena[:, 2048:2432].bitcast(BF16)

        for tk in range(NT512):
            ln_apply_tile(tk)
            for ob in range(2):
                dma("y", wklv[:, ob], scr["aWk"].ap()[ob].transpose([1, 0, 2]),
                    writes=["wklarea"])
                dma("y", wvlv[:, :, ob * 128:(ob + 1) * 128],
                    scr["aWv"].ap()[ob].transpose([1, 0, 2]),
                    writes=["wvlarea"])
            for o in range(NB):
                wvv = wqvs[o % 2]
                wn = f"wqb{o % 2}"
                dma("y", wvv, scr["aWq"].ap()[o].transpose([1, 0, 2]),
                    writes=[wn])
                bank = 2 + o % 2
                sc.op("t", (lambda t, wvv=wvv, bank=bank: [
                    t.matmul(ps[bank][:, 0:512], wvv[:, c, :], xnv2[:, c, :],
                             start=(c == 0), stop=(c == NB - 1))
                    for c in range(NB)][-1]),
                    reads=[wn, "xn"], writes=[f"psb{bank}"])
                eng = "v" if o % 2 == 0 else "s"
                sc.op(eng, (lambda e, bank=bank, o=o, sv=(o % 2 == 1):
                            (e.copy if sv else e.tensor_copy)(
                                qbv[:, o, :], ps[bank][:, 0:512])),
                      reads=[f"psb{bank}"], writes=["qbf"])
            for ob in range(2):
                bank = 2 + ob
                sc.op("t", (lambda t, ob=ob, bank=bank: [
                    t.matmul(ps[bank][:, 0:512], wklv[:, ob, c, :],
                             xnv2[:, c, :], start=(c == 0), stop=(c == NB - 1))
                    for c in range(NB)][-1]),
                    reads=["wklarea", "xn"], writes=[f"psb{bank}"])
                sc.op("v", (lambda v, bank=bank, ob=ob, tk=tk: v.tensor_copy(
                    klv[tk % 2][:, ob, :], ps[bank][:, 0:512])),
                    reads=[f"psb{bank}"], writes=[f"kloc{tk % 2}"])
            for t128 in range(4):
                bank = 2 + t128 % 2
                sc.op("t", (lambda t, t128=t128, bank=bank: [
                    t.matmul(ps[bank][:, 0:256],
                             xnv2[:, c, t128 * 128:(t128 + 1) * 128],
                             wvlv[:, c, 0:256],
                             start=(c == 0), stop=(c == NB - 1))
                    for c in range(NB)][-1]),
                    reads=["wvlarea", "xn"], writes=[f"psb{bank}"])
                sc.op("v", (lambda v, bank=bank, t128=t128, tk=tk: v.tensor_copy(
                    vlv[tk % 2][:, t128, :], ps[bank][:, 0:256])),
                    reads=[f"psb{bank}"], writes=[f"vloc{tk % 2}"])
            if tk == 0:
                dma("y", hfeat[:, 0:1024].rearrange("p (a b) -> p a b", a=2),
                    halo_in.rearrange("(a t) d -> a t d", a=2), slow=True,
                    writes=["halof"])
                sc.op("t", (lambda t: [
                    t.transpose(ps[4][:, c * 128:(c + 1) * 128],
                                hfeat[:, c * 128:(c + 1) * 128], ident)
                    for c in range(4)][-1]),
                    reads=["halof", "const"], writes=["psb4"])
                sc.op("t", (lambda t: [
                    t.transpose(ps[5][:, (c - 4) * 128:(c - 3) * 128],
                                hfeat[:, c * 128:(c + 1) * 128], ident)
                    for c in range(4, 8)][-1]),
                    reads=["halof", "const"], writes=["psb5"])
                sc.op("v", lambda v: v.tensor_copy(hfeat[:, 0:512],
                                                   ps[4][:, 0:512]),
                      reads=["psb4"], writes=["halof"])
                sc.op("v", lambda v: v.tensor_copy(hfeat[:, 512:1024],
                                                   ps[5][:, 0:512]),
                      reads=["psb5"], writes=["halof"])
                sc.op("t", (lambda t: [
                    t.matmul(ps[4][0:1, 0:128], ones[:, 0:1],
                             hfv[:, c, :], start=(c == 0), stop=(c == NB - 1))
                    for c in range(NB)][-1]),
                    reads=["halof", "const"], writes=["psb4"])
                sc.op("s", (lambda e: [
                    e.activation(aqt[:, c * 128:(c + 1) * 128], hfv[:, c, :],
                                 AF.Square) for c in range(NB)][-1]),
                    reads=["halof"], writes=["aqt"])
                sc.op("t", (lambda t: [
                    t.matmul(ps[5][0:1, 0:128], ones[:, 0:1],
                             aqt[:, c * 128:(c + 1) * 128],
                             start=(c == 0), stop=(c == NB - 1))
                    for c in range(NB)][-1]),
                    reads=["aqt", "const"], writes=["psb5"])
                sc.op("v", lambda v: v.tensor_scalar(
                    hst[0:1, 0:128], ps[4][0:1, 0:128], INVD, None, A.mult),
                    reads=["psb4"], writes=["halost"])
                sc.op("v", lambda v: v.tensor_tensor(
                    hst[0:1, 128:256], hst[0:1, 0:128], hst[0:1, 0:128],
                    A.mult), reads=["halost"], writes=["halost"])
                sc.op("v", lambda v: v.scalar_tensor_tensor(
                    hst[0:1, 128:256], ps[5][0:1, 0:128], INVD,
                    hst[0:1, 128:256], A.mult, A.subtract),
                    reads=["psb5", "halost"], writes=["halost"])
                sc.op("s", lambda e: e.activation(
                    hst[0:1, 128:256], hst[0:1, 128:256], AF.Ln, bias=EPS_LN),
                    reads=["halost"], writes=["halost"])
                sc.op("s", lambda e: e.activation(
                    hst[0:1, 128:256], hst[0:1, 128:256], AF.Exp, scale=-0.5),
                    reads=["halost"], writes=["halost"])
                sc.op("v", lambda v: v.scalar_tensor_tensor(
                    hst[0:1, 0:128], hst[0:1, 0:128], -1.0,
                    hst[0:1, 128:256], A.mult, A.mult),
                    reads=["halost"], writes=["halost"])
                sc.op("t", lambda t: t.matmul(
                    ps[4][0:128, 0:128], ones[0:1, 0:128], hst[0:1, 128:256],
                    start=True, stop=True),
                    reads=["halost", "const"], writes=["psb4"])
                sc.op("t", lambda t: t.matmul(
                    ps[5][0:128, 0:128], ones[0:1, 0:128], hst[0:1, 0:128],
                    start=True, stop=True),
                    reads=["halost", "const"], writes=["psb5"])
                sc.op("v", lambda v: v.scalar_tensor_tensor(
                    aqt[:, 0:1024], hfeat[:, 0:1024], 1.0,
                    ps[4][:, 0:128].unsqueeze(1).broadcast_to([128, 8, 128]),
                    A.mult, A.mult), reads=["halof", "psb4"], writes=["aqt"])
                sc.op("v", lambda v: v.scalar_tensor_tensor(
                    hxn[:, 0:1024], aqt[:, 0:1024], 1.0,
                    ps[5][:, 0:128].unsqueeze(1).broadcast_to([128, 8, 128]),
                    A.mult, A.add), reads=["aqt", "psb5"], writes=["aqt"])
                for ob in range(2):
                    bank = 4 + ob
                    sc.op("t", (lambda t, ob=ob, bank=bank: [
                        t.matmul(ps[bank][:, 0:128], wklv[:, ob, c, :],
                                 hxv[:, c, :], start=(c == 0),
                                 stop=(c == NB - 1)) for c in range(NB)][-1]),
                        reads=["wklarea", "aqt"], writes=[f"psb{bank}"])
                    sc.op("v", (lambda v, bank=bank, ob=ob: v.tensor_copy(
                        hkvv[:, ob, :], ps[bank][:, 0:128])),
                        reads=[f"psb{bank}"], writes=["halokv"])
                sc.op("t", (lambda t: [
                    t.matmul(ps[4][:, 0:256], hxv[:, c, :], wvlv[:, c, 0:256],
                             start=(c == 0), stop=(c == NB - 1))
                    for c in range(NB)][-1]),
                    reads=["wvlarea", "aqt"], writes=["psb4"])
                sc.op("v", (lambda v: v.tensor_copy(
                    hkv[:, 256:512], ps[4][:, 0:256])),
                    reads=["psb4"], writes=["halokv"])
            for th in range(2):
                c0 = th * 256
                sc.op("s", (lambda e, c0=c0: e.activation(
                    qe[:, 0:1536], qbv[:, 2:8, c0:c0 + 256], AF.Exp)),
                    reads=["qbf"], writes=["qearea"])
                sc.op("t", (lambda t, c0=c0, th=th: [
                    t.matmul(ps[4][0:12, th * 256:th * 256 + 256],
                             const_sb[:, CR.off["hsum12"] + 12 * cb:
                                      CR.off["hsum12"] + 12 * cb + 12],
                             qev[:, cb, :], start=(cb == 0), stop=(cb == 5))
                    for cb in range(6)][-1]),
                    reads=["qearea", "const"], writes=["psb4"])
                def _recip(v, th=th):
                    with nc.allow_low_precision("bf16 softmax recip"):
                        return v.reciprocal(
                            qrec[0:12, th * 256:th * 256 + 256],
                            ps[4][0:12, th * 256:th * 256 + 256])
                sc.op("v", _recip, reads=["psb4"], writes=["qrec"])
                for cb in range(6):
                    sc.op("t", (lambda t, cb=cb, th=th: t.matmul(
                        ps[5][:, 0:256],
                        const_sb[0:12, CR.off["hexp12"]:CR.off["hexp12"] + 384]
                        .bitcast(BF16)[:, 128 * cb:128 * cb + 128],
                        qrec[0:12, th * 256:th * 256 + 256],
                        start=True, stop=True)),
                        reads=["qrec", "const"], writes=["psb5"])
                    sc.op("v", (lambda v, cb=cb, th=th, c0=c0: v.tensor_tensor(
                        qsv[:, cb, c0:c0 + 256], qev[:, cb, :],
                        ps[5][:, 0:256], A.mult)),
                        reads=["qearea", "psb5"], writes=["qsm"])
            qsR4 = qsmR.rearrange("p (h q t) -> p h q t", h=6, q=2)
            dma("y", qsR4[0:64, :, 0, :], qsm[0:64, 0:3072]
                .rearrange("p (c t) -> p c t", c=6), reads=["qsm"],
                writes=["qearea", "qsm"])
            dma("y", qsR4[0:64, :, 1, :], qsm[64:128, 0:3072]
                .rearrange("p (c t) -> p c t", c=6), reads=["qsm"],
                writes=["qearea", "qsm"])
            for b in range(4):
                first = (tk == 0 and b == 0)
                if b == 0:
                    if tk == 0:
                        kprev = hkvv[:, 0:2, :]
                        vprev = hkv[:, 256:512]
                        pn = ["halokv"]
                    else:
                        kprev = klv[(tk - 1) % 2][:, :, 384:512]
                        vprev = vloc[(tk - 1) % 2][:, 3 * 256:4 * 256]
                        pn = [f"kloc{(tk - 1) % 2}", f"vloc{(tk - 1) % 2}"]
                else:
                    kprev = klv[tk % 2][:, :, (b - 1) * 128:b * 128]
                    vprev = vloc[tk % 2][:, (b - 1) * 256:b * 256]
                    pn = [f"kloc{tk % 2}", f"vloc{tk % 2}"]
                kself = klv[tk % 2][:, :, b * 128:(b + 1) * 128]
                vself = vloc[tk % 2][:, b * 256:(b + 1) * 256]
                sn = [f"kloc{tk % 2}", f"vloc{tk % 2}"]
                for par in range(2):
                    sc.op("v", (lambda v, par=par, b=b: v.tensor_scalar(
                        qmv[par][:, :, :], qbv[:, 0:2, b * 128:(b + 1) * 128],
                        pmask[:, par:par + 1], None, A.mult)),
                        reads=["qbf", "const"], writes=["halof"])
                def mmdots(t, kprev=kprev, kself=kself, b=b):
                    last = None
                    for h in range(4):
                        par, ob = h % 2, h // 2
                        if kprev.ndim == 3:
                            kp = kprev[:, ob, :]
                        else:
                            kp = kprev.rearrange("p (o t) -> p o t", o=2)[:, ob, :]
                        last = t.matmul(
                            ps[2][:, h * 128:(h + 1) * 128], kp,
                            qmv[par][:, ob, :], start=True, stop=True)
                        last = t.matmul(
                            ps[3][:, h * 128:(h + 1) * 128],
                            kself[:, ob, :],
                            qmv[par][:, ob, :], start=True, stop=True)
                    return last
                sc.op("t", mmdots, reads=pn + sn + ["halof"],
                      writes=["psb2", "psb3"])
                if first:
                    mprev = cview("mask_prev0").unsqueeze(1).broadcast_to(
                        [128, 4, 128])
                    sc.op("v", (lambda v, mprev=mprev: v.scalar_tensor_tensor(
                        ps[2][:, 0:512].rearrange("p (h t) -> p h t", h=4),
                        ps[2][:, 0:512].rearrange("p (h t) -> p h t", h=4),
                        0.125, mprev, A.mult, A.add)),
                        reads=["psb2", "const"], writes=["psb2"])
                else:
                    sc.op("v", lambda v: v.tensor_scalar(
                        ps[2][:, 0:512], ps[2][:, 0:512], 0.125, None, A.mult),
                        reads=["psb2"], writes=["psb2"])
                msf = cview("mask_self").unsqueeze(1).broadcast_to([128, 4, 128])
                sc.op("v", (lambda v, msf=msf: v.scalar_tensor_tensor(
                    ps[3][:, 0:512].rearrange("p (h t) -> p h t", h=4),
                    ps[3][:, 0:512].rearrange("p (h t) -> p h t", h=4),
                    0.125, msf, A.mult, A.add)),
                    reads=["psb3", "const"], writes=["psb3"])
                sc.op("s", lambda e: e.activation(
                    scp[:, 0:512], ps[2][:, 0:512], AF.Exp),
                    reads=["psb2"], writes=["wklarea"])
                sc.op("s", lambda e: e.activation(
                    scs[:, 0:512], ps[3][:, 0:512], AF.Exp),
                    reads=["psb3"], writes=["wklarea"])
                sc.op("t", lambda t: [
                    t.matmul(ps[5][0:1, 0:512], onesb[:, 0:1], scp[:, 0:512],
                             start=True, stop=False),
                    t.matmul(ps[5][0:1, 0:512], onesb[:, 0:1], scs[:, 0:512],
                             start=False, stop=True)][-1],
                    reads=["wklarea", "const"], writes=["psb5"])
                sc.op("v", lambda v: v.reciprocal(
                    recrow[0:1, 0:512], ps[5][0:1, 0:512]),
                    reads=["psb5"], writes=["wklarea"])
                def mmav(t, vprev=vprev, vself=vself):
                    last = None
                    for h in range(4):
                        out = ps[4][0:64, h * 128:(h + 1) * 128]
                        last = t.matmul(out, vprev[:, h * 64:(h + 1) * 64],
                                        scpv[:, h, :], start=True, stop=False)
                        last = t.matmul(out, vself[:, h * 64:(h + 1) * 64],
                                        scsv[:, h, :], start=False, stop=True)
                    return last
                sc.op("t", mmav, reads=pn + sn + ["wklarea"], writes=["psb4"])
                def mmrec(t):
                    last = None
                    for h in range(4):
                        last = t.matmul(
                            ps[5][0:64, h * 128:(h + 1) * 128],
                            ones[0:1, 0:64],
                            recrow[0:1, h * 128:(h + 1) * 128],
                            start=True, stop=True)
                    return last
                sc.op("t", mmrec, reads=["wklarea", "const"], writes=["psb5"])
                sc.op("s", lambda e: e.copy(rtile[0:64, 0:512],
                                            ps[5][0:64, 0:512]),
                      reads=["psb5"], writes=["halof"])
                sc.op("v", (lambda v: v.tensor_tensor(
                    ltile[0:64, 0:512], ps[4][0:64, 0:512],
                    rtile[0:64, 0:512], A.mult)),
                    reads=["psb4", "halof"], writes=["halof"])
                lt4 = ltile[0:64, 0:512].rearrange("p (h q t) -> p h q t",
                                                    h=2, q=2)
                dma("y", atv[0:64, 0:2, b * 128:(b + 1) * 128],
                    lt4[:, :, 0, :], reads=["halof"], writes=["xn"])
                dma("y", atv[64:128, 0:2, b * 128:(b + 1) * 128],
                    lt4[:, :, 1, :], reads=["halof"], writes=["xn"])
            for cl in range(8):
                cu = tk * 8 + cl
                crb = ctxrb[cu % 2]
                crn = "wvlarea"
                dma("y", crb[0:64, 0:784], scr["ctx"].ap()[cu],
                    reads=["scrctx"], writes=[crn])
                sc.op("v", lambda v: v.tensor_scalar(
                    arena[0:64, 18188:18200], ksacc[0:64, 0:12], EPS_CTX,
                    None, A.add), reads=["ksacc"], writes=["kstmp"])
                sc.op("v", lambda v: v.reciprocal(
                    arena[0:64, 18188:18200], arena[0:64, 18188:18200]),
                    reads=["kstmp"], writes=["kstmp"])
                sc.op("v", lambda v: v.tensor_tensor(
                    ctxs_bf[0:64, 0:768].rearrange("p (h e) -> p h e", h=12),
                    ctxacc[0:64, 0:768].rearrange("p (h e) -> p h e", h=12),
                    arena[0:64, 18188:18200].unsqueeze(2)
                    .broadcast_to([64, 12, 64]), A.mult),
                    reads=["ctxacc", "kstmp"], writes=["aqt"])
                def mmglob(t, cl=cl):
                    last = None
                    for gh in range(GLOBAL_HEADS):
                        bank, col = (4, gh * 64) if gh < 8 else \
                                    (5, (gh - 8) * 64)
                        last = t.matmul(
                            ps[bank][0:64, col:col + 64],
                            ctxs_bf[0:64, gh * 64:gh * 64 + 64],
                            qsRv[0:64, gh, cl * 64:(cl + 1) * 64],
                            start=True, stop=True)
                    return last
                sc.op("t", mmglob, reads=["aqt", "qearea", "qsm"],
                      writes=["psb4", "psb5"])
                ch = cl % 4
                sc.op("v", (lambda v, ch=ch: v.tensor_copy(
                    gsv[0:64, 0:8, ch * 64:(ch + 1) * 64],
                    ps[4][0:64, 0:512].rearrange("p (h e) -> p h e", h=8))),
                    reads=["psb4"], writes=["aqt"])
                sc.op("s", (lambda e, ch=ch: e.copy(
                    gsv[0:64, 8:12, ch * 64:(ch + 1) * 64],
                    ps[5][0:64, 0:256].rearrange("p (h e) -> p h e", h=4))),
                    reads=["psb5"], writes=["aqt"])
                sc.op("v", (lambda v, crb=crb: v.tensor_tensor(
                    ksacc[0:64, 0:12], ksacc[0:64, 0:12],
                    crb[0:64, 768:780], A.add)),
                    reads=[crn], writes=["ksacc"])
                sc.op("v", (lambda v, crb=crb: v.tensor_tensor(
                    ctxacc[0:64, 0:768], ctxacc[0:64, 0:768],
                    crb[0:64, 0:768], A.add)),
                    reads=[crn], writes=["ctxacc"])
                if cl % 4 == 3:
                    th0 = (cl // 4) * 256
                    gs4 = gstrip.rearrange("p (h q t) -> p h q t", h=6, q=2)
                    dma("y", atv[0:64, 2:8, th0:th0 + 256], gs4[0:64, :, 0, :],
                        reads=["aqt"], writes=["xn"])
                    dma("y", atv[64:128, 2:8, th0:th0 + 256], gs4[0:64, :, 1, :],
                        reads=["aqt"], writes=["xn"])
            t0 = tk * 512
            for o in range(NB):
                wb = wovs[o % 2]
                wn = "qearea"
                dma("y", wb, scr["aWo"].ap()[o].transpose([1, 0, 2]),
                    writes=[wn])
                bank = 2 + o % 2
                sc.op("t", (lambda t, wb=wb, bank=bank: [
                    t.matmul(ps[bank][:, 0:512], wb[:, c, :], atv[:, c, :],
                             start=(c == 0), stop=(c == NB - 1))
                    for c in range(NB)][-1]),
                    reads=[wn, "xn"], writes=[f"psb{bank}"])
                sc.op("v", (lambda v, bank=bank, o=o, t0=t0:
                            v.scalar_tensor_tensor(
                                xv[:, o, t0:t0 + 512], ps[bank][:, 0:512],
                                abo_sb[:, o:o + 1], xv[:, o, t0:t0 + 512],
                                A.add, A.add)),
                      reads=[f"psb{bank}", "abo"] + xnames(t0, t0 + 512),
                      writes=xnames(t0, t0 + 512))
        sc.fence()

    def emit_attn_b():
        arb = Arena()
        zc = arb.f32(0, 2048, "bz")
        zv = zc.rearrange("p (s f) -> p s f", s=2)
        ngT = zv
        ccs2 = arena[:, 0:1024]
        spc = arb.f32(2048, 2048, "bsp")
        spv = spc.rearrange("p (s f) -> p s f", s=2)
        aqt3 = spc
        xn3 = arb.bf16(4096, 2048, "xn")
        xnv3 = xn3.rearrange("p (c t) -> p c t", c=NB)
        bq = arb.bf16(5120, 2048, "bq")
        bqv = bq.rearrange("p (c t) -> p c t", c=NB)
        bkt = arb.bf16(6144, 2048, "bkt")
        bktv = bkt.rearrange("p (s f) -> p s f", s=2)
        bent = arb.bf16(7168, 2048, "bent")
        bentv = bent.rearrange("p (s f) -> p s f", s=2)
        bepv = bent.rearrange("p (c t) -> p c t", c=NB)
        bkf = arb.bf16(8192, 2048, "bkf")
        bkfv = bkf.rearrange("p (c t) -> p c t", c=NB)
        bvt = arb.bf16(9216, 2048, "bvt")
        bvtv = bvt.rearrange("p (s d) -> p s d", s=2)
        bog = arb.bf16(10240, 4096, "bog")
        bogv = bog.rearrange("p (c t) -> p c t", c=NB)
        ngFs = zc.rearrange("p (s c t) -> p s c t", s=2, c=NB)
        ngtots = arena[:, 16048:16080]
        ngtv = ngtots.rearrange("p (c u) -> p c u", c=NB)
        bqps = [arb.bf16(12288, 512, "bqps0"), arb.bf16(12544, 512, "bqps1")]
        bsA = arb.bf16(12800, 1024, "bsA")
        bw = [arb.bf16(13312, 2048, "bw0"), arb.bf16(14336, 2048, "bw1")]
        bwv = [w.rearrange("p (c o) -> p c o", c=NB) for w in bw]
        bsq = arb.f32(15360, 512, "bsq")
        rmsrow = arena[32:33, 15360:15872]
        ebuf = arena[0:2, 15360:15872]
        bD = arb.f32(15872, 32, "bD")
        bDv = bD.rearrange("p (c u) -> p c u", c=NB)
        bngc = arb.f32(15904, 8, "bngc")
        bngp = arb.f32(15912, 8, "bngp")
        bA = [arb.bf16(15920, 128, "bA0"), arb.bf16(15984, 128, "bA1")]
        pmask = cview("chind2")
        NT256 = T // 256
        emit_ln_stats(1.0)

        def ln_apply(t0, w, dstv, rbslot):
            j, c0 = t0 // 512, t0 % 512
            rb = rowb[rbslot % 2]
            rbn = f"rb{rbslot % 2}"
            src = stats[j:j + 1, :].rearrange("p (q c) -> p q c", q=2)[:, :, c0:c0 + w]
            dma("y", rb[0:1, 0:2 * w].rearrange("p (q c) -> p q c", q=2), src,
                reads=["stats"], writes=[rbn])
            obase = 32 * (1 + rbslot % 2)
            orow = ones[obase:obase + 1, 0:128]
            sc.op("t", (lambda t, rb=rb, orow=orow, w=w: t.matmul(
                ps[6][0:128, 0:w], orow, rb[0:1, 0:w],
                start=True, stop=True)), reads=[rbn, "const"], writes=["psb6"])
            sc.op("t", (lambda t, rb=rb, orow=orow, w=w: t.matmul(
                ps[7][0:128, 0:w], orow, rb[0:1, w:2 * w],
                start=True, stop=True)), reads=[rbn, "const"], writes=["psb7"])
            sc.op("v", (lambda v, t0=t0, w=w: v.scalar_tensor_tensor(
                aqt3[:, 0:8 * w], xv[:, :, t0:t0 + w], 1.0,
                ps[6][:, 0:w].unsqueeze(1).broadcast_to([128, 8, w]),
                A.mult, A.mult)),
                reads=xnames(t0, t0 + w) + ["psb6"], writes=["bsp"])
            sc.op("v", (lambda v, dstv=dstv, w=w: v.scalar_tensor_tensor(
                dstv[:, :, 0:w], aqt3[:, 0:8 * w], 1.0,
                ps[7][:, 0:w].unsqueeze(1).broadcast_to([128, 8, w]),
                A.mult, A.add)),
                reads=["bsp", "psb7"], writes=["xn"])

        def emit_group_out(g0):
            for c in range(NB):
                sc.op("s", (lambda e, c=c: e.activation(
                    bsq[:, 0:512], bogv[:, c, :], AF.Square)),
                    reads=["bog"], writes=["bsq"])
                sc.op("t", (lambda t, c=c: t.matmul(
                    ps[6][0:1, 0:512], ones[:, 0:1], bsq[:, 0:512],
                    start=(c == 0), stop=(c == NB - 1))),
                    reads=["bsq", "const"], writes=["psb6"])
            sc.op("v", lambda v: v.tensor_scalar(
                rmsrow[0:1, 0:512], ps[6][0:1, 0:512], INVD, None, A.mult),
                reads=["psb6"], writes=["bsq"])
            sc.op("s", lambda e: e.activation(
                rmsrow[0:1, 0:512], rmsrow[0:1, 0:512], AF.Ln, bias=EPS_RMS),
                reads=["bsq"], writes=["bsq"])
            sc.op("s", lambda e: e.activation(
                rmsrow[0:1, 0:512], rmsrow[0:1, 0:512], AF.Exp, scale=-0.5),
                reads=["bsq"], writes=["bsq"])
            sc.op("t", lambda t: t.matmul(
                ps[7][0:128, 0:512], ones[32:33, 0:128], rmsrow[0:1, 0:512],
                start=True, stop=True), reads=["bsq", "const"], writes=["psb7"])
            for c in range(NB):
                sc.op("v", (lambda v, c=c: v.scalar_tensor_tensor(
                    bogv[:, c, :], bogv[:, c, :], bgw_sb[:, c:c + 1],
                    ps[7][:, 0:512], A.mult, A.mult)),
                    reads=["psb7", "bgw"], writes=["bog"])
            for o in range(NB):
                wv, wn = wload("bWo", [o])
                bank = 4 + o % 2
                sc.op("t", (lambda t, wv=wv, bank=bank: [
                    t.matmul(ps[bank][:, 0:512], wv[:, c, 0:128], bogv[:, c, :],
                             start=(c == 0), stop=(c == NB - 1))
                    for c in range(NB)][-1]),
                    reads=[wn, "bog"], writes=[f"psb{bank}"])
                sc.op("v", (lambda v, bank=bank, o=o, g0=g0: v.tensor_tensor(
                    xv[:, o, g0:g0 + 512], ps[bank][:, 0:512],
                    xv[:, o, g0:g0 + 512], A.add)),
                    reads=[f"psb{bank}"] + xnames(g0, g0 + 512),
                    writes=xnames(g0, g0 + 512))

        sc.op("v", lambda v: v.memset(S_state[:, :], 0.0), writes=["S"])
        sc.op("v", lambda v: v.memset(bngc[:, 0:8], 0.0), writes=["bngc"])
        wcnt = [0]

        def wload(nm, obs):
            i = wcnt[0] % 2
            wcnt[0] += 1
            wn = f"bw{i}"
            for k, ob in enumerate(obs):
                dma("y", bwv[i][:, :, k * 128:(k + 1) * 128],
                    scr[nm].ap()[ob].transpose([1, 0, 2]), writes=[wn])
            return bwv[i][:, :, 0:128 * len(obs)], wn

        for tk in range(NT256):
            t0 = tk * 256
            ln_apply(t0, 256, xnv3, tk)
            for nm, which in (("bWf", "z"), ("bWi", "v")):
                for og in range(4):
                    wv, wn = wload(nm, [og * 2, og * 2 + 1])
                    for slot in range(2):
                        bank = 2 + slot
                        def mmz(t, wv=wv, slot=slot, bank=bank):
                            last = None
                            for c in range(NB):
                                last = t.matmul(
                                    ps[bank][:, 0:256],
                                    xnv3[:, c, slot * 128:(slot + 1) * 128],
                                    wv[:, c, :],
                                    start=(c == 0), stop=(c == NB - 1))
                            return last
                        dstv_ = zv if which == "z" else bvtv
                        dn = "bz" if which == "z" else "bvt"
                        sc.op("t", mmz, reads=["xn", wn], writes=[f"psb{bank}"])
                        sc.op("v" if slot == 0 else "s",
                              (lambda e, slot=slot, og=og, bank=bank,
                               dstv_=dstv_, sv=(slot == 1):
                               (e.copy if sv else e.tensor_copy)(
                                  dstv_[:, slot, og * 256:(og + 1) * 256],
                                  ps[bank][:, 0:256])),
                              reads=[f"psb{bank}"], writes=[dn])
            for slot in range(2):
                sc.op("s", (lambda e, slot=slot: e.activation(
                    spv[:, slot, :], zv[:, slot, :], AF.Exp, scale=-1.0)),
                    reads=["bz"], writes=["bsp"])
                sc.op("s", (lambda e, slot=slot: e.activation(
                    spv[:, slot, :], spv[:, slot, :], AF.Ln, bias=1.0)),
                    reads=["bsp"], writes=["bsp"])
                sc.op("s", (lambda e, slot=slot: e.activation(
                    bentv[:, slot, :], spv[:, slot, :], AF.Exp, scale=-1.0)),
                    reads=["bsp"], writes=["bent"])
                sc.op("v", (lambda v, slot=slot: v.tensor_scalar(
                    bktv[:, slot, :], bentv[:, slot, :], -1.0, 1.0,
                    A.mult, A.add)),
                    reads=["bent"], writes=["bkt"])
            kubuf = bsA
            for slot in range(2):
                sc.op("t", (lambda t, slot=slot: [
                    t.matmul(ps[4 + fh][:, 0:512], cview("tri64"),
                             spv[:, slot, fh * 512:(fh + 1) * 512],
                             start=True, stop=True)
                    for fh in range(2)][-1]),
                    reads=["bsp", "const"], writes=["psb4", "psb5"])
                for fh in range(2):
                    sc.op("s", (lambda e, slot=slot, fh=fh: e.activation(
                        bentv[:, slot, fh * 512:(fh + 1) * 512],
                        ps[4 + fh][:, 0:512], AF.Exp)),
                        reads=[f"psb{4 + fh}"], writes=["bent"])
                    sc.op("v", (lambda v, slot=slot, fh=fh: v.tensor_copy(
                        ngT[:, slot, fh * 512:(fh + 1) * 512],
                        ps[4 + fh][:, 0:512])),
                        reads=[f"psb{4 + fh}"], writes=["bz"])
                sc.op("v", (lambda v, slot=slot: v.tensor_tensor(
                    bktv[:, slot, :], bktv[:, slot, :], bentv[:, slot, :],
                    A.mult)), reads=["bent"], writes=["bkt"])
                for half in range(2):
                    bank = 4 + half
                    sc.op("t", (lambda t, slot=slot, half=half, bank=bank: [
                        t.transpose(ps[bank][:, i * 64:(i + 1) * 64]
                                    .bitcast(BF16),
                                    bktv[:, slot, (half * 4 + i) * 128:
                                         (half * 4 + i + 1) * 128], identb)
                        for i in range(4)][-1]),
                        reads=["bkt", "const"], writes=[f"psb{bank}"])
                    sc.op("v", (lambda v, slot=slot, half=half, bank=bank:
                                v.tensor_copy(
                                    bkfv[:, half * 4:half * 4 + 4,
                                         slot * 128:(slot + 1) * 128],
                                    ps[bank][:, 0:256].bitcast(BF16)
                                    .rearrange("p (c t) -> p c t", c=4))),
                          reads=[f"psb{bank}"], writes=["bkf"])

            for slot in range(2):
                for half in range(2):
                    bank = 4 + half
                    sc.op("t", (lambda t, slot=slot, half=half, bank=bank: [
                        t.transpose(ps[bank][:, i * 128:(i + 1) * 128],
                                    ngT[:, slot, (half * 4 + i) * 128:
                                        (half * 4 + i + 1) * 128], ident)
                        for i in range(4)][-1]),
                        reads=["bz", "const"], writes=[f"psb{bank}"])
                    sc.op("v", (lambda v, slot=slot, half=half, bank=bank:
                                v.tensor_copy(
                                    ngFs[:, slot, half * 4:half * 4 + 4, :],
                                    ps[bank][:, 0:512]
                                    .rearrange("p (c t) -> p c t", c=4))),
                          reads=[f"psb{bank}"], writes=["bz"])
            ngF5 = zc.rearrange("p (s c q t) -> p s c q t", s=2, c=NB, q=2)
            sc.op("v", lambda v: v.tensor_copy(
                ngtots.rearrange("p (c s q) -> p c s q", c=NB, s=2),
                ngF5[:, :, :, :, 63:64].transpose([0, 2, 1, 3, 4])
                .rearrange("p c s q one -> p c s (q one)")),
                reads=["bz"], writes=["bngt"])
            sc.op("s", lambda e: e.activation(
                bD[:, 0:32], ngtots[:, 0:32], AF.Exp, scale=-1.0),
                reads=["bngt"], writes=["bD"])
            sc.op("s", lambda e: e.activation(
                bent[:, 0:2048].rearrange("p (c s t) -> p s c t", s=2, c=NB),
                ngFs[:, :, :, :], AF.Exp, scale=-1.0,
                bias=float(SCALAR_CONSTS["lnscale_b"])),
                reads=["bz"], writes=["bent"])
            for o in range(NB):
                wv, wn = wload("bWq", [o])
                bank = 2 + o % 2
                sc.op("t", (lambda t, wv=wv, bank=bank: [
                    t.matmul(ps[bank][:, 0:256], wv[:, c, 0:128],
                             xnv3[:, c, :], start=(c == 0), stop=(c == NB - 1))
                    for c in range(NB)][-1]),
                    reads=[wn, "xn"], writes=[f"psb{bank}"])
                sc.op("s", (lambda e, bank=bank, o=o: e.activation(
                    bqv[:, o, :], ps[bank][:, 0:256], AF.Silu)),
                    reads=[f"psb{bank}"], writes=["bq"])
            sc.op("v", lambda v: v.tensor_tensor(
                bq[:, 0:2048], bq[:, 0:2048], bent[:, 0:2048], A.mult),
                reads=["bent"], writes=["bq"])
            if tk < 2:
                for cl in range(4):
                    cu = tk * 4 + cl
                    sc.op("s", lambda e: e.activation(
                        bngp[:, 0:8], bngc[:, 0:8], AF.Exp, scale=-1.0),
                        reads=["bngc"], writes=["bngp"])
                    qp = bqps[cu % 2]
                    qn = f"bqps{cu % 2}"
                    sc.op("v", (lambda v, cl=cl, qp=qp: v.tensor_tensor(
                        qp[:, 0:512].rearrange("p (c t) -> p c t", c=NB),
                        bqv[:, :, cl * 64:(cl + 1) * 64],
                        bngp[:, 0:8].unsqueeze(2).broadcast_to([128, 8, 64]),
                        A.mult)), reads=["bq", "bngp"], writes=[qn])
                    dma("y", scr["qp0"].ap()[cu], qp[:, 0:512], reads=[qn], writes=["scrqp0"])
                    sc.op("v", (lambda v, cl=cl: v.tensor_tensor(
                        bngc[:, 0:8], bngc[:, 0:8],
                        ngtv[:, :, cl:cl + 1], A.add)),
                        reads=["bngt"], writes=["bngc"])
            for slot in range(2):
                obank = [2 * slot, 2 * slot + 1]
                for h in range(HEADS_B):
                    pa = 4 + h % 2
                    sc.op("t", (lambda t, slot=slot, h=h, pa=pa: t.matmul(
                        ps[pa][:, 0:128], bkfv[:, h, slot * 128:(slot + 1) * 128],
                        bqv[:, h, slot * 128:(slot + 1) * 128],
                        start=True, stop=True)),
                        reads=["bkf", "bq"], writes=[f"psb{pa}"])
                    sc.op("v", (lambda t_, h=h, pa=pa: t_.tensor_tensor(
                        bA[h % 2][:, 0:128], ps[pa][:, 0:128],
                        cview("amask").bitcast(BF16), A.mult)),
                        reads=[f"psb{pa}", "const"], writes=[f"bA{h % 2}"])
                    ob = obank[h // 4]
                    sc.op("t", (lambda t, slot=slot, h=h, ob=ob: t.matmul(
                        ps[ob][:, (h % 4) * 128:(h % 4) * 128 + 128],
                        bvtv[:, slot, h * 128:(h + 1) * 128],
                        bA[h % 2][:, 0:128], start=(h % 4 == 0), stop=False)),
                        reads=[f"bA{h % 2}", "bvt"], writes=[f"psb{ob}"])
                for cpar in range(2):
                    cl = slot * 2 + cpar
                    cu = tk * 4 + cl
                    sb = cpar * 64
                    def mminter(t, slot=slot, cpar=cpar, sb=sb, obank=tuple(obank)):
                        last = None
                        for h in range(HEADS_B):
                            ob = obank[h // 4]
                            last = t.matmul(
                                ps[ob][:, (h % 4) * 128 + sb:
                                       (h % 4) * 128 + sb + 64],
                                S_state[:, h * 128:(h + 1) * 128].bitcast(BF16)
                                if False else
                                S_state[:, h * 128:(h + 1) * 128],
                                bqv[:, h, slot * 128 + sb:slot * 128 + sb + 64],
                                start=False, stop=(cpar == 1 and h % 4 == 3))
                        return last
                    sc.op("t", mminter, reads=["S", "bq"],
                          writes=[f"psb{obank[0]}", f"psb{obank[1]}"])
                    for fh in range(2):
                        sc.op("t", (lambda t, slot=slot, cpar=cpar, fh=fh:
                                    t.matmul(
                                        ps[6 + fh][0:1, 0:512],
                                        pmask[:, cpar:cpar + 1],
                                        spv[:, slot, fh * 512:(fh + 1) * 512],
                                        start=True, stop=True)),
                              reads=["bsp", "const"], writes=[f"psb{6 + fh}"])
                        sc.op("s", (lambda e, fh=fh: e.activation(
                            ebuf[0:1, 0:512], ps[6 + fh][0:1, 0:512], AF.Exp,
                            scale=-1.0)),
                            reads=[f"psb{6 + fh}"], writes=["bsq"])
                        sc.op("t", (lambda t, cpar=cpar, fh=fh: t.matmul(
                            ps[6 + fh][0:128, 0:512],
                            cview("pind2")[0:1, cpar * 128:cpar * 128 + 128],
                            ebuf[0:1, 0:512], start=True, stop=True)),
                            reads=["bsq", "const"], writes=[f"psb{6 + fh}"])
                        sc.op("v", (lambda v, slot=slot, fh=fh:
                                    v.tensor_tensor(
                                        kubuf[:, fh * 512:(fh + 1) * 512],
                                        bktv[:, slot, fh * 512:(fh + 1) * 512],
                                        ps[6 + fh][:, 0:512], A.mult)),
                              reads=[f"psb{6 + fh}", "bkt"], writes=["bsA"])
                    def mmu(t, slot=slot):
                        last = None
                        for h in range(HEADS_B):
                            bank = 6 + h // 4
                            last = t.matmul(
                                ps[bank][:, (h % 4) * 128:(h % 4) * 128 + 128],
                                kubuf[:, h * 128:(h + 1) * 128],
                                bvtv[:, slot, h * 128:(h + 1) * 128],
                                start=True, stop=True)
                        return last
                    sc.op("t", mmu, reads=["bsA", "bvt"],
                          writes=["psb6", "psb7"])
                    def scan(v, cl=cl):
                        last = None
                        for h in range(HEADS_B):
                            bank = 6 + h // 4
                            last = v.scalar_tensor_tensor(
                                S_state[:, h * 128:(h + 1) * 128],
                                S_state[:, h * 128:(h + 1) * 128],
                                bD[:, h * 4 + cl:h * 4 + cl + 1],
                                ps[bank][:, (h % 4) * 128:(h % 4) * 128 + 128],
                                A.mult, A.add)
                        return last
                    sc.op("v", scan, reads=["psb6", "psb7", "bD"],
                          writes=["S"])
                for bi in range(2):
                    ob = obank[bi]
                    sc.op("v" if bi == 0 else "s",
                          (lambda e, ob=ob, bi=bi, slot=slot, tk=tk,
                           sv=(bi == 1): (e.copy if sv else e.tensor_copy)(
                              bogv[:, bi * 4:bi * 4 + 4,
                                   (tk % 2) * 256 + slot * 128:
                                   (tk % 2) * 256 + slot * 128 + 128],
                              ps[ob][:, 0:512]
                              .rearrange("p (c t) -> p c t", c=4))),
                          reads=[f"psb{ob}"], writes=["bog"])
            if tk % 2 == 1:
                g0 = (tk - 1) * 256
                if g0 == 0:
                    dma("y", scr["og0"].ap()[:, :], bog[:, 0:4096], reads=["bog"], writes=["scrog0"])
                else:
                    emit_group_out(g0)
        flagC = cview("flags")[:, 0:1]
        flagU = cview("flags")[:, 1:2]
        sc.op("v", lambda v: v.tensor_scalar(
            ccs2[:, 0:1024], S_state[:, 0:1024], flagC, None, A.mult),
            reads=["S", "const"], writes=["bz"])
        dma("y", cc2_in.ap(), ccs2[:, 0:1024], reads=["bz"], writes=["cc2i"])
        if no_coll:
            dma("y", cc2_out.ap(), cc2_in.ap(), reads=["cc2i"], writes=["cc2o"])
        else:
            sc.op("g", lambda g: g.collective_compute(
                "AllReduce", A.add, replica_groups=RGROUPS,
                ins=[cc2_in.ap().opt()], outs=[cc2_out.ap().opt()]),
                reads=["cc2i"], writes=["cc2o"])
        dma("y", ccs2[:, 0:1024], cc2_out.ap(), reads=["cc2o"], writes=["bz"])
        sc.op("v", lambda v: v.tensor_scalar(
            bsA[:, 0:1024], ccs2[:, 0:1024], flagU, None, A.mult),
            reads=["bz", "const"], writes=["bsA"])
        qpb = arena[:, 8192:10240].bitcast(BF16)
        qpv = qpb.rearrange("p (u c t) -> p u c t", u=8, c=NB)
        dma("y", bog[:, 0:4096], scr["og0"].ap()[:, :], reads=["scrog0"], writes=["bog"])
        for cu in range(8):
            dma("y", qpb[:, cu * 512:(cu + 1) * 512], scr["qp0"].ap()[cu],
                reads=["scrqp0"], writes=["bkf", "bvt"])
        for h in range(HEADS_B):
            sc.op("t", (lambda t, h=h: t.matmul(
                ps[6][:, 0:512], bsA[:, h * 128:(h + 1) * 128],
                qpv[:, :, h, :], start=True, stop=True)),
                reads=["bsA", "bkf", "bvt"], writes=["psb6"])
            sc.op("v", (lambda v, h=h: v.tensor_tensor(
                bogv[:, h, :], bogv[:, h, :], ps[6][:, 0:512], A.add)),
                reads=["psb6"], writes=["bog"])
        emit_group_out(0)
        sc.fence()

    def emit_ffn():
        h_bf = ABf(0, 16384)
        hvb = h_bf.rearrange("p (c t) -> p c t", c=NH)
        xi = ABf(32, 4096)
        xivb = xi.rearrange("p (c t) -> p c t", c=NB)
        qt = AFr(40, 2048)
        qtmp = [AFr(48, 512), AFr(50, 512)]
        wb1 = [ABf(52, 1024), ABf(54, 1024)]
        wb2 = [ABf(56, 2048), ABf(60, 2048)]
        emit_ln_stats(16.0)
        for tk in range(NT512):
            pS = ps[0 if tk % 2 == 0 else 6]
            pB = ps[1 if tk % 2 == 0 else 7]
            nS = "psb0" if tk % 2 == 0 else "psb6"
            nB = "psb1" if tk % 2 == 0 else "psb7"
            rb = rowb[tk % 2]
            rbn = f"rb{tk % 2}"
            dma("y", rb, stats[tk:tk + 1, 0:1024], reads=["stats"], writes=[rbn])
            obase = 32 * (1 + tk % 2)
            orow = ones[obase:obase + 1, 0:128]
            sc.op("t", (lambda t, pS=pS, rb=rb, orow=orow: t.matmul(
                pS[0:128, 0:512], orow, rb[0:1, 0:512],
                start=True, stop=True)), reads=[rbn, "const"], writes=[nS])
            sc.op("t", (lambda t, pB=pB, rb=rb, orow=orow: t.matmul(
                pB[0:128, 0:512], orow, rb[0:1, 512:1024],
                start=True, stop=True)), reads=[rbn, "const"], writes=[nB])
            t0 = tk * 512
            for th in range(2):
                sl = slice(t0 + th * 256, t0 + th * 256 + 256)
                xsrc = xv[:, :, sl]
                sc.op("v", (lambda v, xsrc=xsrc, pS=pS, th=th: v.scalar_tensor_tensor(
                    qt[:, 0:2048], xsrc, 1.0,
                    pS[:, th * 256:(th + 1) * 256].unsqueeze(1)
                    .broadcast_to([128, 8, 256]),
                    A.mult, A.mult)),
                    reads=xnames(t0, t0 + 512) + [nS], writes=["qtarea"])
                sc.op("v", (lambda v, pB=pB, th=th: v.scalar_tensor_tensor(
                    qt[:, 0:2048], qt[:, 0:2048], 1.0,
                    pB[:, th * 256:(th + 1) * 256].unsqueeze(1)
                    .broadcast_to([128, 8, 256]),
                    A.mult, A.add)),
                    reads=[nB], writes=["qtarea"])
                sc.op("v", lambda v: v.tensor_scalar(
                    qt[:, 0:2048], qt[:, 0:2048], -128.0, 127.0, A.max, A.min),
                    reads=["qtarea"], writes=["qtarea"])
                xdst = xivb[:, :, th * 256:(th + 1) * 256]
                sc.op("v", (lambda v, xdst=xdst: v.tensor_scalar(
                    xdst, qt[:, 0:2048], MAGIC, -MAGIC, A.add, A.add)),
                    reads=["qtarea"], writes=["xi"])
            for o1 in range(NH):
                wb = wb1[o1 % 2]
                wn = f"wb1_{o1 % 2}"
                wvb = wb.rearrange("p (c o) -> p c o", c=NB)
                dma("y", wvb, scr["w1q"].ap()[o1].transpose([1, 0, 2]),
                    writes=[wn])
                bank = 2 + o1 % 2
                bn = f"psb{bank}"
                sc.op("t", (lambda t, wvb=wvb, bank=bank: [
                    t.matmul(ps[bank][:, 0:512], wvb[:, c, :], xivb[:, c, :],
                             start=(c == 0), stop=(c == NB - 1))
                    for c in range(NB)][-1]),
                    reads=[wn, "xi"], writes=[bn])
                qm = qtmp[o1 % 2]
                qmn = f"qtmp{o1 % 2}"
                e1 = "v" if o1 % 2 == 0 else "s"
                if e1 == "v":
                    sc.op("v", (lambda v, bank=bank, qm=qm, o1=o1: v.tensor_scalar(
                        qm[:], ps[bank][:, 0:512], b1q16[:, o1:o1 + 1], 0.0,
                        A.add, A.max)), reads=[bn, "b1q16"], writes=[qmn])
                else:
                    sc.op("s", (lambda e, bank=bank, qm=qm, o1=o1: e.activation(
                        qm[:], ps[bank][:, 0:512], AF.Relu,
                        bias=b1q16[:, o1:o1 + 1])), reads=[bn, "b1q16"],
                        writes=[qmn])
                sc.op("v", (lambda v, qm=qm: v.tensor_scalar(
                    qm[:], qm[:], 127.0, MAGIC, A.min, A.add)),
                    reads=[qmn], writes=[qmn])
                e3 = "s" if o1 % 2 == 0 else "v"
                if e3 == "s":
                    sc.op("s", (lambda e, qm=qm, o1=o1: e.activation(
                        hvb[:, o1, :], qm[:], AF.Copy, bias=-MAGIC)),
                        reads=[qmn], writes=[f"h{o1}"])
                else:
                    sc.op("v", (lambda v, qm=qm, o1=o1: v.tensor_scalar(
                        hvb[:, o1, :], qm[:], -MAGIC, None, A.add)),
                        reads=[qmn], writes=[f"h{o1}"])
            for o2 in range(NB):
                bank = 4 + o2 % 2
                bn = f"psb{bank}"
                for hh in range(2):
                    wb = wb2[(o2 * 2 + hh) % 2]
                    wn = f"wb2_{(o2 * 2 + hh) % 2}"
                    wvb = wb.rearrange("p (c o) -> p c o", c=16)
                    dma("y", wvb,
                        scr["w2q"].ap()[o2, hh * 16:(hh + 1) * 16]
                        .transpose([1, 0, 2]), writes=[wn])
                    sc.op("t", (lambda t, wvb=wvb, bank=bank, hh=hh: [
                        t.matmul(ps[bank][:, 0:512], wvb[:, c, :],
                                 hvb[:, hh * 16 + c, :],
                                 start=(hh == 0 and c == 0),
                                 stop=(hh == 1 and c == 15))
                        for c in range(16)][-1]),
                        reads=[wn] + [f"h{hh * 16 + c}" for c in range(16)],
                        writes=[bn])
                sc.op("v", (lambda v, bank=bank, o2=o2, t0=t0: v.scalar_tensor_tensor(
                    xv[:, o2, t0:t0 + 512], ps[bank][:, 0:512],
                    b2q[:, o2:o2 + 1], xv[:, o2, t0:t0 + 512],
                    A.add, A.add)),
                    reads=[bn, "b2q"] + xnames(t0, t0 + 512),
                    writes=xnames(t0, t0 + 512))
        sc.fence()

    if "attn_a" in phases:
        emit_attn_a()
    if "ffn1" in phases:
        emit_ffn()
    if "attn_b" in phases:
        emit_attn_b()
    if "ffn2" in phases:
        emit_ffn()

    sbuf_store = [AFr(0, 1024), AFr(4, 1024)]
    store_ops = []
    for i in range(NT128):
        tb = sbuf_store[i % 2]
        tbn = f"stb{i % 2}"
        for h in range(2):
            bank = 2 + h
            sc.op("t", (lambda t, i=i, h=h, bank=bank: [
                t.transpose(ps[bank][:, c * 128:(c + 1) * 128],
                            xv[:, 4 * h + c, i * 128:(i + 1) * 128], ident)
                for c in range(4)][-1]),
                reads=[f"x_{i}", "const"], writes=[f"psb{bank}"])
            sc.op("v" if h == 0 else "s",
                  (lambda e, h=h, bank=bank, tb=tb, s=(h == 1):
                   (e.copy if s else e.tensor_copy)(
                       tb[:, h * 512:(h + 1) * 512], ps[bank][:, 0:512])),
                  reads=[f"psb{bank}"], writes=[tbn])
        store_ops.append(
            dma("y", y_out[i * 128:(i + 1) * 128, :], tb[:], reads=[tbn]))

    sc.op("g", lambda g: g.nop(), extra=store_ops)

    sc.finalize()
    with nc.Block() as block:
        @block.tensor
        def _(t):
            sc.run("t", t)

        @block.vector
        def _(v):
            sc.run("v", v)

        @block.scalar
        def _(s):
            sc.run("s", s)

        @block.gpsimd
        def _(g):
            sc.run("g", g)

        @block.sync
        def _(y):
            sc.run("y", y)

    return nc


_NC_CACHE = {}


def get_nc(T, n_cores=8):
    key = (T, n_cores)
    if key not in _NC_CACHE:
        _NC_CACHE[key] = build_nc(T, n_cores=n_cores)
    return _NC_CACHE[key]


def make_wmap(inputs):
    wmap = {
        "aWq": inputs["L0_attn_a_Wq"], "aWk": inputs["L0_attn_a_Wk"],
        "aWv": inputs["L0_attn_a_Wv"], "aWo": inputs["L0_attn_a_Wo"],
        "abo": np.asarray(inputs["L0_attn_a_bo"], np.float32).reshape(1, D),
        "bWq": inputs["L0_attn_b_Wq"], "bWf": inputs["L0_attn_b_Wf"],
        "bWi": inputs["L0_attn_b_Wi"], "bWo": inputs["L0_attn_b_Wo"],
        "bgw": np.asarray(inputs["L0_attn_b_g_weight"], np.float32).reshape(1, D),
        "fw1": inputs["L0_ffn_w1"],
        "fb1": np.asarray(inputs["L0_ffn_b1"], np.float32).reshape(1, HID),
        "fw2": inputs["L0_ffn_w2"],
        "fb2": np.asarray(inputs["L0_ffn_b2"], np.float32).reshape(1, D),
    }
    return {k: np.ascontiguousarray(np.asarray(v, np.float32))
            for k, v in wmap.items()}


def kernel(**inputs):
    x = np.asarray(inputs["L0_attn_a_x"], np.float32)
    Bz, S, _ = x.shape
    T = S // 2
    nc = get_nc(T, 2 * Bz)
    wmap = make_wmap(inputs)
    consts = [make_consts(False), make_consts(True)]
    zero_halo = np.zeros((WINDOW, D), np.float32)
    in_maps = []
    for b in range(Bz):
        for h in range(2):
            m = dict(wmap)
            m["x_in"] = np.ascontiguousarray(x[b, h * T:(h + 1) * T, :])
            m["halo_in"] = (np.ascontiguousarray(x[b, T - WINDOW:T, :]) if h == 1
                            else zero_halo)
            m["consts"] = consts[h]
            in_maps.append(m)
    res = run_bass_kernel_spmd(nc, in_maps, core_ids=list(range(2 * Bz)))
    out = np.empty((Bz, S, D), np.float32)
    for b in range(Bz):
        for h in range(2):
            out[b, h * T:(h + 1) * T, :] = res.results[2 * b + h]["y_out"]
    return out


# version 07
# speedup vs baseline: 1.5897x; 1.1838x over previous
import heapq
import numpy as np
import concourse.bass as bass
import concourse.mybir as mybir
from concourse.bass_utils import run_bass_kernel_spmd
from concourse.alu_op_type import AluOpType as A
from contextlib import ExitStack

F32 = mybir.dt.float32
BF16 = mybir.dt.bfloat16
AF = mybir.ActivationFunctionType

D = 1024
NB = D // 128
WINDOW = 128
CHUNK = 64
LOCAL_HEADS = 4
GLOBAL_HEADS = 12
DH = 64
HEADS_B = 8
DF = 128
HID = 4096
NH = HID // 128
MAGIC = float(np.float32(1.5 * 2.0 ** 23))
NEG = -1.0e30
EPS_LN = 1e-5
EPS_RMS = 1e-5
EPS_CTX = 1e-6
INVD = 1.0 / D


class ConstReg:
    def __init__(self):
        self.off = {}
        self.w = {}
        self.n = 0

    def reg(self, name, w):
        self.off[name] = self.n
        self.w[name] = w
        self.n += w


CR = ConstReg()
CR.reg("ident", 128)
CR.reg("identb", 64)
CR.reg("tri64", 128)
CR.reg("mask_self", 128)
CR.reg("mask_prev0", 128)
CR.reg("amask", 64)
CR.reg("ones", 128)
CR.reg("onesb", 8)
CR.reg("chind2", 2)
CR.reg("pind2", 256)
CR.reg("selcols", 64)
CR.reg("hsum12", 72)
CR.reg("hexp12", 384)
CR.reg("flags", 4)
CR.reg("cscal", 8)
SCALAR_CONSTS = {
    "eps": 1e-5,
    "negmagic": -MAGIC,
    "lnscale_b": float(-0.5 * np.log(128.0)),
    "epsctx": EPS_CTX,
    "one": 1.0,
}
CONST_COLS = ((CR.n + 15) // 16) * 16


def make_consts(is_half1):
    import ml_dtypes
    c = np.zeros((128, CONST_COLS), np.float32)

    def put(name, arr):
        o = CR.off[name]
        c[:, o:o + CR.w[name]] = arr

    put("ident", np.eye(128, dtype=np.float32))
    ibb = np.ascontiguousarray(np.eye(128, dtype=np.float32).astype(ml_dtypes.bfloat16))
    put("identb", ibb.view(np.float32))
    tri = np.zeros((128, 128), np.float32)
    for blk in range(2):
        for tt in range(64):
            tri[blk * 64:blk * 64 + tt + 1, blk * 64 + tt] = 1.0
    put("tri64", tri)
    ms = np.where(np.arange(128)[:, None] > np.arange(128)[None, :], NEG, 0.0)
    put("mask_self", ms.astype(np.float32))
    put("mask_prev0", np.zeros((128, 128), np.float32) if is_half1
        else np.full((128, 128), NEG, np.float32))
    am = np.zeros((128, 128), np.float32)
    for blk in range(2):
        for tt in range(64):
            am[blk * 64:blk * 64 + tt + 1, blk * 64 + tt] = 1.0
    put("amask", np.ascontiguousarray(am.astype(ml_dtypes.bfloat16)).view(np.float32))
    put("ones", np.ones((128, 128), np.float32))
    put("onesb", np.ascontiguousarray(
        np.ones((128, 16), np.float32).astype(ml_dtypes.bfloat16)).view(np.float32))
    ci = np.zeros((128, 2), np.float32)
    ci[0:64, 0] = 1.0
    ci[64:128, 1] = 1.0
    put("chind2", ci)
    pi = np.zeros((128, 256), np.float32)
    pi[0, 0:64] = 1.0
    pi[0, 128 + 64:256] = 1.0
    put("pind2", pi)
    sel = np.zeros((128, 64), np.float32)
    for j in range(8):
        sel[:, 8 * j + j] = 1.0
    put("selcols", sel)
    hs = np.zeros((128, 72), np.float32)
    he = np.zeros((128, 768), np.float32)
    for cb in range(6):
        hs[0:64, 12 * cb + 2 * cb] = 1.0
        hs[64:128, 12 * cb + 2 * cb + 1] = 1.0
        he[2 * cb, 128 * cb:128 * cb + 64] = 1.0
        he[2 * cb + 1, 128 * cb + 64:128 * cb + 128] = 1.0
    put("hsum12", hs)
    put("hexp12", np.ascontiguousarray(
        he.astype(ml_dtypes.bfloat16)).view(np.float32))
    fl = np.zeros((128, 4), np.float32)
    fl[:, 0] = 0.0 if is_half1 else 1.0
    fl[:, 1] = 1.0 if is_half1 else 0.0
    put("flags", fl)
    cs = np.zeros((128, 8), np.float32)
    for i, v in enumerate(SCALAR_CONSTS.values()):
        cs[:, i] = np.float32(v)
    put("cscal", cs)
    return c


class Op:
    __slots__ = ("eng", "fn", "deps", "dma", "needed", "idx", "sem", "val",
                 "inc", "cost")

    def __init__(self, eng, fn, deps, dma, idx, cost=None):
        self.eng, self.fn, self.deps, self.dma, self.idx = eng, fn, list(deps), dma, idx
        self.needed = dma
        self.sem = None
        self.val = 0
        self.inc = 16 if dma else 1
        self.cost = cost


class Buf:
    __slots__ = ("w", "r")

    def __init__(self):
        self.w = None
        self.r = []


class Sched:
    NPOOL = 4

    def __init__(self, nc, stack, n_dma_sems=48):
        self.nc = nc
        self.ops = []
        self.eng_sem = {e: [stack.enter_context(nc.semaphore(f"es_{e}{i}"))
                            for i in range(self.NPOOL)]
                        for e in ("t", "v", "s", "g", "y")}
        self.dma_sems = [stack.enter_context(nc.semaphore(f"ds_{i}"))
                         for i in range(n_dma_sems)]
        self.dma_i = 0
        self._stack = stack
        self.gdma_sems = []
        self.bufs = {}
        self.last = {e: None for e in ("t", "v", "s", "g", "y")}
        self.dma_open = []
        self.fence_groups = []

    def B(self, name):
        if name not in self.bufs:
            self.bufs[name] = Buf()
        return self.bufs[name]

    def op(self, eng, fn, reads=(), writes=(), dma=False, extra=(), cost=None):
        psr = [b for b in reads if b.startswith("psb")]
        if psr:
            reads = [b for b in reads if not b.startswith("psb")]
            writes = list(writes) + [b for b in psr if b not in writes]
        deps = list(extra)
        for b in reads:
            bb = self.B(b)
            if bb.w is not None:
                deps.append(bb.w)
        for b in writes:
            bb = self.B(b)
            deps.extend(bb.r)
            if bb.w is not None:
                deps.append(bb.w)
        o = Op(eng, fn, [d for d in deps if d is not None], dma, len(self.ops),
               cost=cost)
        self.ops.append(o)
        self.last[eng] = o
        if dma:
            self.dma_open.append(o)
        for b in reads:
            self.B(b).r.append(o)
        for b in writes:
            bb = self.B(b)
            bb.w = o
            bb.r = []
        return o

    def fence(self):
        prev = [o for o in self.last.values() if o is not None] + self.dma_open
        self.dma_open = []
        grp = []
        for e in ("t", "v", "s", "g", "y"):
            grp.append(self.op(e, lambda eng: eng.nop(), extra=prev))
        self.fence_groups.append(grp)
        f = {e: self.last[e] for e in self.last}
        for b in self.bufs.values():
            if b.w is not None:
                b.w = f[b.w.eng]
            b.r = []

    _DEF_COST = {"t": 400.0, "v": 600.0, "s": 700.0, "g": 2500.0, "y": 300.0}

    def _list_schedule(self, region):
        n = len(region)
        if n < 3:
            return list(region)
        idx_in = {id(o): i for i, o in enumerate(region)}
        rev = [[] for _ in range(n)]
        pend = [0] * n
        for i, o in enumerate(region):
            for d in o.deps:
                j = idx_in.get(id(d))
                if j is not None:
                    rev[j].append(i)
                    pend[i] += 1
        ready = {e: [] for e in ("t", "v", "s", "g", "y")}
        for i, o in enumerate(region):
            if pend[i] == 0:
                heapq.heappush(ready[o.eng], i)
        ebusy = {e: 0.0 for e in ready}
        events = []
        order = []
        now = 0.0
        scheduled = 0
        while scheduled < n:
            started = True
            while started:
                started = False
                for e, h in ready.items():
                    if h and ebusy[e] <= now + 1e-9:
                        i = heapq.heappop(h)
                        o = region[i]
                        if o.dma:
                            ec, dc = 150.0, (o.cost or 1500.0)
                        else:
                            ec = dc = o.cost or self._DEF_COST[e]
                        ebusy[e] = now + ec
                        heapq.heappush(events, (now + dc, i))
                        order.append(o)
                        scheduled += 1
                        started = True
            if scheduled >= n:
                break
            cand = [events[0][0]] if events else []
            cand += [ebusy[e] for e, h in ready.items() if h]
            assert cand, "list-schedule deadlock (cyclic deps?)"
            now = min(cand)
            while events and events[0][0] <= now + 1e-9:
                _, i2 = heapq.heappop(events)
                for k in rev[i2]:
                    pend[k] -= 1
                    if pend[k] == 0:
                        heapq.heappush(ready[region[k].eng], k)
        return order

    def reorder(self):
        ops = self.ops
        new_ops = []
        last_eng = {}
        pos = 0
        for grp in self.fence_groups:
            start = grp[0].idx
            assert [o.idx for o in grp] == list(range(start, start + len(grp)))
            region = ops[pos:start]
            pos = grp[-1].idx + 1
            sched = self._list_schedule(region)
            new_ops.extend(sched)
            for o in sched:
                last_eng[o.eng] = o
            dmas = [o for o in region if o.dma]
            for fo in grp:
                fo.deps = list(last_eng.values()) + dmas
            new_ops.extend(grp)
            for o in grp:
                last_eng[o.eng] = o
        new_ops.extend(ops[pos:])
        assert len(new_ops) == len(ops)
        self.ops = new_ops
        for i, o in enumerate(self.ops):
            o.idx = i

    def finalize(self):
        for op in self.ops:
            seen, dd = set(), []
            for d in op.deps:
                if id(d) in seen:
                    continue
                seen.add(id(d))
                dd.append(d)
                assert d.idx < op.idx
                d.needed = True
            op.deps = dd
        cnt = {e: 0 for e in self.eng_sem}
        use = {}
        for op in self.ops:
            if not op.needed:
                continue
            if op.dma:
                if op.eng == "g":
                    sem = self._stack.enter_context(
                        self.nc.semaphore(f"gs_{len(self.gdma_sems)}"))
                    self.gdma_sems.append(sem)
                    op.sem, op.val = sem, 16
                    continue
                sem = self.dma_sems[self.dma_i % len(self.dma_sems)]
                self.dma_i += 1
                use[id(sem)] = use.get(id(sem), 0) + 16
                op.sem, op.val = sem, use[id(sem)]
            else:
                sem = self.eng_sem[op.eng][cnt[op.eng] % self.NPOOL]
                cnt[op.eng] += 1
                use[id(sem)] = use.get(id(sem), 0) + 1
                op.sem, op.val = sem, use[id(sem)]

    def run(self, ekey, eng):
        mark = {}
        for op in self.ops:
            if op.eng != ekey:
                continue
            for d in op.deps:
                k = id(d.sem)
                if mark.get(k, 0) >= d.val:
                    continue
                eng.wait_ge(d.sem, d.val)
                mark[k] = d.val
            last = op.fn(eng)
            if op.needed:
                assert last is not None, f"op idx {op.idx} on {ekey} emitted nothing"
                last.then_inc(op.sem, op.inc)


ALL_PHASES = ("attn_a", "ffn1", "attn_b", "ffn2")


def build_nc(T, phases=ALL_PHASES, n_cores=8, debug=False, no_coll=False, aa_stop=None):
    assert T % 512 == 0
    NT128 = T // 128
    NT512 = T // 512
    nc = bass.Bass("TRN2", target_bir_lowering=False, debug=debug)
    stack = ExitStack()

    x_in = nc.dram_tensor("x_in", [T, D], F32, kind="ExternalInput").ap()
    halo_in = nc.dram_tensor("halo_in", [WINDOW, D], F32, kind="ExternalInput").ap()
    consts_in = nc.dram_tensor("consts", [128, CONST_COLS], F32, kind="ExternalInput").ap()
    y_out = nc.dram_tensor("y_out", [T, D], F32, kind="ExternalOutput").ap()

    Wd = {}
    for nm, shp in [
        ("aWq", (D, D)), ("aWk", (D, D)), ("aWv", (D, D)), ("aWo", (D, D)),
        ("abo", (1, D)),
        ("bWq", (D, D)), ("bWf", (D, D)), ("bWi", (D, D)), ("bWo", (D, D)),
        ("bgw", (1, D)),
        ("fw1", (HID, D)), ("fb1", (1, HID)), ("fw2", (D, HID)), ("fb2", (1, D)),
    ]:
        Wd[nm] = nc.dram_tensor(nm, list(shp), F32, kind="ExternalInput").ap()

    scr = {}
    scr["w1q"] = nc.dram_tensor("scr_w1q", [NH, NB, 128, 128], BF16, kind="Internal")
    scr["w2q"] = nc.dram_tensor("scr_w2q", [NB, NH, 128, 128], BF16, kind="Internal")
    for nm in ("bWq", "bWf", "bWi", "bWo"):
        scr[nm] = nc.dram_tensor("scr_" + nm, [NB, NB, 128, 128], BF16, kind="Internal")
    for nm in ("aWq", "aWk", "aWv", "aWo"):
        scr[nm] = nc.dram_tensor("scr_" + nm, [NB, NB, 128, 128], BF16, kind="Internal")

    scr["ctx"] = nc.dram_tensor("scr_ctx", [T // CHUNK, 64, 784], BF16,
                                kind="Internal")
    scr["og0"] = nc.dram_tensor("scr_og0", [128, 4096], BF16, kind="Internal")
    scr["qp0"] = nc.dram_tensor("scr_qp0", [8, 128, 512], BF16, kind="Internal")

    CTX_ROWS = 128
    cc1_in = nc.dram_tensor("cc1_in", [64, 792], F32, kind="Internal")
    cc1_out = nc.dram_tensor("cc1_out", [64, 792], F32, kind="Internal")
    cc2_in = nc.dram_tensor("cc2_in", [128, HEADS_B * 128], F32, kind="Internal")
    cc2_out = nc.dram_tensor("cc2_out", [128, HEADS_B * 128], F32, kind="Internal")

    sbt = {}

    def sbuf(name, shape, dtype):
        sbt[name] = stack.enter_context(nc.sbuf_tensor(name, list(shape), dtype))
        return sbt[name]

    x_sb = sbuf("x", [128, NB * T], F32)
    const_sb = sbuf("const", [128, CONST_COLS], F32)
    b1q16 = sbuf("b1q16", [128, NH], F32)
    b2q = sbuf("b2q", [128, NB], F32)
    abo_sb = sbuf("abo_pp", [128, NB], F32)
    bgw_sb = sbuf("bgw_pp", [128, NB], F32)
    S_state = sbuf("S_state", [128, HEADS_B * 128], BF16)
    ARENA_COLS = 18304
    arena = sbuf("arena", [128, ARENA_COLS], F32)

    ps = [stack.enter_context(nc.psum_tensor(f"ps{i}", [128, 512], F32))
          for i in range(8)]

    sc = Sched(nc, stack)
    xv = x_sb.ap().rearrange("p (c t) -> p c t", c=NB)

    def cview(name):
        o = CR.off[name]
        return const_sb[:, o:o + CR.w[name]]

    ident = cview("ident")
    identb = cview("identb").bitcast(BF16)
    ones = cview("ones")
    onesb = cview("onesb").bitcast(BF16)
    for i, (k, v) in enumerate(SCALAR_CONSTS.items()):
        capv = const_sb[:, CR.off["cscal"] + i:CR.off["cscal"] + i + 1]
        nc.const_aps.register(F32, float(v), capv)
        if float(np.float32(v)) != float(v):
            nc.const_aps.register(F32, float(np.float32(v)), capv)

    def AFr(off_kb, cols):
        o = off_kb * 256
        assert o + cols <= ARENA_COLS, (off_kb, cols)
        return arena[:, o:o + cols]

    def ABf(off_kb, cols_bf):
        o = off_kb * 256
        assert o + cols_bf // 2 <= ARENA_COLS
        return arena[:, o:o + cols_bf // 2].bitcast(BF16)

    class Arena:

        def __init__(self):
            self.spans = []

        def _reg(self, off, n, buf):
            assert off + n <= ARENA_COLS, (off, n)
            for (a, b, nm) in self.spans:
                if not (off + n <= a or off >= b):
                    assert nm == buf, f"arena overlap {buf} vs {nm} @{off}"
            self.spans.append((off, off + n, buf))

        def f32(self, off, n, buf):
            self._reg(off, n, buf)
            return arena[:, off:off + n]

        def bf16(self, off, nbf, buf):
            assert nbf % 2 == 0
            self._reg(off, nbf // 2, buf)
            return arena[:, off:off + nbf // 2].bitcast(BF16)

    def dma(queue, out, in_, reads=(), writes=(), extra=(), slow=False):
        return sc.op(queue, lambda e, out=out, in_=in_, slow=slow: e.dma_start(
            out=out, in_=in_, allow_slow_non_contiguous=slow),
            reads=reads, writes=writes, dma=True, extra=extra)

    def mm_group(items, reads=(), writes=(), extra=()):
        def fn(t):
            last = None
            for (o, l, r, st, sp) in items:
                last = t.matmul(o, l, r, start=st, stop=sp)
            return last
        return sc.op("t", fn, reads=reads, writes=writes, extra=extra)

    dma("y", const_sb[:], consts_in, writes=["const"])

    ldbuf = [AFr(0, 1024), AFr(4, 1024)]
    for i in range(NT128):
        tb = ldbuf[i % 2]
        tbn = f"ldbuf{i % 2}"
        dma("y", tb, x_in[i * 128:(i + 1) * 128, :], writes=[tbn])
        for h in range(2):
            bank = h
            sc.op("t", (lambda t, tb=tb, h=h, bank=bank: [
                t.transpose(ps[bank][:, c * 128:(c + 1) * 128],
                            tb[:, (4 * h + c) * 128:(4 * h + c + 1) * 128], ident)
                for c in range(4)][-1]),
                reads=[tbn, "const"], writes=[f"psb{bank}"])
            sc.op("v" if h == 0 else "s",
                  (lambda e, i=i, h=h, bank=bank, s=(h == 1):
                   (e.copy if s else e.tensor_copy)(
                       xv[:, 4 * h:4 * h + 4, i * 128:(i + 1) * 128],
                       ps[bank][:, 0:512])),
                  reads=[f"psb{bank}"], writes=[f"x_{i}"])
    sc.fence()

    def prep_transposed(src, scr_t, n_ob, n_ic, quant_scale=None, out_scale=1.0):
        lb = [AFr(0, 1024), AFr(4, 1024)]
        qb = [AFr(8, 1024), AFr(12, 1024)]
        ob_bf = [ABf(16, n_ic * 128), ABf(16 + (n_ic * 128 // 512), n_ic * 128)]
        for ob in range(n_ob):
            tb = lb[ob % 2]
            tbn = f"prep_lb{ob % 2}"
            for half in range(n_ic // 8):
                dma("y", tb, src[ob * 128:(ob + 1) * 128,
                                 half * 1024:(half + 1) * 1024], writes=[tbn])
                for h in range(2):
                    bank = 4 + h
                    sc.op("t", (lambda t, tb=tb, h=h, bank=bank: [
                        t.transpose(ps[bank][:, c * 128:(c + 1) * 128],
                                    tb[:, (4 * h + c) * 128:(4 * h + c + 1) * 128],
                                    ident)
                        for c in range(4)][-1]),
                        reads=[tbn, "const"], writes=[f"psb{bank}"])
                    q = qb[h]
                    qn = f"prep_qb{h}"
                    obf = ob_bf[ob % 2]
                    obn = f"prep_ob{ob % 2}"
                    if quant_scale is not None:
                        qs = float(quant_scale)
                        sc.op("v", (lambda v, bank=bank, q=q: v.tensor_scalar(
                            q[:, 0:512], ps[bank][:, 0:512], qs, -128.0,
                            A.mult, A.max)), reads=[f"psb{bank}"], writes=[qn])
                        sc.op("v", (lambda v, q=q: v.tensor_scalar(
                            q[:, 0:512], q[:, 0:512], 127.0, MAGIC,
                            A.min, A.add)), reads=[qn], writes=[qn])
                        osc = float(out_scale / quant_scale)
                        sc.op("v", (lambda v, q=q, obf=obf, h=h, half=half:
                                    v.tensor_scalar(
                                        obf[:, (half * 8 + 4 * h) * 128:
                                            (half * 8 + 4 * h + 4) * 128],
                                        q[:, 0:512], -MAGIC, osc, A.add, A.mult)),
                              reads=[qn], writes=[obn])
                    else:
                        sc.op("v", (lambda v, bank=bank, obf=obf, h=h, half=half:
                                    v.tensor_copy(
                                        obf[:, (half * 8 + 4 * h) * 128:
                                            (half * 8 + 4 * h + 4) * 128],
                                        ps[bank][:, 0:512])),
                              reads=[f"psb{bank}"], writes=[obn])
            obf = ob_bf[ob % 2]
            obn = f"prep_ob{ob % 2}"
            dma("y", scr_t.ap()[ob].transpose([1, 0, 2]),
                obf[:].rearrange("p (c o) -> p c o", c=n_ic), reads=[obn])
        sc.fence()

    def prep_natural(src, scr_t, n_ic, n_ob):
        cb = ABf(0, n_ob * 128)
        for ic in range(n_ic):
            dma("g", cb[:], src[ic * 128:(ic + 1) * 128, :], writes=["prep_cb"])
            dma("y", scr_t.ap()[:, ic].transpose([1, 0, 2]),
                cb[:].rearrange("p (o c) -> p o c", o=n_ob), reads=["prep_cb"])
        sc.fence()

    def prep_bias(src, dst, n, name, qscale=None, out_scale=1.0):
        dma("y", dst[:, 0:n], src.rearrange("a (o p) -> a p o", p=128),
            writes=[name], slow=True)
        if qscale is not None:
            qs = float(qscale)
            osc = float(out_scale / qscale)
            sc.op("v", lambda v: v.tensor_scalar(
                dst[:, 0:n], dst[:, 0:n], qs, -128.0, A.mult, A.max),
                reads=[name], writes=[name])
            sc.op("v", lambda v: v.tensor_scalar(
                dst[:, 0:n], dst[:, 0:n], 127.0, MAGIC, A.min, A.add),
                reads=[name], writes=[name])
            sc.op("v", lambda v: v.tensor_scalar(
                dst[:, 0:n], dst[:, 0:n], -MAGIC, osc, A.add, A.mult),
                reads=[name], writes=[name])

    if "ffn1" in phases or "ffn2" in phases:
        prep_transposed(Wd["fw1"], scr["w1q"], NH, NB, quant_scale=64.0)
        prep_transposed(Wd["fw2"], scr["w2q"], NB, NH, quant_scale=64.0,
                        out_scale=1.0 / 16.0)
        prep_bias(Wd["fb1"], b1q16, NH, "b1q16", qscale=64.0, out_scale=16.0)
        prep_bias(Wd["fb2"], b2q, NB, "b2q", qscale=64.0, out_scale=1.0)
        sc.fence()
    if "attn_b" in phases:
        for nm in ("bWq", "bWf", "bWi", "bWo"):
            prep_transposed(Wd[nm], scr[nm], NB, NB)
        prep_bias(Wd["bgw"], bgw_sb, NB, "bgw")
        sc.fence()
    if "attn_a" in phases:
        for nm in ("aWq", "aWk", "aWv", "aWo"):
            prep_natural(Wd[nm], scr[nm], NB, NB)
        prep_bias(Wd["abo"], abo_sb, NB, "abo")
        sc.fence()

    ST_OFF = 16384
    stats = arena[:, ST_OFF:ST_OFF + 1024]
    rowb = [arena[32:33, ST_OFF:ST_OFF + 1024],
            arena[64:65, ST_OFF:ST_OFF + 1024]]

    def xnames(t0, t1):
        return [f"x_{i}" for i in range(t0 // 128, t1 // 128)]

    def emit_ln_stats(out_scale):
        sqb = [AFr(48, 512), AFr(50, 512)]
        tmp = [AFr(40, 512), AFr(42, 512), AFr(44, 512)]
        n = NT512 * NB
        k = 0
        for j in range(NT512):
            sel = const_sb[:, CR.off["selcols"] + 8 * j:CR.off["selcols"] + 8 * j + 8]
            for c in range(NB):
                mm_sum = (lambda t, j=j, c=c, sel=sel, k=k: t.matmul(
                    ps[0][0:8, 0:512], sel,
                    xv[:, c, j * 512:(j + 1) * 512],
                    start=(k == 0), stop=(k == n - 1)))
                sc.op("t", mm_sum, reads=xnames(j * 512, (j + 1) * 512) + ["const"],
                      writes=["psb0"])
                sq = sqb[k % 2]
                sqn = f"sqb{k % 2}"
                sc.op("s", (lambda e, j=j, c=c, sq=sq: e.activation(
                    sq[:], xv[:, c, j * 512:(j + 1) * 512], AF.Square)),
                    reads=xnames(j * 512, (j + 1) * 512), writes=[sqn])
                sc.op("t", (lambda t, sq=sq, sel=sel, k=k: t.matmul(
                    ps[1][0:8, 0:512], sel, sq[:],
                    start=(k == 0), stop=(k == n - 1))),
                    reads=[sqn, "const"], writes=["psb1"])
                k += 1
        m, m2, v_ = tmp
        sc.op("v", lambda v: v.tensor_scalar(
            m[0:8, :], ps[0][0:8, 0:512], INVD, None, A.mult),
            reads=["psb0"], writes=["qtarea"])
        sc.op("v", lambda v: v.tensor_tensor(
            m2[0:8, :], m[0:8, :], m[0:8, :], A.mult),
            writes=["qtarea"])
        sc.op("v", lambda v: v.scalar_tensor_tensor(
            v_[0:8, :], ps[1][0:8, 0:512], INVD, m2[0:8, :],
            A.mult, A.subtract),
            reads=["psb1"], writes=["qtarea"])
        sc.op("s", lambda e: e.activation(
            m2[0:8, :], v_[0:8, :], AF.Ln, bias=EPS_LN),
            writes=["qtarea"])
        sc.op("s", lambda e: e.activation(
            v_[0:8, :], m2[0:8, :], AF.Exp, scale=-0.5),
            writes=["qtarea"])
        osf = float(out_scale)
        sc.op("v", lambda v: v.tensor_scalar(
            stats[0:8, 0:512], v_[0:8, :], osf, None, A.mult),
            reads=["qtarea"], writes=["stats"])
        sc.op("v", lambda v: v.scalar_tensor_tensor(
            stats[0:8, 512:1024], m[0:8, :], -osf, v_[0:8, :],
            A.mult, A.mult),
            reads=["qtarea"], writes=["stats"])
        sc.fence()

    RGROUPS = [[2 * b, 2 * b + 1] for b in range(n_cores // 2)]

    def emit_attn_a():
        NCH = T // CHUNK
        ar = Arena()
        ctxacc = ar.f32(17408, 768, "ctxacc")
        ksacc = ar.f32(18176, 12, "ksacc")
        emit_ln_stats(1.0)

        xn = ar.bf16(0, 4096, "xn")
        xnv = xn.rearrange("p (c t) -> p c t", c=NB)
        aqt = ar.f32(2048, 2048, "aqt")
        pmask = cview("chind2")

        def ln_apply_tile(tk):
            rb = rowb[tk % 2]
            rbn = f"rb{tk % 2}"
            dma("y", rb, stats[tk:tk + 1, 0:1024], reads=["stats"], writes=[rbn])
            obase = 32 * (1 + tk % 2)
            orow = ones[obase:obase + 1, 0:128]
            sc.op("t", (lambda t, rb=rb, orow=orow: t.matmul(
                ps[0][0:128, 0:512], orow, rb[0:1, 0:512],
                start=True, stop=True)), reads=[rbn, "const"], writes=["psb0"])
            sc.op("t", (lambda t, rb=rb, orow=orow: t.matmul(
                ps[1][0:128, 0:512], orow, rb[0:1, 512:1024],
                start=True, stop=True)), reads=[rbn, "const"], writes=["psb1"])
            t0 = tk * 512
            for th in range(2):
                xsrc = xv[:, :, t0 + th * 256:t0 + th * 256 + 256]
                sc.op("v", (lambda v, xsrc=xsrc, th=th: v.scalar_tensor_tensor(
                    aqt[:, 0:2048], xsrc, 1.0,
                    ps[0][:, th * 256:(th + 1) * 256].unsqueeze(1)
                    .broadcast_to([128, 8, 256]),
                    A.mult, A.mult)),
                    reads=xnames(t0, t0 + 512) + ["psb0"], writes=["aqt"])
                dst = xnv[:, :, th * 256:(th + 1) * 256]
                sc.op("v", (lambda v, th=th, dst=dst: v.scalar_tensor_tensor(
                    dst, aqt[:, 0:2048], 1.0,
                    ps[1][:, th * 256:(th + 1) * 256].unsqueeze(1)
                    .broadcast_to([128, 8, 256]),
                    A.mult, A.add)),
                    reads=["aqt", "psb1"], writes=["xn"])

        wkvg = ar.bf16(4096, 12288, "wkvg")
        wkvv = wkvg.rearrange("p (w c o) -> p w c o", w=2, c=NB)
        ccst = arena[:, 4096:4888]
        kT = ar.bf16(10240, 3072, "kT")
        kTv = kT.rearrange("p (s o) -> p s o", s=4)
        vT = ar.bf16(11776, 3072, "vT")
        vTv = vT.rearrange("p (s o) -> p s o", s=4)
        elu1 = ar.f32(13312, 768, "elu1")
        elu2 = ar.f32(14080, 768, "elu2")
        kmask = arena[:, 13312:13696].bitcast(BF16)
        ctxo = [ar.bf16(14848, 784, "ctxo0"), ar.bf16(15240, 784, "ctxo1")]

        sc.op("v", lambda v: v.memset(ctxacc[0:64, 0:768], 0.0),
              writes=["ctxacc"])
        sc.op("v", lambda v: v.memset(ksacc[0:64, 0:12], 0.0), writes=["ksacc"])
        sc.op("v", lambda v: v.memset(ctxo[0][0:64, 780:784], 0.0),
              writes=["ctxo0"])
        sc.op("v", lambda v: v.memset(ctxo[1][0:64, 780:784], 0.0),
              writes=["ctxo1"])
        if aa_stop == "none":
            sc.fence()
            return
        for tk in range(NT512 if aa_stop != "prep" else 0):
            ln_apply_tile(tk)
            if aa_stop == "ln":
                continue
            for w, nm in ((0, "aWk"), (1, "aWv")):
                for ob in range(6):
                    dma("y", wkvv[:, w, :, ob * 128:(ob + 1) * 128],
                        scr[nm].ap()[2 + ob].transpose([1, 0, 2]),
                        writes=["wkvg"])
            for t128 in range(4):
                for w, (bankA, bankB) in ((0, (2, 3)), (1, (4, 5))):
                    def mmkv(t, w=w, t128=t128, bankA=bankA, bankB=bankB):
                        last = None
                        for sp, bank, o0, o1 in ((0, bankA, 0, 512),
                                                 (1, bankB, 512, 768)):
                            for c in range(NB):
                                lhsT = xnv[:, c, t128 * 128:(t128 + 1) * 128]
                                last = t.matmul(
                                    ps[bank][:, 0:o1 - o0], lhsT,
                                    wkvv[:, w, c, o0:o1],
                                    start=(c == 0), stop=(c == NB - 1))
                        return last
                    sc.op("t", mmkv, reads=["xn", "wkvg"],
                          writes=[f"psb{bankA}", f"psb{bankB}"])
                for sp, bank, o0, o1 in ((0, 2, 0, 512), (1, 3, 512, 768)):
                    n = o1 - o0
                    sc.op("v", (lambda v, bank=bank, o0=o0, n=n: v.tensor_scalar(
                        elu1[:, o0:o0 + n], ps[bank][:, 0:n], 0.0, None, A.min)),
                        reads=[f"psb{bank}"], writes=["elu1"])
                    sc.op("s", (lambda e, o0=o0, n=n: e.activation(
                        elu2[:, o0:o0 + n], elu1[:, o0:o0 + n], AF.Exp)),
                        reads=["elu1"], writes=["elu2"])
                    sc.op("v", (lambda v, bank=bank, o0=o0, n=n, t128=t128:
                                v.scalar_tensor_tensor(
                                    kTv[:, t128, o0:o0 + n], ps[bank][:, 0:n],
                                    0.0, elu2[:, o0:o0 + n], A.max, A.add)),
                          reads=[f"psb{bank}", "elu2"], writes=["kT"])
                    eng = "v" if sp == 0 else "s"
                    sc.op(eng, (lambda e, bank=bank, o0=o0, n=n, t128=t128,
                                sv=(sp == 1): (e.copy if sv else e.tensor_copy)(
                                    vTv[:, t128, o0:o0 + n],
                                    ps[bank + 2][:, 0:n])),
                          reads=[f"psb{bank + 2}"], writes=["vT"])
                for cpar in range(2 if aa_stop != "kv" else 0):
                    cu = tk * 8 + t128 * 2 + cpar
                    sc.op("v", (lambda v, t128=t128, cpar=cpar: v.tensor_scalar(
                        kmask[:, 0:768], kTv[:, t128, :],
                        pmask[:, cpar:cpar + 1], None, A.mult)),
                        reads=["kT", "const"], writes=["elu1"])
                    def mmctx(t, t128=t128):
                        last = None
                        for gh in range(GLOBAL_HEADS):
                            bank, col = (6, gh * 64) if gh < 8 else \
                                        (7, (gh - 8) * 64)
                            last = t.matmul(
                                ps[bank][0:64, col:col + 64],
                                kmask[:, gh * 64:gh * 64 + 64],
                                vTv[:, t128, gh * 64:gh * 64 + 64],
                                start=True, stop=True)
                        for gh in range(GLOBAL_HEADS):
                            last = t.matmul(
                                ps[7][0:64, 384 + gh:385 + gh],
                                kmask[:, gh * 64:gh * 64 + 64],
                                onesb[:, 0:1],
                                start=True, stop=True)
                        return last
                    sc.op("t", mmctx, reads=["elu1", "vT", "const"],
                          writes=["psb6", "psb7"])
                    if aa_stop == "ctx1":
                        continue
                    cb = ctxo[cu % 2]
                    cbn = f"ctxo{cu % 2}"
                    sc.op("v", (lambda v, cb=cb: v.tensor_copy(
                        cb[0:64, 0:512], ps[6][0:64, 0:512])),
                        reads=["psb6"], writes=[cbn])
                    sc.op("s", (lambda e, cb=cb: e.copy(
                        cb[0:64, 512:768], ps[7][0:64, 0:256])),
                        reads=["psb7"], writes=[cbn])
                    sc.op("s", (lambda e, cb=cb: e.copy(
                        cb[0:64, 768:780], ps[7][0:64, 384:396])),
                        reads=["psb7"], writes=[cbn])
                    sc.op("v", lambda v: v.tensor_tensor(
                        ctxacc[0:64, 0:512], ctxacc[0:64, 0:512],
                        ps[6][0:64, 0:512], A.add),
                        reads=["psb6"], writes=["ctxacc"])
                    sc.op("v", lambda v: v.tensor_tensor(
                        ctxacc[0:64, 512:768], ctxacc[0:64, 512:768],
                        ps[7][0:64, 0:256], A.add),
                        reads=["psb7"], writes=["ctxacc"])
                    sc.op("v", lambda v: v.tensor_tensor(
                        ksacc[0:64, 0:12], ksacc[0:64, 0:12],
                        ps[7][0:64, 384:396], A.add),
                        reads=["psb7"], writes=["ksacc"])
                    if aa_stop != "ctx2":
                        dma("y", scr["ctx"].ap()[cu], cb[0:64, 0:784],
                            reads=[cbn], writes=["scrctx"])

        if aa_stop in ("prep", "1a", "ln", "kv", "ctx1", "ctx2"):
            sc.fence()
            return
        flagC = cview("flags")[0:64, 0:1]
        flagU = cview("flags")[0:64, 1:2]
        sc.op("v", lambda v: v.memset(ccst[0:64, 780:792], 0.0),
              writes=["wkvg"])
        sc.op("v", lambda v: v.tensor_scalar(
            ccst[0:64, 0:768], ctxacc[0:64, 0:768], flagC, None, A.mult),
            reads=["ctxacc", "const"], writes=["wkvg"])
        sc.op("v", lambda v: v.tensor_scalar(
            ccst[0:64, 768:780], ksacc[0:64, 0:12], flagC, None, A.mult),
            reads=["ksacc", "const"], writes=["wkvg"])
        dma("y", cc1_in.ap(), ccst[0:64, 0:792], reads=["wkvg"], writes=["cc1i"])
        if no_coll:
            dma("y", cc1_out.ap(), cc1_in.ap(), reads=["cc1i"], writes=["cc1o"])
        else:
            sc.op("g", lambda g: g.collective_compute(
                "AllReduce", A.add, replica_groups=RGROUPS,
                ins=[cc1_in.ap().opt()], outs=[cc1_out.ap().opt()]),
                reads=["cc1i"], writes=["cc1o"])
        dma("y", ccst[0:64, 0:792], cc1_out.ap(), reads=["cc1o"], writes=["wkvg"])
        sc.op("v", lambda v: v.tensor_scalar(
            ctxacc[0:64, 0:768], ccst[0:64, 0:768], flagU, None, A.mult),
            reads=["wkvg", "const"], writes=["ctxacc"])
        sc.op("v", lambda v: v.tensor_scalar(
            ksacc[0:64, 0:12], ccst[0:64, 768:780], flagU, None, A.mult),
            reads=["wkvg", "const"], writes=["ksacc"])
        sc.fence()
        if aa_stop == "coll":
            return

        ar2 = Arena()
        ar2.f32(17408, 768, "ctxacc")
        ar2.f32(18176, 12, "ksacc")
        xn2 = ar2.bf16(0, 4096, "xn")
        xnv2 = xn2.rearrange("p (c t) -> p c t", c=NB)
        attnb = ar2.bf16(0, 4096, "xn")
        atv = attnb.rearrange("p (c t) -> p c t", c=NB)
        ar2.f32(2048, 2048, "aqt")
        qbf = ar2.bf16(4096, 4096, "qbf")
        qbv = qbf.rearrange("p (c t) -> p c t", c=NB)
        qe = ar2.f32(6144, 1536, "qearea")
        qev = qe.rearrange("p (c t) -> p c t", c=6)
        qsmR = arena[:, 6144:9216].bitcast(BF16)
        qsRv = qsmR.rearrange("p (h t) -> p h t", h=12)
        wob = [ar2.bf16(6144, 1024, "qearea"), ar2.bf16(6656, 1024, "qearea")]
        wovs = [w.rearrange("p (c o) -> p c o", c=NB) for w in wob]
        qsm = ar2.bf16(7680, 3072, "qsm")
        qsv = qsm.rearrange("p (c t) -> p c t", c=6)
        gstrip = arena[:, 2432:3968].bitcast(BF16)
        gsv = gstrip.rearrange("p (h t) -> p h t", h=12)
        qrec = ar2.bf16(9216, 1024, "qrec")
        kloc = [ar2.bf16(9728, 1024, "kloc0"), ar2.bf16(10240, 1024, "kloc1")]
        klv = [k.rearrange("p (c t) -> p c t", c=2) for k in kloc]
        vloc = [ar2.bf16(10752, 1024, "vloc0"), ar2.bf16(11264, 1024, "vloc1")]
        vlv = [v.rearrange("p (c t) -> p c t", c=4) for v in vloc]
        hkv = ar2.bf16(11776, 512, "halokv")
        hkvv = hkv.rearrange("p (c t) -> p c t", c=4)
        wklb = ar2.bf16(12032, 2048, "wklarea")
        wklv = wklb.rearrange("p (o c t) -> p o c t", o=2, c=NB)
        scp = ar2.bf16(12032, 512, "wklarea")
        scpv = scp.rearrange("p (h t) -> p h t", h=4)
        scs = ar2.bf16(12288, 512, "wklarea")
        scsv = scs.rearrange("p (h t) -> p h t", h=4)
        recrow = ar2.f32(12544, 512, "wklarea")
        wvlb = ar2.bf16(13056, 2048, "wvlarea")
        wvlv = wvlb.rearrange("p (c t) -> p c t", c=NB)
        ctxrb = [ar2.bf16(13056, 784, "wvlarea"), ar2.bf16(13448, 784, "wvlarea")]
        wqb = [ar2.bf16(14080, 1024, "wqb0"), ar2.bf16(14592, 1024, "wqb1")]
        wqvs = [w.rearrange("p (c o) -> p c o", c=NB) for w in wqb]
        hfeat = ar2.f32(15104, 1024, "halof")
        hfv = hfeat.rearrange("p (c t) -> p c t", c=NB)
        rtile = ar2.f32(15104, 512, "halof")
        qm = [ar2.bf16(15616, 256, "halof"), ar2.bf16(15744, 256, "halof")]
        qmv = [q.rearrange("p (c t) -> p c t", c=2) for q in qm]
        ltile = ar2.bf16(15872, 512, "halof")
        hst = ar2.f32(16128, 256, "halost")
        hxn = ar2.bf16(3072, 1024, "aqt")
        hxv = hxn.rearrange("p (c t) -> p c t", c=NB)
        ctxs_bf = arena[:, 2048:2432].bitcast(BF16)

        for tk in range(NT512):
            ln_apply_tile(tk)
            for ob in range(2):
                dma("y", wklv[:, ob], scr["aWk"].ap()[ob].transpose([1, 0, 2]),
                    writes=["wklarea"])
                dma("y", wvlv[:, :, ob * 128:(ob + 1) * 128],
                    scr["aWv"].ap()[ob].transpose([1, 0, 2]),
                    writes=["wvlarea"])
            for o in range(NB):
                wvv = wqvs[o % 2]
                wn = f"wqb{o % 2}"
                dma("y", wvv, scr["aWq"].ap()[o].transpose([1, 0, 2]),
                    writes=[wn])
                bank = 2 + o % 2
                sc.op("t", (lambda t, wvv=wvv, bank=bank: [
                    t.matmul(ps[bank][:, 0:512], wvv[:, c, :], xnv2[:, c, :],
                             start=(c == 0), stop=(c == NB - 1))
                    for c in range(NB)][-1]),
                    reads=[wn, "xn"], writes=[f"psb{bank}"])
                eng = "v" if o % 2 == 0 else "s"
                sc.op(eng, (lambda e, bank=bank, o=o, sv=(o % 2 == 1):
                            (e.copy if sv else e.tensor_copy)(
                                qbv[:, o, :], ps[bank][:, 0:512])),
                      reads=[f"psb{bank}"], writes=["qbf"])
            for ob in range(2):
                bank = 2 + ob
                sc.op("t", (lambda t, ob=ob, bank=bank: [
                    t.matmul(ps[bank][:, 0:512], wklv[:, ob, c, :],
                             xnv2[:, c, :], start=(c == 0), stop=(c == NB - 1))
                    for c in range(NB)][-1]),
                    reads=["wklarea", "xn"], writes=[f"psb{bank}"])
                sc.op("v", (lambda v, bank=bank, ob=ob, tk=tk: v.tensor_copy(
                    klv[tk % 2][:, ob, :], ps[bank][:, 0:512])),
                    reads=[f"psb{bank}"], writes=[f"kloc{tk % 2}"])
            for t128 in range(4):
                bank = 2 + t128 % 2
                sc.op("t", (lambda t, t128=t128, bank=bank: [
                    t.matmul(ps[bank][:, 0:256],
                             xnv2[:, c, t128 * 128:(t128 + 1) * 128],
                             wvlv[:, c, 0:256],
                             start=(c == 0), stop=(c == NB - 1))
                    for c in range(NB)][-1]),
                    reads=["wvlarea", "xn"], writes=[f"psb{bank}"])
                sc.op("v", (lambda v, bank=bank, t128=t128, tk=tk: v.tensor_copy(
                    vlv[tk % 2][:, t128, :], ps[bank][:, 0:256])),
                    reads=[f"psb{bank}"], writes=[f"vloc{tk % 2}"])
            if tk == 0:
                dma("y", hfeat[:, 0:1024].rearrange("p (a b) -> p a b", a=2),
                    halo_in.rearrange("(a t) d -> a t d", a=2), slow=True,
                    writes=["halof"])
                sc.op("t", (lambda t: [
                    t.transpose(ps[4][:, c * 128:(c + 1) * 128],
                                hfeat[:, c * 128:(c + 1) * 128], ident)
                    for c in range(4)][-1]),
                    reads=["halof", "const"], writes=["psb4"])
                sc.op("t", (lambda t: [
                    t.transpose(ps[5][:, (c - 4) * 128:(c - 3) * 128],
                                hfeat[:, c * 128:(c + 1) * 128], ident)
                    for c in range(4, 8)][-1]),
                    reads=["halof", "const"], writes=["psb5"])
                sc.op("v", lambda v: v.tensor_copy(hfeat[:, 0:512],
                                                   ps[4][:, 0:512]),
                      reads=["psb4"], writes=["halof"])
                sc.op("v", lambda v: v.tensor_copy(hfeat[:, 512:1024],
                                                   ps[5][:, 0:512]),
                      reads=["psb5"], writes=["halof"])
                sc.op("t", (lambda t: [
                    t.matmul(ps[4][0:1, 0:128], ones[:, 0:1],
                             hfv[:, c, :], start=(c == 0), stop=(c == NB - 1))
                    for c in range(NB)][-1]),
                    reads=["halof", "const"], writes=["psb4"])
                sc.op("s", (lambda e: [
                    e.activation(aqt[:, c * 128:(c + 1) * 128], hfv[:, c, :],
                                 AF.Square) for c in range(NB)][-1]),
                    reads=["halof"], writes=["aqt"])
                sc.op("t", (lambda t: [
                    t.matmul(ps[5][0:1, 0:128], ones[:, 0:1],
                             aqt[:, c * 128:(c + 1) * 128],
                             start=(c == 0), stop=(c == NB - 1))
                    for c in range(NB)][-1]),
                    reads=["aqt", "const"], writes=["psb5"])
                sc.op("v", lambda v: v.tensor_scalar(
                    hst[0:1, 0:128], ps[4][0:1, 0:128], INVD, None, A.mult),
                    reads=["psb4"], writes=["halost"])
                sc.op("v", lambda v: v.tensor_tensor(
                    hst[0:1, 128:256], hst[0:1, 0:128], hst[0:1, 0:128],
                    A.mult), reads=["halost"], writes=["halost"])
                sc.op("v", lambda v: v.scalar_tensor_tensor(
                    hst[0:1, 128:256], ps[5][0:1, 0:128], INVD,
                    hst[0:1, 128:256], A.mult, A.subtract),
                    reads=["psb5", "halost"], writes=["halost"])
                sc.op("s", lambda e: e.activation(
                    hst[0:1, 128:256], hst[0:1, 128:256], AF.Ln, bias=EPS_LN),
                    reads=["halost"], writes=["halost"])
                sc.op("s", lambda e: e.activation(
                    hst[0:1, 128:256], hst[0:1, 128:256], AF.Exp, scale=-0.5),
                    reads=["halost"], writes=["halost"])
                sc.op("v", lambda v: v.scalar_tensor_tensor(
                    hst[0:1, 0:128], hst[0:1, 0:128], -1.0,
                    hst[0:1, 128:256], A.mult, A.mult),
                    reads=["halost"], writes=["halost"])
                sc.op("t", lambda t: t.matmul(
                    ps[4][0:128, 0:128], ones[0:1, 0:128], hst[0:1, 128:256],
                    start=True, stop=True),
                    reads=["halost", "const"], writes=["psb4"])
                sc.op("t", lambda t: t.matmul(
                    ps[5][0:128, 0:128], ones[0:1, 0:128], hst[0:1, 0:128],
                    start=True, stop=True),
                    reads=["halost", "const"], writes=["psb5"])
                sc.op("v", lambda v: v.scalar_tensor_tensor(
                    aqt[:, 0:1024], hfeat[:, 0:1024], 1.0,
                    ps[4][:, 0:128].unsqueeze(1).broadcast_to([128, 8, 128]),
                    A.mult, A.mult), reads=["halof", "psb4"], writes=["aqt"])
                sc.op("v", lambda v: v.scalar_tensor_tensor(
                    hxn[:, 0:1024], aqt[:, 0:1024], 1.0,
                    ps[5][:, 0:128].unsqueeze(1).broadcast_to([128, 8, 128]),
                    A.mult, A.add), reads=["aqt", "psb5"], writes=["aqt"])
                for ob in range(2):
                    bank = 4 + ob
                    sc.op("t", (lambda t, ob=ob, bank=bank: [
                        t.matmul(ps[bank][:, 0:128], wklv[:, ob, c, :],
                                 hxv[:, c, :], start=(c == 0),
                                 stop=(c == NB - 1)) for c in range(NB)][-1]),
                        reads=["wklarea", "aqt"], writes=[f"psb{bank}"])
                    sc.op("v", (lambda v, bank=bank, ob=ob: v.tensor_copy(
                        hkvv[:, ob, :], ps[bank][:, 0:128])),
                        reads=[f"psb{bank}"], writes=["halokv"])
                sc.op("t", (lambda t: [
                    t.matmul(ps[4][:, 0:256], hxv[:, c, :], wvlv[:, c, 0:256],
                             start=(c == 0), stop=(c == NB - 1))
                    for c in range(NB)][-1]),
                    reads=["wvlarea", "aqt"], writes=["psb4"])
                sc.op("v", (lambda v: v.tensor_copy(
                    hkv[:, 256:512], ps[4][:, 0:256])),
                    reads=["psb4"], writes=["halokv"])
            for th in range(2):
                c0 = th * 256
                sc.op("s", (lambda e, c0=c0: e.activation(
                    qe[:, 0:1536], qbv[:, 2:8, c0:c0 + 256], AF.Exp)),
                    reads=["qbf"], writes=["qearea"])
                sc.op("t", (lambda t, c0=c0, th=th: [
                    t.matmul(ps[4][0:12, th * 256:th * 256 + 256],
                             const_sb[:, CR.off["hsum12"] + 12 * cb:
                                      CR.off["hsum12"] + 12 * cb + 12],
                             qev[:, cb, :], start=(cb == 0), stop=(cb == 5))
                    for cb in range(6)][-1]),
                    reads=["qearea", "const"], writes=["psb4"])
                def _recip(v, th=th):
                    with nc.allow_low_precision("bf16 softmax recip"):
                        return v.reciprocal(
                            qrec[0:12, th * 256:th * 256 + 256],
                            ps[4][0:12, th * 256:th * 256 + 256])
                sc.op("v", _recip, reads=["psb4"], writes=["qrec"])
                for cb in range(6):
                    sc.op("t", (lambda t, cb=cb, th=th: t.matmul(
                        ps[5][:, 0:256],
                        const_sb[0:12, CR.off["hexp12"]:CR.off["hexp12"] + 384]
                        .bitcast(BF16)[:, 128 * cb:128 * cb + 128],
                        qrec[0:12, th * 256:th * 256 + 256],
                        start=True, stop=True)),
                        reads=["qrec", "const"], writes=["psb5"])
                    sc.op("v", (lambda v, cb=cb, th=th, c0=c0: v.tensor_tensor(
                        qsv[:, cb, c0:c0 + 256], qev[:, cb, :],
                        ps[5][:, 0:256], A.mult)),
                        reads=["qearea", "psb5"], writes=["qsm"])
            qsR4 = qsmR.rearrange("p (h q t) -> p h q t", h=6, q=2)
            dma("y", qsR4[0:64, :, 0, :], qsm[0:64, 0:3072]
                .rearrange("p (c t) -> p c t", c=6), reads=["qsm"],
                writes=["qearea", "qsm"])
            dma("y", qsR4[0:64, :, 1, :], qsm[64:128, 0:3072]
                .rearrange("p (c t) -> p c t", c=6), reads=["qsm"],
                writes=["qearea", "qsm"])
            for b in range(4):
                first = (tk == 0 and b == 0)
                if b == 0:
                    if tk == 0:
                        kprev = hkvv[:, 0:2, :]
                        vprev = hkv[:, 256:512]
                        pn = ["halokv"]
                    else:
                        kprev = klv[(tk - 1) % 2][:, :, 384:512]
                        vprev = vloc[(tk - 1) % 2][:, 3 * 256:4 * 256]
                        pn = [f"kloc{(tk - 1) % 2}", f"vloc{(tk - 1) % 2}"]
                else:
                    kprev = klv[tk % 2][:, :, (b - 1) * 128:b * 128]
                    vprev = vloc[tk % 2][:, (b - 1) * 256:b * 256]
                    pn = [f"kloc{tk % 2}", f"vloc{tk % 2}"]
                kself = klv[tk % 2][:, :, b * 128:(b + 1) * 128]
                vself = vloc[tk % 2][:, b * 256:(b + 1) * 256]
                sn = [f"kloc{tk % 2}", f"vloc{tk % 2}"]
                for par in range(2):
                    sc.op("v", (lambda v, par=par, b=b: v.tensor_scalar(
                        qmv[par][:, :, :], qbv[:, 0:2, b * 128:(b + 1) * 128],
                        pmask[:, par:par + 1], None, A.mult)),
                        reads=["qbf", "const"], writes=["halof"])
                def mmdots(t, kprev=kprev, kself=kself, b=b):
                    last = None
                    for h in range(4):
                        par, ob = h % 2, h // 2
                        if kprev.ndim == 3:
                            kp = kprev[:, ob, :]
                        else:
                            kp = kprev.rearrange("p (o t) -> p o t", o=2)[:, ob, :]
                        last = t.matmul(
                            ps[2][:, h * 128:(h + 1) * 128], kp,
                            qmv[par][:, ob, :], start=True, stop=True)
                        last = t.matmul(
                            ps[3][:, h * 128:(h + 1) * 128],
                            kself[:, ob, :],
                            qmv[par][:, ob, :], start=True, stop=True)
                    return last
                sc.op("t", mmdots, reads=pn + sn + ["halof"],
                      writes=["psb2", "psb3"])
                if first:
                    mprev = cview("mask_prev0").unsqueeze(1).broadcast_to(
                        [128, 4, 128])
                    sc.op("v", (lambda v, mprev=mprev: v.scalar_tensor_tensor(
                        ps[2][:, 0:512].rearrange("p (h t) -> p h t", h=4),
                        ps[2][:, 0:512].rearrange("p (h t) -> p h t", h=4),
                        0.125, mprev, A.mult, A.add)),
                        reads=["psb2", "const"], writes=["psb2"])
                else:
                    sc.op("v", lambda v: v.tensor_scalar(
                        ps[2][:, 0:512], ps[2][:, 0:512], 0.125, None, A.mult),
                        reads=["psb2"], writes=["psb2"])
                msf = cview("mask_self").unsqueeze(1).broadcast_to([128, 4, 128])
                sc.op("v", (lambda v, msf=msf: v.scalar_tensor_tensor(
                    ps[3][:, 0:512].rearrange("p (h t) -> p h t", h=4),
                    ps[3][:, 0:512].rearrange("p (h t) -> p h t", h=4),
                    0.125, msf, A.mult, A.add)),
                    reads=["psb3", "const"], writes=["psb3"])
                sc.op("s", lambda e: e.activation(
                    scp[:, 0:512], ps[2][:, 0:512], AF.Exp),
                    reads=["psb2"], writes=["wklarea"])
                sc.op("s", lambda e: e.activation(
                    scs[:, 0:512], ps[3][:, 0:512], AF.Exp),
                    reads=["psb3"], writes=["wklarea"])
                sc.op("t", lambda t: [
                    t.matmul(ps[5][0:1, 0:512], onesb[:, 0:1], scp[:, 0:512],
                             start=True, stop=False),
                    t.matmul(ps[5][0:1, 0:512], onesb[:, 0:1], scs[:, 0:512],
                             start=False, stop=True)][-1],
                    reads=["wklarea", "const"], writes=["psb5"])
                sc.op("v", lambda v: v.reciprocal(
                    recrow[0:1, 0:512], ps[5][0:1, 0:512]),
                    reads=["psb5"], writes=["wklarea"])
                def mmav(t, vprev=vprev, vself=vself):
                    last = None
                    for h in range(4):
                        out = ps[4][0:64, h * 128:(h + 1) * 128]
                        last = t.matmul(out, vprev[:, h * 64:(h + 1) * 64],
                                        scpv[:, h, :], start=True, stop=False)
                        last = t.matmul(out, vself[:, h * 64:(h + 1) * 64],
                                        scsv[:, h, :], start=False, stop=True)
                    return last
                sc.op("t", mmav, reads=pn + sn + ["wklarea"], writes=["psb4"])
                def mmrec(t):
                    last = None
                    for h in range(4):
                        last = t.matmul(
                            ps[5][0:64, h * 128:(h + 1) * 128],
                            ones[0:1, 0:64],
                            recrow[0:1, h * 128:(h + 1) * 128],
                            start=True, stop=True)
                    return last
                sc.op("t", mmrec, reads=["wklarea", "const"], writes=["psb5"])
                sc.op("s", lambda e: e.copy(rtile[0:64, 0:512],
                                            ps[5][0:64, 0:512]),
                      reads=["psb5"], writes=["halof"])
                sc.op("v", (lambda v: v.tensor_tensor(
                    ltile[0:64, 0:512], ps[4][0:64, 0:512],
                    rtile[0:64, 0:512], A.mult)),
                    reads=["psb4", "halof"], writes=["halof"])
                lt4 = ltile[0:64, 0:512].rearrange("p (h q t) -> p h q t",
                                                    h=2, q=2)
                dma("y", atv[0:64, 0:2, b * 128:(b + 1) * 128],
                    lt4[:, :, 0, :], reads=["halof"], writes=["xn"])
                dma("y", atv[64:128, 0:2, b * 128:(b + 1) * 128],
                    lt4[:, :, 1, :], reads=["halof"], writes=["xn"])
            for cl in range(8):
                cu = tk * 8 + cl
                crb = ctxrb[cu % 2]
                crn = "wvlarea"
                dma("y", crb[0:64, 0:784], scr["ctx"].ap()[cu],
                    reads=["scrctx"], writes=[crn])
                sc.op("v", lambda v: v.tensor_scalar(
                    arena[0:64, 18188:18200], ksacc[0:64, 0:12], EPS_CTX,
                    None, A.add), reads=["ksacc"], writes=["kstmp"])
                sc.op("v", lambda v: v.reciprocal(
                    arena[0:64, 18188:18200], arena[0:64, 18188:18200]),
                    reads=["kstmp"], writes=["kstmp"])
                sc.op("v", lambda v: v.tensor_tensor(
                    ctxs_bf[0:64, 0:768].rearrange("p (h e) -> p h e", h=12),
                    ctxacc[0:64, 0:768].rearrange("p (h e) -> p h e", h=12),
                    arena[0:64, 18188:18200].unsqueeze(2)
                    .broadcast_to([64, 12, 64]), A.mult),
                    reads=["ctxacc", "kstmp"], writes=["aqt"])
                def mmglob(t, cl=cl):
                    last = None
                    for gh in range(GLOBAL_HEADS):
                        bank, col = (4, gh * 64) if gh < 8 else \
                                    (5, (gh - 8) * 64)
                        last = t.matmul(
                            ps[bank][0:64, col:col + 64],
                            ctxs_bf[0:64, gh * 64:gh * 64 + 64],
                            qsRv[0:64, gh, cl * 64:(cl + 1) * 64],
                            start=True, stop=True)
                    return last
                sc.op("t", mmglob, reads=["aqt", "qearea", "qsm"],
                      writes=["psb4", "psb5"])
                ch = cl % 4
                sc.op("v", (lambda v, ch=ch: v.tensor_copy(
                    gsv[0:64, 0:8, ch * 64:(ch + 1) * 64],
                    ps[4][0:64, 0:512].rearrange("p (h e) -> p h e", h=8))),
                    reads=["psb4"], writes=["aqt"])
                sc.op("s", (lambda e, ch=ch: e.copy(
                    gsv[0:64, 8:12, ch * 64:(ch + 1) * 64],
                    ps[5][0:64, 0:256].rearrange("p (h e) -> p h e", h=4))),
                    reads=["psb5"], writes=["aqt"])
                sc.op("v", (lambda v, crb=crb: v.tensor_tensor(
                    ksacc[0:64, 0:12], ksacc[0:64, 0:12],
                    crb[0:64, 768:780], A.add)),
                    reads=[crn], writes=["ksacc"])
                sc.op("v", (lambda v, crb=crb: v.tensor_tensor(
                    ctxacc[0:64, 0:768], ctxacc[0:64, 0:768],
                    crb[0:64, 0:768], A.add)),
                    reads=[crn], writes=["ctxacc"])
                if cl % 4 == 3:
                    th0 = (cl // 4) * 256
                    gs4 = gstrip.rearrange("p (h q t) -> p h q t", h=6, q=2)
                    dma("y", atv[0:64, 2:8, th0:th0 + 256], gs4[0:64, :, 0, :],
                        reads=["aqt"], writes=["xn"])
                    dma("y", atv[64:128, 2:8, th0:th0 + 256], gs4[0:64, :, 1, :],
                        reads=["aqt"], writes=["xn"])
            t0 = tk * 512
            for o in range(NB):
                wb = wovs[o % 2]
                wn = "qearea"
                dma("y", wb, scr["aWo"].ap()[o].transpose([1, 0, 2]),
                    writes=[wn])
                bank = 2 + o % 2
                sc.op("t", (lambda t, wb=wb, bank=bank: [
                    t.matmul(ps[bank][:, 0:512], wb[:, c, :], atv[:, c, :],
                             start=(c == 0), stop=(c == NB - 1))
                    for c in range(NB)][-1]),
                    reads=[wn, "xn"], writes=[f"psb{bank}"])
                sc.op("v", (lambda v, bank=bank, o=o, t0=t0:
                            v.scalar_tensor_tensor(
                                xv[:, o, t0:t0 + 512], ps[bank][:, 0:512],
                                abo_sb[:, o:o + 1], xv[:, o, t0:t0 + 512],
                                A.add, A.add)),
                      reads=[f"psb{bank}", "abo"] + xnames(t0, t0 + 512),
                      writes=xnames(t0, t0 + 512))
        sc.fence()

    def emit_attn_b():
        arb = Arena()
        zc = arb.f32(0, 2048, "bz")
        zv = zc.rearrange("p (s f) -> p s f", s=2)
        ngT = zv
        ccs2 = arena[:, 0:1024]
        spc = arb.f32(2048, 2048, "bsp")
        spv = spc.rearrange("p (s f) -> p s f", s=2)
        aqt3 = spc
        xn3 = arb.bf16(4096, 2048, "xn")
        xnv3 = xn3.rearrange("p (c t) -> p c t", c=NB)
        bq = arb.bf16(5120, 2048, "bq")
        bqv = bq.rearrange("p (c t) -> p c t", c=NB)
        bkt = arb.bf16(6144, 2048, "bkt")
        bktv = bkt.rearrange("p (s f) -> p s f", s=2)
        bent = arb.bf16(7168, 2048, "bent")
        bentv = bent.rearrange("p (s f) -> p s f", s=2)
        bepv = bent.rearrange("p (c t) -> p c t", c=NB)
        bkf = arb.bf16(8192, 2048, "bkf")
        bkfv = bkf.rearrange("p (c t) -> p c t", c=NB)
        bvt = arb.bf16(9216, 2048, "bvt")
        bvtv = bvt.rearrange("p (s d) -> p s d", s=2)
        bog = arb.bf16(10240, 4096, "bog")
        bogv = bog.rearrange("p (c t) -> p c t", c=NB)
        ngFs = zc.rearrange("p (s c t) -> p s c t", s=2, c=NB)
        ngtots = arena[:, 16048:16080]
        ngtv = ngtots.rearrange("p (c u) -> p c u", c=NB)
        bqps = [arb.bf16(12288, 512, "bqps0"), arb.bf16(12544, 512, "bqps1")]
        bsA = arb.bf16(12800, 1024, "bsA")
        bw = [arb.bf16(13312, 2048, "bw0"), arb.bf16(14336, 2048, "bw1")]
        bwv = [w.rearrange("p (c o) -> p c o", c=NB) for w in bw]
        bsq = arb.f32(15360, 512, "bsq")
        rmsrow = arena[32:33, 15360:15872]
        ebuf = arena[0:2, 15360:15872]
        bD = arb.f32(15872, 32, "bD")
        bDv = bD.rearrange("p (c u) -> p c u", c=NB)
        bngc = arb.f32(15904, 8, "bngc")
        bngp = arb.f32(15912, 8, "bngp")
        bA = [arb.bf16(15920, 128, "bA0"), arb.bf16(15984, 128, "bA1")]
        pmask = cview("chind2")
        NT256 = T // 256
        emit_ln_stats(1.0)

        def ln_apply(t0, w, dstv, rbslot):
            j, c0 = t0 // 512, t0 % 512
            rb = rowb[rbslot % 2]
            rbn = f"rb{rbslot % 2}"
            src = stats[j:j + 1, :].rearrange("p (q c) -> p q c", q=2)[:, :, c0:c0 + w]
            dma("y", rb[0:1, 0:2 * w].rearrange("p (q c) -> p q c", q=2), src,
                reads=["stats"], writes=[rbn])
            obase = 32 * (1 + rbslot % 2)
            orow = ones[obase:obase + 1, 0:128]
            sc.op("t", (lambda t, rb=rb, orow=orow, w=w: t.matmul(
                ps[6][0:128, 0:w], orow, rb[0:1, 0:w],
                start=True, stop=True)), reads=[rbn, "const"], writes=["psb6"])
            sc.op("t", (lambda t, rb=rb, orow=orow, w=w: t.matmul(
                ps[7][0:128, 0:w], orow, rb[0:1, w:2 * w],
                start=True, stop=True)), reads=[rbn, "const"], writes=["psb7"])
            sc.op("v", (lambda v, t0=t0, w=w: v.scalar_tensor_tensor(
                aqt3[:, 0:8 * w], xv[:, :, t0:t0 + w], 1.0,
                ps[6][:, 0:w].unsqueeze(1).broadcast_to([128, 8, w]),
                A.mult, A.mult)),
                reads=xnames(t0, t0 + w) + ["psb6"], writes=["bsp"])
            sc.op("v", (lambda v, dstv=dstv, w=w: v.scalar_tensor_tensor(
                dstv[:, :, 0:w], aqt3[:, 0:8 * w], 1.0,
                ps[7][:, 0:w].unsqueeze(1).broadcast_to([128, 8, w]),
                A.mult, A.add)),
                reads=["bsp", "psb7"], writes=["xn"])

        def emit_group_out(g0):
            for c in range(NB):
                sc.op("s", (lambda e, c=c: e.activation(
                    bsq[:, 0:512], bogv[:, c, :], AF.Square)),
                    reads=["bog"], writes=["bsq"])
                sc.op("t", (lambda t, c=c: t.matmul(
                    ps[6][0:1, 0:512], ones[:, 0:1], bsq[:, 0:512],
                    start=(c == 0), stop=(c == NB - 1))),
                    reads=["bsq", "const"], writes=["psb6"])
            sc.op("v", lambda v: v.tensor_scalar(
                rmsrow[0:1, 0:512], ps[6][0:1, 0:512], INVD, None, A.mult),
                reads=["psb6"], writes=["bsq"])
            sc.op("s", lambda e: e.activation(
                rmsrow[0:1, 0:512], rmsrow[0:1, 0:512], AF.Ln, bias=EPS_RMS),
                reads=["bsq"], writes=["bsq"])
            sc.op("s", lambda e: e.activation(
                rmsrow[0:1, 0:512], rmsrow[0:1, 0:512], AF.Exp, scale=-0.5),
                reads=["bsq"], writes=["bsq"])
            sc.op("t", lambda t: t.matmul(
                ps[7][0:128, 0:512], ones[32:33, 0:128], rmsrow[0:1, 0:512],
                start=True, stop=True), reads=["bsq", "const"], writes=["psb7"])
            for c in range(NB):
                sc.op("v", (lambda v, c=c: v.scalar_tensor_tensor(
                    bogv[:, c, :], bogv[:, c, :], bgw_sb[:, c:c + 1],
                    ps[7][:, 0:512], A.mult, A.mult)),
                    reads=["psb7", "bgw"], writes=["bog"])
            for o in range(NB):
                wv, wn = wload("bWo", [o])
                bank = 4 + o % 2
                sc.op("t", (lambda t, wv=wv, bank=bank: [
                    t.matmul(ps[bank][:, 0:512], wv[:, c, 0:128], bogv[:, c, :],
                             start=(c == 0), stop=(c == NB - 1))
                    for c in range(NB)][-1]),
                    reads=[wn, "bog"], writes=[f"psb{bank}"])
                sc.op("v", (lambda v, bank=bank, o=o, g0=g0: v.tensor_tensor(
                    xv[:, o, g0:g0 + 512], ps[bank][:, 0:512],
                    xv[:, o, g0:g0 + 512], A.add)),
                    reads=[f"psb{bank}"] + xnames(g0, g0 + 512),
                    writes=xnames(g0, g0 + 512))

        sc.op("v", lambda v: v.memset(S_state[:, :], 0.0), writes=["S"])
        sc.op("v", lambda v: v.memset(bngc[:, 0:8], 0.0), writes=["bngc"])
        wcnt = [0]

        def wload(nm, obs):
            i = wcnt[0] % 2
            wcnt[0] += 1
            wn = f"bw{i}"
            for k, ob in enumerate(obs):
                dma("y", bwv[i][:, :, k * 128:(k + 1) * 128],
                    scr[nm].ap()[ob].transpose([1, 0, 2]), writes=[wn])
            return bwv[i][:, :, 0:128 * len(obs)], wn

        for tk in range(NT256):
            t0 = tk * 256
            ln_apply(t0, 256, xnv3, tk)
            for nm, which in (("bWf", "z"), ("bWi", "v")):
                for og in range(4):
                    wv, wn = wload(nm, [og * 2, og * 2 + 1])
                    for slot in range(2):
                        bank = 2 + slot
                        def mmz(t, wv=wv, slot=slot, bank=bank):
                            last = None
                            for c in range(NB):
                                last = t.matmul(
                                    ps[bank][:, 0:256],
                                    xnv3[:, c, slot * 128:(slot + 1) * 128],
                                    wv[:, c, :],
                                    start=(c == 0), stop=(c == NB - 1))
                            return last
                        dstv_ = zv if which == "z" else bvtv
                        dn = "bz" if which == "z" else "bvt"
                        sc.op("t", mmz, reads=["xn", wn], writes=[f"psb{bank}"])
                        sc.op("v" if slot == 0 else "s",
                              (lambda e, slot=slot, og=og, bank=bank,
                               dstv_=dstv_, sv=(slot == 1):
                               (e.copy if sv else e.tensor_copy)(
                                  dstv_[:, slot, og * 256:(og + 1) * 256],
                                  ps[bank][:, 0:256])),
                              reads=[f"psb{bank}"], writes=[dn])
            for slot in range(2):
                sc.op("s", (lambda e, slot=slot: e.activation(
                    spv[:, slot, :], zv[:, slot, :], AF.Exp, scale=-1.0)),
                    reads=["bz"], writes=["bsp"])
                sc.op("s", (lambda e, slot=slot: e.activation(
                    spv[:, slot, :], spv[:, slot, :], AF.Ln, bias=1.0)),
                    reads=["bsp"], writes=["bsp"])
                sc.op("s", (lambda e, slot=slot: e.activation(
                    bentv[:, slot, :], spv[:, slot, :], AF.Exp, scale=-1.0)),
                    reads=["bsp"], writes=["bent"])
                sc.op("v", (lambda v, slot=slot: v.tensor_scalar(
                    bktv[:, slot, :], bentv[:, slot, :], -1.0, 1.0,
                    A.mult, A.add)),
                    reads=["bent"], writes=["bkt"])
            kubuf = bsA
            for slot in range(2):
                sc.op("t", (lambda t, slot=slot: [
                    t.matmul(ps[4 + fh][:, 0:512], cview("tri64"),
                             spv[:, slot, fh * 512:(fh + 1) * 512],
                             start=True, stop=True)
                    for fh in range(2)][-1]),
                    reads=["bsp", "const"], writes=["psb4", "psb5"])
                for fh in range(2):
                    sc.op("s", (lambda e, slot=slot, fh=fh: e.activation(
                        bentv[:, slot, fh * 512:(fh + 1) * 512],
                        ps[4 + fh][:, 0:512], AF.Exp)),
                        reads=[f"psb{4 + fh}"], writes=["bent"])
                    sc.op("v", (lambda v, slot=slot, fh=fh: v.tensor_copy(
                        ngT[:, slot, fh * 512:(fh + 1) * 512],
                        ps[4 + fh][:, 0:512])),
                        reads=[f"psb{4 + fh}"], writes=["bz"])
                sc.op("v", (lambda v, slot=slot: v.tensor_tensor(
                    bktv[:, slot, :], bktv[:, slot, :], bentv[:, slot, :],
                    A.mult)), reads=["bent"], writes=["bkt"])
                for half in range(2):
                    bank = 4 + half
                    sc.op("t", (lambda t, slot=slot, half=half, bank=bank: [
                        t.transpose(ps[bank][:, i * 64:(i + 1) * 64]
                                    .bitcast(BF16),
                                    bktv[:, slot, (half * 4 + i) * 128:
                                         (half * 4 + i + 1) * 128], identb)
                        for i in range(4)][-1]),
                        reads=["bkt", "const"], writes=[f"psb{bank}"])
                    sc.op("v", (lambda v, slot=slot, half=half, bank=bank:
                                v.tensor_copy(
                                    bkfv[:, half * 4:half * 4 + 4,
                                         slot * 128:(slot + 1) * 128],
                                    ps[bank][:, 0:256].bitcast(BF16)
                                    .rearrange("p (c t) -> p c t", c=4))),
                          reads=[f"psb{bank}"], writes=["bkf"])

            for slot in range(2):
                for half in range(2):
                    bank = 4 + half
                    sc.op("t", (lambda t, slot=slot, half=half, bank=bank: [
                        t.transpose(ps[bank][:, i * 128:(i + 1) * 128],
                                    ngT[:, slot, (half * 4 + i) * 128:
                                        (half * 4 + i + 1) * 128], ident)
                        for i in range(4)][-1]),
                        reads=["bz", "const"], writes=[f"psb{bank}"])
                    sc.op("v", (lambda v, slot=slot, half=half, bank=bank:
                                v.tensor_copy(
                                    ngFs[:, slot, half * 4:half * 4 + 4, :],
                                    ps[bank][:, 0:512]
                                    .rearrange("p (c t) -> p c t", c=4))),
                          reads=[f"psb{bank}"], writes=["bz"])
            ngF5 = zc.rearrange("p (s c q t) -> p s c q t", s=2, c=NB, q=2)
            sc.op("v", lambda v: v.tensor_copy(
                ngtots.rearrange("p (c s q) -> p c s q", c=NB, s=2),
                ngF5[:, :, :, :, 63:64].transpose([0, 2, 1, 3, 4])
                .rearrange("p c s q one -> p c s (q one)")),
                reads=["bz"], writes=["bngt"])
            sc.op("s", lambda e: e.activation(
                bD[:, 0:32], ngtots[:, 0:32], AF.Exp, scale=-1.0),
                reads=["bngt"], writes=["bD"])
            sc.op("s", lambda e: e.activation(
                bent[:, 0:2048].rearrange("p (c s t) -> p s c t", s=2, c=NB),
                ngFs[:, :, :, :], AF.Exp, scale=-1.0,
                bias=float(SCALAR_CONSTS["lnscale_b"])),
                reads=["bz"], writes=["bent"])
            for o in range(NB):
                wv, wn = wload("bWq", [o])
                bank = 2 + o % 2
                sc.op("t", (lambda t, wv=wv, bank=bank: [
                    t.matmul(ps[bank][:, 0:256], wv[:, c, 0:128],
                             xnv3[:, c, :], start=(c == 0), stop=(c == NB - 1))
                    for c in range(NB)][-1]),
                    reads=[wn, "xn"], writes=[f"psb{bank}"])
                sc.op("s", (lambda e, bank=bank, o=o: e.activation(
                    bqv[:, o, :], ps[bank][:, 0:256], AF.Silu)),
                    reads=[f"psb{bank}"], writes=["bq"])
            sc.op("v", lambda v: v.tensor_tensor(
                bq[:, 0:2048], bq[:, 0:2048], bent[:, 0:2048], A.mult),
                reads=["bent"], writes=["bq"])
            if tk < 2:
                for cl in range(4):
                    cu = tk * 4 + cl
                    sc.op("s", lambda e: e.activation(
                        bngp[:, 0:8], bngc[:, 0:8], AF.Exp, scale=-1.0),
                        reads=["bngc"], writes=["bngp"])
                    qp = bqps[cu % 2]
                    qn = f"bqps{cu % 2}"
                    sc.op("v", (lambda v, cl=cl, qp=qp: v.tensor_tensor(
                        qp[:, 0:512].rearrange("p (c t) -> p c t", c=NB),
                        bqv[:, :, cl * 64:(cl + 1) * 64],
                        bngp[:, 0:8].unsqueeze(2).broadcast_to([128, 8, 64]),
                        A.mult)), reads=["bq", "bngp"], writes=[qn])
                    dma("y", scr["qp0"].ap()[cu], qp[:, 0:512], reads=[qn], writes=["scrqp0"])
                    sc.op("v", (lambda v, cl=cl: v.tensor_tensor(
                        bngc[:, 0:8], bngc[:, 0:8],
                        ngtv[:, :, cl:cl + 1], A.add)),
                        reads=["bngt"], writes=["bngc"])
            for slot in range(2):
                obank = [2 * slot, 2 * slot + 1]
                for h in range(HEADS_B):
                    pa = 4 + h % 2
                    sc.op("t", (lambda t, slot=slot, h=h, pa=pa: t.matmul(
                        ps[pa][:, 0:128], bkfv[:, h, slot * 128:(slot + 1) * 128],
                        bqv[:, h, slot * 128:(slot + 1) * 128],
                        start=True, stop=True)),
                        reads=["bkf", "bq"], writes=[f"psb{pa}"])
                    sc.op("v", (lambda t_, h=h, pa=pa: t_.tensor_tensor(
                        bA[h % 2][:, 0:128], ps[pa][:, 0:128],
                        cview("amask").bitcast(BF16), A.mult)),
                        reads=[f"psb{pa}", "const"], writes=[f"bA{h % 2}"])
                    ob = obank[h // 4]
                    sc.op("t", (lambda t, slot=slot, h=h, ob=ob: t.matmul(
                        ps[ob][:, (h % 4) * 128:(h % 4) * 128 + 128],
                        bvtv[:, slot, h * 128:(h + 1) * 128],
                        bA[h % 2][:, 0:128], start=(h % 4 == 0), stop=False)),
                        reads=[f"bA{h % 2}", "bvt"], writes=[f"psb{ob}"])
                for cpar in range(2):
                    cl = slot * 2 + cpar
                    cu = tk * 4 + cl
                    sb = cpar * 64
                    def mminter(t, slot=slot, cpar=cpar, sb=sb, obank=tuple(obank)):
                        last = None
                        for h in range(HEADS_B):
                            ob = obank[h // 4]
                            last = t.matmul(
                                ps[ob][:, (h % 4) * 128 + sb:
                                       (h % 4) * 128 + sb + 64],
                                S_state[:, h * 128:(h + 1) * 128].bitcast(BF16)
                                if False else
                                S_state[:, h * 128:(h + 1) * 128],
                                bqv[:, h, slot * 128 + sb:slot * 128 + sb + 64],
                                start=False, stop=(cpar == 1 and h % 4 == 3))
                        return last
                    sc.op("t", mminter, reads=["S", "bq"],
                          writes=[f"psb{obank[0]}", f"psb{obank[1]}"])
                    for fh in range(2):
                        sc.op("t", (lambda t, slot=slot, cpar=cpar, fh=fh:
                                    t.matmul(
                                        ps[6 + fh][0:1, 0:512],
                                        pmask[:, cpar:cpar + 1],
                                        spv[:, slot, fh * 512:(fh + 1) * 512],
                                        start=True, stop=True)),
                              reads=["bsp", "const"], writes=[f"psb{6 + fh}"])
                        sc.op("s", (lambda e, fh=fh: e.activation(
                            ebuf[0:1, 0:512], ps[6 + fh][0:1, 0:512], AF.Exp,
                            scale=-1.0)),
                            reads=[f"psb{6 + fh}"], writes=["bsq"])
                        sc.op("t", (lambda t, cpar=cpar, fh=fh: t.matmul(
                            ps[6 + fh][0:128, 0:512],
                            cview("pind2")[0:1, cpar * 128:cpar * 128 + 128],
                            ebuf[0:1, 0:512], start=True, stop=True)),
                            reads=["bsq", "const"], writes=[f"psb{6 + fh}"])
                        sc.op("v", (lambda v, slot=slot, fh=fh:
                                    v.tensor_tensor(
                                        kubuf[:, fh * 512:(fh + 1) * 512],
                                        bktv[:, slot, fh * 512:(fh + 1) * 512],
                                        ps[6 + fh][:, 0:512], A.mult)),
                              reads=[f"psb{6 + fh}", "bkt"], writes=["bsA"])
                    def mmu(t, slot=slot):
                        last = None
                        for h in range(HEADS_B):
                            bank = 6 + h // 4
                            last = t.matmul(
                                ps[bank][:, (h % 4) * 128:(h % 4) * 128 + 128],
                                kubuf[:, h * 128:(h + 1) * 128],
                                bvtv[:, slot, h * 128:(h + 1) * 128],
                                start=True, stop=True)
                        return last
                    sc.op("t", mmu, reads=["bsA", "bvt"],
                          writes=["psb6", "psb7"])
                    def scan(v, cl=cl):
                        last = None
                        for h in range(HEADS_B):
                            bank = 6 + h // 4
                            last = v.scalar_tensor_tensor(
                                S_state[:, h * 128:(h + 1) * 128],
                                S_state[:, h * 128:(h + 1) * 128],
                                bD[:, h * 4 + cl:h * 4 + cl + 1],
                                ps[bank][:, (h % 4) * 128:(h % 4) * 128 + 128],
                                A.mult, A.add)
                        return last
                    sc.op("v", scan, reads=["psb6", "psb7", "bD"],
                          writes=["S"])
                for bi in range(2):
                    ob = obank[bi]
                    sc.op("v" if bi == 0 else "s",
                          (lambda e, ob=ob, bi=bi, slot=slot, tk=tk,
                           sv=(bi == 1): (e.copy if sv else e.tensor_copy)(
                              bogv[:, bi * 4:bi * 4 + 4,
                                   (tk % 2) * 256 + slot * 128:
                                   (tk % 2) * 256 + slot * 128 + 128],
                              ps[ob][:, 0:512]
                              .rearrange("p (c t) -> p c t", c=4))),
                          reads=[f"psb{ob}"], writes=["bog"])
            if tk % 2 == 1:
                g0 = (tk - 1) * 256
                if g0 == 0:
                    dma("y", scr["og0"].ap()[:, :], bog[:, 0:4096], reads=["bog"], writes=["scrog0"])
                else:
                    emit_group_out(g0)
        flagC = cview("flags")[:, 0:1]
        flagU = cview("flags")[:, 1:2]
        sc.op("v", lambda v: v.tensor_scalar(
            ccs2[:, 0:1024], S_state[:, 0:1024], flagC, None, A.mult),
            reads=["S", "const"], writes=["bz"])
        dma("y", cc2_in.ap(), ccs2[:, 0:1024], reads=["bz"], writes=["cc2i"])
        if no_coll:
            dma("y", cc2_out.ap(), cc2_in.ap(), reads=["cc2i"], writes=["cc2o"])
        else:
            sc.op("g", lambda g: g.collective_compute(
                "AllReduce", A.add, replica_groups=RGROUPS,
                ins=[cc2_in.ap().opt()], outs=[cc2_out.ap().opt()]),
                reads=["cc2i"], writes=["cc2o"])
        dma("y", ccs2[:, 0:1024], cc2_out.ap(), reads=["cc2o"], writes=["bz"])
        sc.op("v", lambda v: v.tensor_scalar(
            bsA[:, 0:1024], ccs2[:, 0:1024], flagU, None, A.mult),
            reads=["bz", "const"], writes=["bsA"])
        qpb = arena[:, 8192:10240].bitcast(BF16)
        qpv = qpb.rearrange("p (u c t) -> p u c t", u=8, c=NB)
        dma("y", bog[:, 0:4096], scr["og0"].ap()[:, :], reads=["scrog0"], writes=["bog"])
        for cu in range(8):
            dma("y", qpb[:, cu * 512:(cu + 1) * 512], scr["qp0"].ap()[cu],
                reads=["scrqp0"], writes=["bkf", "bvt"])
        for h in range(HEADS_B):
            sc.op("t", (lambda t, h=h: t.matmul(
                ps[6][:, 0:512], bsA[:, h * 128:(h + 1) * 128],
                qpv[:, :, h, :], start=True, stop=True)),
                reads=["bsA", "bkf", "bvt"], writes=["psb6"])
            sc.op("v", (lambda v, h=h: v.tensor_tensor(
                bogv[:, h, :], bogv[:, h, :], ps[6][:, 0:512], A.add)),
                reads=["psb6"], writes=["bog"])
        emit_group_out(0)
        sc.fence()

    def emit_ffn():
        h_bf = ABf(0, 16384)
        hvb = h_bf.rearrange("p (c t) -> p c t", c=NH)
        xi = ABf(32, 4096)
        xivb = xi.rearrange("p (c t) -> p c t", c=NB)
        qt = AFr(40, 2048)
        qtmp = [AFr(48, 512), AFr(50, 512)]
        wb1 = [ABf(52, 1024), ABf(54, 1024)]
        wb2 = [ABf(56, 2048), ABf(60, 2048)]
        emit_ln_stats(16.0)
        for tk in range(NT512):
            pS = ps[0 if tk % 2 == 0 else 6]
            pB = ps[1 if tk % 2 == 0 else 7]
            nS = "psb0" if tk % 2 == 0 else "psb6"
            nB = "psb1" if tk % 2 == 0 else "psb7"
            rb = rowb[tk % 2]
            rbn = f"rb{tk % 2}"
            dma("y", rb, stats[tk:tk + 1, 0:1024], reads=["stats"], writes=[rbn])
            obase = 32 * (1 + tk % 2)
            orow = ones[obase:obase + 1, 0:128]
            sc.op("t", (lambda t, pS=pS, rb=rb, orow=orow: t.matmul(
                pS[0:128, 0:512], orow, rb[0:1, 0:512],
                start=True, stop=True)), reads=[rbn, "const"], writes=[nS])
            sc.op("t", (lambda t, pB=pB, rb=rb, orow=orow: t.matmul(
                pB[0:128, 0:512], orow, rb[0:1, 512:1024],
                start=True, stop=True)), reads=[rbn, "const"], writes=[nB])
            t0 = tk * 512
            for th in range(2):
                sl = slice(t0 + th * 256, t0 + th * 256 + 256)
                xsrc = xv[:, :, sl]
                sc.op("v", (lambda v, xsrc=xsrc, pS=pS, th=th: v.scalar_tensor_tensor(
                    qt[:, 0:2048], xsrc, 1.0,
                    pS[:, th * 256:(th + 1) * 256].unsqueeze(1)
                    .broadcast_to([128, 8, 256]),
                    A.mult, A.mult)),
                    reads=xnames(t0, t0 + 512) + [nS], writes=["qtarea"])
                sc.op("v", (lambda v, pB=pB, th=th: v.scalar_tensor_tensor(
                    qt[:, 0:2048], qt[:, 0:2048], 1.0,
                    pB[:, th * 256:(th + 1) * 256].unsqueeze(1)
                    .broadcast_to([128, 8, 256]),
                    A.mult, A.add)),
                    reads=[nB], writes=["qtarea"])
                sc.op("v", lambda v: v.tensor_scalar(
                    qt[:, 0:2048], qt[:, 0:2048], -128.0, 127.0, A.max, A.min),
                    reads=["qtarea"], writes=["qtarea"])
                xdst = xivb[:, :, th * 256:(th + 1) * 256]
                sc.op("v", (lambda v, xdst=xdst: v.tensor_scalar(
                    xdst, qt[:, 0:2048], MAGIC, -MAGIC, A.add, A.add)),
                    reads=["qtarea"], writes=["xi"])
            for o1 in range(NH):
                wb = wb1[o1 % 2]
                wn = f"wb1_{o1 % 2}"
                wvb = wb.rearrange("p (c o) -> p c o", c=NB)
                dma("y", wvb, scr["w1q"].ap()[o1].transpose([1, 0, 2]),
                    writes=[wn])
                bank = 2 + o1 % 2
                bn = f"psb{bank}"
                sc.op("t", (lambda t, wvb=wvb, bank=bank: [
                    t.matmul(ps[bank][:, 0:512], wvb[:, c, :], xivb[:, c, :],
                             start=(c == 0), stop=(c == NB - 1))
                    for c in range(NB)][-1]),
                    reads=[wn, "xi"], writes=[bn], cost=3600.0)
                qm = qtmp[o1 % 2]
                qmn = f"qtmp{o1 % 2}"
                e1 = "v" if o1 % 2 == 0 else "s"
                if e1 == "v":
                    sc.op("v", (lambda v, bank=bank, qm=qm, o1=o1: v.tensor_scalar(
                        qm[:], ps[bank][:, 0:512], b1q16[:, o1:o1 + 1], 0.0,
                        A.add, A.max)), reads=[bn, "b1q16"], writes=[qmn])
                else:
                    sc.op("s", (lambda e, bank=bank, qm=qm, o1=o1: e.activation(
                        qm[:], ps[bank][:, 0:512], AF.Relu,
                        bias=b1q16[:, o1:o1 + 1])), reads=[bn, "b1q16"],
                        writes=[qmn])
                sc.op("v", (lambda v, qm=qm: v.tensor_scalar(
                    qm[:], qm[:], 127.0, MAGIC, A.min, A.add)),
                    reads=[qmn], writes=[qmn])
                e3 = "s" if o1 % 2 == 0 else "v"
                if e3 == "s":
                    sc.op("s", (lambda e, qm=qm, o1=o1: e.activation(
                        hvb[:, o1, :], qm[:], AF.Copy, bias=-MAGIC)),
                        reads=[qmn], writes=[f"h{o1}"])
                else:
                    sc.op("v", (lambda v, qm=qm, o1=o1: v.tensor_scalar(
                        hvb[:, o1, :], qm[:], -MAGIC, None, A.add)),
                        reads=[qmn], writes=[f"h{o1}"])
            for o2 in range(NB):
                bank = 4 + o2 % 2
                bn = f"psb{bank}"
                for hh in range(2):
                    wb = wb2[(o2 * 2 + hh) % 2]
                    wn = f"wb2_{(o2 * 2 + hh) % 2}"
                    wvb = wb.rearrange("p (c o) -> p c o", c=16)
                    dma("y", wvb,
                        scr["w2q"].ap()[o2, hh * 16:(hh + 1) * 16]
                        .transpose([1, 0, 2]), writes=[wn])
                    sc.op("t", (lambda t, wvb=wvb, bank=bank, hh=hh: [
                        t.matmul(ps[bank][:, 0:512], wvb[:, c, :],
                                 hvb[:, hh * 16 + c, :],
                                 start=(hh == 0 and c == 0),
                                 stop=(hh == 1 and c == 15))
                        for c in range(16)][-1]),
                        reads=[wn] + [f"h{hh * 16 + c}" for c in range(16)],
                        writes=[bn], cost=7200.0)
                sc.op("v", (lambda v, bank=bank, o2=o2, t0=t0: v.scalar_tensor_tensor(
                    xv[:, o2, t0:t0 + 512], ps[bank][:, 0:512],
                    b2q[:, o2:o2 + 1], xv[:, o2, t0:t0 + 512],
                    A.add, A.add)),
                    reads=[bn, "b2q"] + xnames(t0, t0 + 512),
                    writes=xnames(t0, t0 + 512))
        sc.fence()

    if "attn_a" in phases:
        emit_attn_a()
    if "ffn1" in phases:
        emit_ffn()
    if "attn_b" in phases:
        emit_attn_b()
    if "ffn2" in phases:
        emit_ffn()

    sbuf_store = [AFr(0, 1024), AFr(4, 1024)]
    store_ops = []
    for i in range(NT128):
        tb = sbuf_store[i % 2]
        tbn = f"stb{i % 2}"
        for h in range(2):
            bank = 2 + h
            sc.op("t", (lambda t, i=i, h=h, bank=bank: [
                t.transpose(ps[bank][:, c * 128:(c + 1) * 128],
                            xv[:, 4 * h + c, i * 128:(i + 1) * 128], ident)
                for c in range(4)][-1]),
                reads=[f"x_{i}", "const"], writes=[f"psb{bank}"])
            sc.op("v" if h == 0 else "s",
                  (lambda e, h=h, bank=bank, tb=tb, s=(h == 1):
                   (e.copy if s else e.tensor_copy)(
                       tb[:, h * 512:(h + 1) * 512], ps[bank][:, 0:512])),
                  reads=[f"psb{bank}"], writes=[tbn])
        store_ops.append(
            dma("y", y_out[i * 128:(i + 1) * 128, :], tb[:], reads=[tbn]))

    sc.op("g", lambda g: g.nop(), extra=store_ops)

    sc.reorder()
    sc.finalize()
    with nc.Block() as block:
        @block.tensor
        def _(t):
            sc.run("t", t)

        @block.vector
        def _(v):
            sc.run("v", v)

        @block.scalar
        def _(s):
            sc.run("s", s)

        @block.gpsimd
        def _(g):
            sc.run("g", g)

        @block.sync
        def _(y):
            sc.run("y", y)

    return nc


_NC_CACHE = {}


def get_nc(T, n_cores=8):
    key = (T, n_cores)
    if key not in _NC_CACHE:
        _NC_CACHE[key] = build_nc(T, n_cores=n_cores)
    return _NC_CACHE[key]


def make_wmap(inputs):
    wmap = {
        "aWq": inputs["L0_attn_a_Wq"], "aWk": inputs["L0_attn_a_Wk"],
        "aWv": inputs["L0_attn_a_Wv"], "aWo": inputs["L0_attn_a_Wo"],
        "abo": np.asarray(inputs["L0_attn_a_bo"], np.float32).reshape(1, D),
        "bWq": inputs["L0_attn_b_Wq"], "bWf": inputs["L0_attn_b_Wf"],
        "bWi": inputs["L0_attn_b_Wi"], "bWo": inputs["L0_attn_b_Wo"],
        "bgw": np.asarray(inputs["L0_attn_b_g_weight"], np.float32).reshape(1, D),
        "fw1": inputs["L0_ffn_w1"],
        "fb1": np.asarray(inputs["L0_ffn_b1"], np.float32).reshape(1, HID),
        "fw2": inputs["L0_ffn_w2"],
        "fb2": np.asarray(inputs["L0_ffn_b2"], np.float32).reshape(1, D),
    }
    return {k: np.ascontiguousarray(np.asarray(v, np.float32))
            for k, v in wmap.items()}


def kernel(**inputs):
    x = np.asarray(inputs["L0_attn_a_x"], np.float32)
    Bz, S, _ = x.shape
    T = S // 2
    nc = get_nc(T, 2 * Bz)
    wmap = make_wmap(inputs)
    consts = [make_consts(False), make_consts(True)]
    zero_halo = np.zeros((WINDOW, D), np.float32)
    in_maps = []
    for b in range(Bz):
        for h in range(2):
            m = dict(wmap)
            m["x_in"] = np.ascontiguousarray(x[b, h * T:(h + 1) * T, :])
            m["halo_in"] = (np.ascontiguousarray(x[b, T - WINDOW:T, :]) if h == 1
                            else zero_halo)
            m["consts"] = consts[h]
            in_maps.append(m)
    res = run_bass_kernel_spmd(nc, in_maps, core_ids=list(range(2 * Bz)))
    out = np.empty((Bz, S, D), np.float32)
    for b in range(Bz):
        for h in range(2):
            out[b, h * T:(h + 1) * T, :] = res.results[2 * b + h]["y_out"]
    return out


# version 08
# speedup vs baseline: 1.6655x; 1.0477x over previous
import heapq
import numpy as np
import concourse.bass as bass
import concourse.mybir as mybir
from concourse.bass_utils import run_bass_kernel_spmd
from concourse.alu_op_type import AluOpType as A
from contextlib import ExitStack

F32 = mybir.dt.float32
BF16 = mybir.dt.bfloat16
AF = mybir.ActivationFunctionType

D = 1024
NB = D // 128
WINDOW = 128
CHUNK = 64
LOCAL_HEADS = 4
GLOBAL_HEADS = 12
DH = 64
HEADS_B = 8
DF = 128
HID = 4096
NH = HID // 128
MAGIC = float(np.float32(1.5 * 2.0 ** 23))
NEG = -1.0e30
EPS_LN = 1e-5
EPS_RMS = 1e-5
EPS_CTX = 1e-6
INVD = 1.0 / D


class ConstReg:
    def __init__(self):
        self.off = {}
        self.w = {}
        self.n = 0

    def reg(self, name, w):
        self.off[name] = self.n
        self.w[name] = w
        self.n += w


CR = ConstReg()
CR.reg("ident", 128)
CR.reg("identb", 64)
CR.reg("tri64", 128)
CR.reg("mask_self", 128)
CR.reg("mask_prev0", 128)
CR.reg("amask", 64)
CR.reg("ones", 128)
CR.reg("onesb", 8)
CR.reg("chind2", 2)
CR.reg("pind2T", 128)
CR.reg("selcols", 64)
CR.reg("hsum12", 72)
CR.reg("hexp12", 384)
CR.reg("flags", 4)
CR.reg("cscal", 8)
SCALAR_CONSTS = {
    "eps": 1e-5,
    "negmagic": -MAGIC,
    "lnscale_b": float(-0.5 * np.log(128.0)),
    "epsctx": EPS_CTX,
    "one": 1.0,
}
CONST_COLS = ((CR.n + 15) // 16) * 16


def make_consts(is_half1):
    import ml_dtypes
    c = np.zeros((128, CONST_COLS), np.float32)

    def put(name, arr):
        o = CR.off[name]
        c[:, o:o + CR.w[name]] = arr

    put("ident", np.eye(128, dtype=np.float32))
    ibb = np.ascontiguousarray(np.eye(128, dtype=np.float32).astype(ml_dtypes.bfloat16))
    put("identb", ibb.view(np.float32))
    tri = np.zeros((128, 128), np.float32)
    for blk in range(2):
        for tt in range(64):
            tri[blk * 64:blk * 64 + tt + 1, blk * 64 + tt] = 1.0
    put("tri64", tri)
    ms = np.where(np.arange(128)[:, None] > np.arange(128)[None, :], NEG, 0.0)
    put("mask_self", ms.astype(np.float32))
    put("mask_prev0", np.zeros((128, 128), np.float32) if is_half1
        else np.full((128, 128), NEG, np.float32))
    am = np.zeros((128, 128), np.float32)
    for blk in range(2):
        for tt in range(64):
            am[blk * 64:blk * 64 + tt + 1, blk * 64 + tt] = 1.0
    put("amask", np.ascontiguousarray(am.astype(ml_dtypes.bfloat16)).view(np.float32))
    put("ones", np.ones((128, 128), np.float32))
    put("onesb", np.ascontiguousarray(
        np.ones((128, 16), np.float32).astype(ml_dtypes.bfloat16)).view(np.float32))
    ci = np.zeros((128, 2), np.float32)
    ci[0:64, 0] = 1.0
    ci[64:128, 1] = 1.0
    put("chind2", ci)
    pT = np.zeros((128, 256), np.float32)
    pT[0, 0:64] = 1.0
    pT[1, 128 + 64:256] = 1.0
    put("pind2T", np.ascontiguousarray(
        pT.astype(ml_dtypes.bfloat16)).view(np.float32))
    sel = np.zeros((128, 64), np.float32)
    for j in range(8):
        sel[:, 8 * j + j] = 1.0
    put("selcols", sel)
    hs = np.zeros((128, 72), np.float32)
    he = np.zeros((128, 768), np.float32)
    for cb in range(6):
        hs[0:64, 12 * cb + 2 * cb] = 1.0
        hs[64:128, 12 * cb + 2 * cb + 1] = 1.0
        he[2 * cb, 128 * cb:128 * cb + 64] = 1.0
        he[2 * cb + 1, 128 * cb + 64:128 * cb + 128] = 1.0
    put("hsum12", hs)
    put("hexp12", np.ascontiguousarray(
        he.astype(ml_dtypes.bfloat16)).view(np.float32))
    fl = np.zeros((128, 4), np.float32)
    fl[:, 0] = 0.0 if is_half1 else 1.0
    fl[:, 1] = 1.0 if is_half1 else 0.0
    put("flags", fl)
    cs = np.zeros((128, 8), np.float32)
    for i, v in enumerate(SCALAR_CONSTS.values()):
        cs[:, i] = np.float32(v)
    put("cscal", cs)
    return c


class Op:
    __slots__ = ("eng", "fn", "deps", "dma", "needed", "idx", "sem", "val",
                 "inc", "cost")

    def __init__(self, eng, fn, deps, dma, idx, cost=None):
        self.eng, self.fn, self.deps, self.dma, self.idx = eng, fn, list(deps), dma, idx
        self.needed = dma
        self.sem = None
        self.val = 0
        self.inc = 16 if dma else 1
        self.cost = cost


class Buf:
    __slots__ = ("w", "r")

    def __init__(self):
        self.w = None
        self.r = []


class Sched:
    NPOOL = 4

    def __init__(self, nc, stack, n_dma_sems=48):
        self.nc = nc
        self.ops = []
        self.eng_sem = {e: [stack.enter_context(nc.semaphore(f"es_{e}{i}"))
                            for i in range(self.NPOOL)]
                        for e in ("t", "v", "s", "g", "y")}
        self.dma_sems = [stack.enter_context(nc.semaphore(f"ds_{i}"))
                         for i in range(n_dma_sems)]
        self.dma_i = 0
        self._stack = stack
        self.gdma_sems = []
        self.bufs = {}
        self.last = {e: None for e in ("t", "v", "s", "g", "y")}
        self.dma_open = []
        self.fence_groups = []

    def B(self, name):
        if name not in self.bufs:
            self.bufs[name] = Buf()
        return self.bufs[name]

    def op(self, eng, fn, reads=(), writes=(), dma=False, extra=(), cost=None):
        psr = [b for b in reads if b.startswith("psb")]
        if psr:
            reads = [b for b in reads if not b.startswith("psb")]
            writes = list(writes) + [b for b in psr if b not in writes]
        deps = list(extra)
        for b in reads:
            bb = self.B(b)
            if bb.w is not None:
                deps.append(bb.w)
        for b in writes:
            bb = self.B(b)
            deps.extend(bb.r)
            if bb.w is not None:
                deps.append(bb.w)
        o = Op(eng, fn, [d for d in deps if d is not None], dma, len(self.ops),
               cost=cost)
        self.ops.append(o)
        self.last[eng] = o
        if dma:
            self.dma_open.append(o)
        for b in reads:
            self.B(b).r.append(o)
        for b in writes:
            bb = self.B(b)
            bb.w = o
            bb.r = []
        return o

    def fence(self):
        prev = [o for o in self.last.values() if o is not None] + self.dma_open
        self.dma_open = []
        grp = []
        for e in ("t", "v", "s", "g", "y"):
            grp.append(self.op(e, lambda eng: eng.nop(), extra=prev))
        self.fence_groups.append(grp)
        f = {e: self.last[e] for e in self.last}
        for b in self.bufs.values():
            if b.w is not None:
                b.w = f[b.w.eng]
            b.r = []

    _DEF_COST = {"t": 400.0, "v": 600.0, "s": 700.0, "g": 2500.0, "y": 300.0}

    def _list_schedule(self, region):
        n = len(region)
        if n < 3:
            return list(region)
        idx_in = {id(o): i for i, o in enumerate(region)}
        rev = [[] for _ in range(n)]
        pend = [0] * n
        for i, o in enumerate(region):
            for d in o.deps:
                j = idx_in.get(id(d))
                if j is not None:
                    rev[j].append(i)
                    pend[i] += 1
        ready = {e: [] for e in ("t", "v", "s", "g", "y")}
        for i, o in enumerate(region):
            if pend[i] == 0:
                heapq.heappush(ready[o.eng], i)
        ebusy = {e: 0.0 for e in ready}
        events = []
        order = []
        now = 0.0
        scheduled = 0
        while scheduled < n:
            started = True
            while started:
                started = False
                for e, h in ready.items():
                    if h and ebusy[e] <= now + 1e-9:
                        i = heapq.heappop(h)
                        o = region[i]
                        if o.dma:
                            ec, dc = 150.0, (o.cost or 1500.0)
                        else:
                            ec = dc = o.cost or self._DEF_COST[e]
                        ebusy[e] = now + ec
                        heapq.heappush(events, (now + dc, i))
                        order.append(o)
                        scheduled += 1
                        started = True
            if scheduled >= n:
                break
            cand = [events[0][0]] if events else []
            cand += [ebusy[e] for e, h in ready.items() if h]
            assert cand, "list-schedule deadlock (cyclic deps?)"
            now = min(cand)
            while events and events[0][0] <= now + 1e-9:
                _, i2 = heapq.heappop(events)
                for k in rev[i2]:
                    pend[k] -= 1
                    if pend[k] == 0:
                        heapq.heappush(ready[region[k].eng], k)
        return order

    def reorder(self):
        ops = self.ops
        new_ops = []
        last_eng = {}
        pos = 0
        for grp in self.fence_groups:
            start = grp[0].idx
            assert [o.idx for o in grp] == list(range(start, start + len(grp)))
            region = ops[pos:start]
            pos = grp[-1].idx + 1
            sched = self._list_schedule(region)
            new_ops.extend(sched)
            for o in sched:
                last_eng[o.eng] = o
            dmas = [o for o in region if o.dma]
            for fo in grp:
                fo.deps = list(last_eng.values()) + dmas
            new_ops.extend(grp)
            for o in grp:
                last_eng[o.eng] = o
        new_ops.extend(ops[pos:])
        assert len(new_ops) == len(ops)
        self.ops = new_ops
        for i, o in enumerate(self.ops):
            o.idx = i

    def finalize(self):
        for op in self.ops:
            seen, dd = set(), []
            for d in op.deps:
                if id(d) in seen:
                    continue
                seen.add(id(d))
                dd.append(d)
                assert d.idx < op.idx
                d.needed = True
            op.deps = dd
        cnt = {e: 0 for e in self.eng_sem}
        use = {}
        for op in self.ops:
            if not op.needed:
                continue
            if op.dma:
                if op.eng == "g":
                    sem = self._stack.enter_context(
                        self.nc.semaphore(f"gs_{len(self.gdma_sems)}"))
                    self.gdma_sems.append(sem)
                    op.sem, op.val = sem, 16
                    continue
                sem = self.dma_sems[self.dma_i % len(self.dma_sems)]
                self.dma_i += 1
                use[id(sem)] = use.get(id(sem), 0) + 16
                op.sem, op.val = sem, use[id(sem)]
            else:
                sem = self.eng_sem[op.eng][cnt[op.eng] % self.NPOOL]
                cnt[op.eng] += 1
                use[id(sem)] = use.get(id(sem), 0) + 1
                op.sem, op.val = sem, use[id(sem)]

    def run(self, ekey, eng):
        mark = {}
        for op in self.ops:
            if op.eng != ekey:
                continue
            for d in op.deps:
                k = id(d.sem)
                if mark.get(k, 0) >= d.val:
                    continue
                eng.wait_ge(d.sem, d.val)
                mark[k] = d.val
            last = op.fn(eng)
            if op.needed:
                assert last is not None, f"op idx {op.idx} on {ekey} emitted nothing"
                last.then_inc(op.sem, op.inc)


ALL_PHASES = ("attn_a", "ffn1", "attn_b", "ffn2")


def build_nc(T, phases=ALL_PHASES, n_cores=8, debug=False, no_coll=False, aa_stop=None):
    assert T % 512 == 0
    NT128 = T // 128
    NT512 = T // 512
    nc = bass.Bass("TRN2", target_bir_lowering=False, debug=debug)
    stack = ExitStack()

    x_in = nc.dram_tensor("x_in", [T, D], F32, kind="ExternalInput").ap()
    halo_in = nc.dram_tensor("halo_in", [WINDOW, D], F32, kind="ExternalInput").ap()
    consts_in = nc.dram_tensor("consts", [128, CONST_COLS], F32, kind="ExternalInput").ap()
    y_out = nc.dram_tensor("y_out", [T, D], F32, kind="ExternalOutput").ap()

    Wd = {}
    for nm, shp in [
        ("aWq", (D, D)), ("aWk", (D, D)), ("aWv", (D, D)), ("aWo", (D, D)),
        ("abo", (1, D)),
        ("bWq", (D, D)), ("bWf", (D, D)), ("bWi", (D, D)), ("bWo", (D, D)),
        ("bgw", (1, D)),
        ("fw1", (HID, D)), ("fb1", (1, HID)), ("fw2", (D, HID)), ("fb2", (1, D)),
    ]:
        Wd[nm] = nc.dram_tensor(nm, list(shp), F32, kind="ExternalInput").ap()

    scr = {}
    scr["w1q"] = nc.dram_tensor("scr_w1q", [NH, NB, 128, 128], BF16, kind="Internal")
    scr["w2q"] = nc.dram_tensor("scr_w2q", [NB, NH, 128, 128], BF16, kind="Internal")
    for nm in ("bWq", "bWf", "bWi", "bWo"):
        scr[nm] = nc.dram_tensor("scr_" + nm, [NB, NB, 128, 128], BF16, kind="Internal")
    for nm in ("aWq", "aWk", "aWv", "aWo"):
        scr[nm] = nc.dram_tensor("scr_" + nm, [NB, NB, 128, 128], BF16, kind="Internal")

    scr["ctx"] = nc.dram_tensor("scr_ctx", [T // CHUNK, 64, 784], BF16,
                                kind="Internal")
    scr["og0"] = nc.dram_tensor("scr_og0", [128, 4096], BF16, kind="Internal")
    scr["qp0"] = nc.dram_tensor("scr_qp0", [8, 128, 512], BF16, kind="Internal")

    CTX_ROWS = 128
    cc1_in = nc.dram_tensor("cc1_in", [64, 792], F32, kind="Internal")
    cc1_out = nc.dram_tensor("cc1_out", [64, 792], F32, kind="Internal")
    cc2_in = nc.dram_tensor("cc2_in", [128, HEADS_B * 128], F32, kind="Internal")
    cc2_out = nc.dram_tensor("cc2_out", [128, HEADS_B * 128], F32, kind="Internal")

    sbt = {}

    def sbuf(name, shape, dtype):
        sbt[name] = stack.enter_context(nc.sbuf_tensor(name, list(shape), dtype))
        return sbt[name]

    x_sb = sbuf("x", [128, NB * T], F32)
    const_sb = sbuf("const", [128, CONST_COLS], F32)
    b1q16 = sbuf("b1q16", [128, NH], F32)
    b2q = sbuf("b2q", [128, NB], F32)
    abo_sb = sbuf("abo_pp", [128, NB], F32)
    bgw_sb = sbuf("bgw_pp", [128, NB], F32)
    S_state = sbuf("S_state", [128, HEADS_B * 128], BF16)
    ARENA_COLS = 18304
    arena = sbuf("arena", [128, ARENA_COLS], F32)

    ps = [stack.enter_context(nc.psum_tensor(f"ps{i}", [128, 512], F32))
          for i in range(8)]

    sc = Sched(nc, stack)
    xv = x_sb.ap().rearrange("p (c t) -> p c t", c=NB)

    def cview(name):
        o = CR.off[name]
        return const_sb[:, o:o + CR.w[name]]

    ident = cview("ident")
    identb = cview("identb").bitcast(BF16)
    ones = cview("ones")
    onesb = cview("onesb").bitcast(BF16)
    for i, (k, v) in enumerate(SCALAR_CONSTS.items()):
        capv = const_sb[:, CR.off["cscal"] + i:CR.off["cscal"] + i + 1]
        nc.const_aps.register(F32, float(v), capv)
        if float(np.float32(v)) != float(v):
            nc.const_aps.register(F32, float(np.float32(v)), capv)

    def AFr(off_kb, cols):
        o = off_kb * 256
        assert o + cols <= ARENA_COLS, (off_kb, cols)
        return arena[:, o:o + cols]

    def ABf(off_kb, cols_bf):
        o = off_kb * 256
        assert o + cols_bf // 2 <= ARENA_COLS
        return arena[:, o:o + cols_bf // 2].bitcast(BF16)

    class Arena:

        def __init__(self):
            self.spans = []

        def _reg(self, off, n, buf):
            assert off + n <= ARENA_COLS, (off, n)
            for (a, b, nm) in self.spans:
                if not (off + n <= a or off >= b):
                    assert nm == buf, f"arena overlap {buf} vs {nm} @{off}"
            self.spans.append((off, off + n, buf))

        def f32(self, off, n, buf):
            self._reg(off, n, buf)
            return arena[:, off:off + n]

        def bf16(self, off, nbf, buf):
            assert nbf % 2 == 0
            self._reg(off, nbf // 2, buf)
            return arena[:, off:off + nbf // 2].bitcast(BF16)

    def dma(queue, out, in_, reads=(), writes=(), extra=(), slow=False):
        return sc.op(queue, lambda e, out=out, in_=in_, slow=slow: e.dma_start(
            out=out, in_=in_, allow_slow_non_contiguous=slow),
            reads=reads, writes=writes, dma=True, extra=extra)

    def mm_group(items, reads=(), writes=(), extra=()):
        def fn(t):
            last = None
            for (o, l, r, st, sp) in items:
                last = t.matmul(o, l, r, start=st, stop=sp)
            return last
        return sc.op("t", fn, reads=reads, writes=writes, extra=extra)

    dma("y", const_sb[:], consts_in, writes=["const"])

    ldbuf = [AFr(0, 1024), AFr(4, 1024)]
    for i in range(NT128):
        tb = ldbuf[i % 2]
        tbn = f"ldbuf{i % 2}"
        dma("y", tb, x_in[i * 128:(i + 1) * 128, :], writes=[tbn])
        for h in range(2):
            bank = h
            sc.op("t", (lambda t, tb=tb, h=h, bank=bank: [
                t.transpose(ps[bank][:, c * 128:(c + 1) * 128],
                            tb[:, (4 * h + c) * 128:(4 * h + c + 1) * 128], ident)
                for c in range(4)][-1]),
                reads=[tbn, "const"], writes=[f"psb{bank}"])
            sc.op("v" if h == 0 else "s",
                  (lambda e, i=i, h=h, bank=bank, s=(h == 1):
                   (e.copy if s else e.tensor_copy)(
                       xv[:, 4 * h:4 * h + 4, i * 128:(i + 1) * 128],
                       ps[bank][:, 0:512])),
                  reads=[f"psb{bank}"], writes=[f"x_{i}"])
    sc.fence()

    def prep_transposed(src, scr_t, n_ob, n_ic, quant_scale=None, out_scale=1.0):
        lb = [AFr(0, 1024), AFr(4, 1024)]
        qb = [AFr(8, 1024), AFr(12, 1024)]
        ob_bf = [ABf(16, n_ic * 128), ABf(16 + (n_ic * 128 // 512), n_ic * 128)]
        for ob in range(n_ob):
            tb = lb[ob % 2]
            tbn = f"prep_lb{ob % 2}"
            for half in range(n_ic // 8):
                dma("y", tb, src[ob * 128:(ob + 1) * 128,
                                 half * 1024:(half + 1) * 1024], writes=[tbn])
                for h in range(2):
                    bank = 4 + h
                    sc.op("t", (lambda t, tb=tb, h=h, bank=bank: [
                        t.transpose(ps[bank][:, c * 128:(c + 1) * 128],
                                    tb[:, (4 * h + c) * 128:(4 * h + c + 1) * 128],
                                    ident)
                        for c in range(4)][-1]),
                        reads=[tbn, "const"], writes=[f"psb{bank}"])
                    q = qb[h]
                    qn = f"prep_qb{h}"
                    obf = ob_bf[ob % 2]
                    obn = f"prep_ob{ob % 2}"
                    if quant_scale is not None:
                        qs = float(quant_scale)
                        sc.op("v", (lambda v, bank=bank, q=q: v.tensor_scalar(
                            q[:, 0:512], ps[bank][:, 0:512], qs, -128.0,
                            A.mult, A.max)), reads=[f"psb{bank}"], writes=[qn])
                        sc.op("v", (lambda v, q=q: v.tensor_scalar(
                            q[:, 0:512], q[:, 0:512], 127.0, MAGIC,
                            A.min, A.add)), reads=[qn], writes=[qn])
                        osc = float(out_scale / quant_scale)
                        sc.op("v", (lambda v, q=q, obf=obf, h=h, half=half:
                                    v.tensor_scalar(
                                        obf[:, (half * 8 + 4 * h) * 128:
                                            (half * 8 + 4 * h + 4) * 128],
                                        q[:, 0:512], -MAGIC, osc, A.add, A.mult)),
                              reads=[qn], writes=[obn])
                    else:
                        sc.op("v", (lambda v, bank=bank, obf=obf, h=h, half=half:
                                    v.tensor_copy(
                                        obf[:, (half * 8 + 4 * h) * 128:
                                            (half * 8 + 4 * h + 4) * 128],
                                        ps[bank][:, 0:512])),
                              reads=[f"psb{bank}"], writes=[obn])
            obf = ob_bf[ob % 2]
            obn = f"prep_ob{ob % 2}"
            dma("y", scr_t.ap()[ob].transpose([1, 0, 2]),
                obf[:].rearrange("p (c o) -> p c o", c=n_ic), reads=[obn])
        sc.fence()

    def prep_natural(src, scr_t, n_ic, n_ob):
        cb = ABf(0, n_ob * 128)
        for ic in range(n_ic):
            dma("g", cb[:], src[ic * 128:(ic + 1) * 128, :], writes=["prep_cb"])
            dma("y", scr_t.ap()[:, ic].transpose([1, 0, 2]),
                cb[:].rearrange("p (o c) -> p o c", o=n_ob), reads=["prep_cb"])
        sc.fence()

    def prep_bias(src, dst, n, name, qscale=None, out_scale=1.0):
        dma("y", dst[:, 0:n], src.rearrange("a (o p) -> a p o", p=128),
            writes=[name], slow=True)
        if qscale is not None:
            qs = float(qscale)
            osc = float(out_scale / qscale)
            sc.op("v", lambda v: v.tensor_scalar(
                dst[:, 0:n], dst[:, 0:n], qs, -128.0, A.mult, A.max),
                reads=[name], writes=[name])
            sc.op("v", lambda v: v.tensor_scalar(
                dst[:, 0:n], dst[:, 0:n], 127.0, MAGIC, A.min, A.add),
                reads=[name], writes=[name])
            sc.op("v", lambda v: v.tensor_scalar(
                dst[:, 0:n], dst[:, 0:n], -MAGIC, osc, A.add, A.mult),
                reads=[name], writes=[name])

    if "ffn1" in phases or "ffn2" in phases:
        prep_transposed(Wd["fw1"], scr["w1q"], NH, NB, quant_scale=64.0)
        prep_transposed(Wd["fw2"], scr["w2q"], NB, NH, quant_scale=64.0,
                        out_scale=1.0 / 16.0)
        prep_bias(Wd["fb1"], b1q16, NH, "b1q16", qscale=64.0, out_scale=16.0)
        prep_bias(Wd["fb2"], b2q, NB, "b2q", qscale=64.0, out_scale=1.0)
        sc.fence()
    if "attn_b" in phases:
        for nm in ("bWq", "bWf", "bWi", "bWo"):
            prep_transposed(Wd[nm], scr[nm], NB, NB)
        prep_bias(Wd["bgw"], bgw_sb, NB, "bgw")
        sc.fence()
    if "attn_a" in phases:
        for nm in ("aWq", "aWk", "aWv", "aWo"):
            prep_natural(Wd[nm], scr[nm], NB, NB)
        prep_bias(Wd["abo"], abo_sb, NB, "abo")
        sc.fence()

    ST_OFF = 16384
    stats = arena[:, ST_OFF:ST_OFF + 1024]
    rowb = [arena[32:33, ST_OFF:ST_OFF + 1024],
            arena[64:65, ST_OFF:ST_OFF + 1024]]

    def xnames(t0, t1):
        return [f"x_{i}" for i in range(t0 // 128, t1 // 128)]

    def emit_ln_stats(out_scale):
        sqb = [AFr(48, 512), AFr(50, 512)]
        tmp = [AFr(40, 512), AFr(42, 512), AFr(44, 512)]
        n = NT512 * NB
        k = 0
        for j in range(NT512):
            sel = const_sb[:, CR.off["selcols"] + 8 * j:CR.off["selcols"] + 8 * j + 8]
            for c in range(NB):
                mm_sum = (lambda t, j=j, c=c, sel=sel, k=k: t.matmul(
                    ps[0][0:8, 0:512], sel,
                    xv[:, c, j * 512:(j + 1) * 512],
                    start=(k == 0), stop=(k == n - 1)))
                sc.op("t", mm_sum, reads=xnames(j * 512, (j + 1) * 512) + ["const"],
                      writes=["psb0"])
                sq = sqb[k % 2]
                sqn = f"sqb{k % 2}"
                sc.op("s", (lambda e, j=j, c=c, sq=sq: e.activation(
                    sq[:], xv[:, c, j * 512:(j + 1) * 512], AF.Square)),
                    reads=xnames(j * 512, (j + 1) * 512), writes=[sqn])
                sc.op("t", (lambda t, sq=sq, sel=sel, k=k: t.matmul(
                    ps[1][0:8, 0:512], sel, sq[:],
                    start=(k == 0), stop=(k == n - 1))),
                    reads=[sqn, "const"], writes=["psb1"])
                k += 1
        m, m2, v_ = tmp
        sc.op("v", lambda v: v.tensor_scalar(
            m[0:8, :], ps[0][0:8, 0:512], INVD, None, A.mult),
            reads=["psb0"], writes=["qtarea"])
        sc.op("v", lambda v: v.tensor_tensor(
            m2[0:8, :], m[0:8, :], m[0:8, :], A.mult),
            writes=["qtarea"])
        sc.op("v", lambda v: v.scalar_tensor_tensor(
            v_[0:8, :], ps[1][0:8, 0:512], INVD, m2[0:8, :],
            A.mult, A.subtract),
            reads=["psb1"], writes=["qtarea"])
        sc.op("s", lambda e: e.activation(
            m2[0:8, :], v_[0:8, :], AF.Ln, bias=EPS_LN),
            writes=["qtarea"])
        sc.op("s", lambda e: e.activation(
            v_[0:8, :], m2[0:8, :], AF.Exp, scale=-0.5),
            writes=["qtarea"])
        osf = float(out_scale)
        sc.op("v", lambda v: v.tensor_scalar(
            stats[0:8, 0:512], v_[0:8, :], osf, None, A.mult),
            reads=["qtarea"], writes=["stats"])
        sc.op("v", lambda v: v.scalar_tensor_tensor(
            stats[0:8, 512:1024], m[0:8, :], -osf, v_[0:8, :],
            A.mult, A.mult),
            reads=["qtarea"], writes=["stats"])
        sc.fence()

    RGROUPS = [[2 * b, 2 * b + 1] for b in range(n_cores // 2)]

    def emit_attn_a():
        NCH = T // CHUNK
        ar = Arena()
        ctxacc = ar.f32(17408, 768, "ctxacc")
        ksacc = ar.f32(18176, 12, "ksacc")
        emit_ln_stats(1.0)

        xn = ar.bf16(0, 4096, "xn")
        xnv = xn.rearrange("p (c t) -> p c t", c=NB)
        aqt = ar.f32(2048, 2048, "aqt")
        pmask = cview("chind2")

        def ln_apply_tile(tk):
            rb = rowb[tk % 2]
            rbn = f"rb{tk % 2}"
            dma("y", rb, stats[tk:tk + 1, 0:1024], reads=["stats"], writes=[rbn])
            obase = 32 * (1 + tk % 2)
            orow = ones[obase:obase + 1, 0:128]
            sc.op("t", (lambda t, rb=rb, orow=orow: t.matmul(
                ps[0][0:128, 0:512], orow, rb[0:1, 0:512],
                start=True, stop=True)), reads=[rbn, "const"], writes=["psb0"])
            sc.op("t", (lambda t, rb=rb, orow=orow: t.matmul(
                ps[1][0:128, 0:512], orow, rb[0:1, 512:1024],
                start=True, stop=True)), reads=[rbn, "const"], writes=["psb1"])
            t0 = tk * 512
            for th in range(2):
                xsrc = xv[:, :, t0 + th * 256:t0 + th * 256 + 256]
                sc.op("v", (lambda v, xsrc=xsrc, th=th: v.scalar_tensor_tensor(
                    aqt[:, 0:2048], xsrc, 1.0,
                    ps[0][:, th * 256:(th + 1) * 256].unsqueeze(1)
                    .broadcast_to([128, 8, 256]),
                    A.mult, A.mult)),
                    reads=xnames(t0, t0 + 512) + ["psb0"], writes=["aqt"])
                dst = xnv[:, :, th * 256:(th + 1) * 256]
                sc.op("v", (lambda v, th=th, dst=dst: v.scalar_tensor_tensor(
                    dst, aqt[:, 0:2048], 1.0,
                    ps[1][:, th * 256:(th + 1) * 256].unsqueeze(1)
                    .broadcast_to([128, 8, 256]),
                    A.mult, A.add)),
                    reads=["aqt", "psb1"], writes=["xn"])

        wkvg = ar.bf16(4096, 12288, "wkvg")
        wkvv = wkvg.rearrange("p (w c o) -> p w c o", w=2, c=NB)
        ccst = arena[:, 4096:4888]
        kT = ar.bf16(10240, 3072, "kT")
        kTv = kT.rearrange("p (s o) -> p s o", s=4)
        vT = ar.bf16(11776, 3072, "vT")
        vTv = vT.rearrange("p (s o) -> p s o", s=4)
        elu1 = ar.f32(13312, 768, "elu1")
        elu2 = ar.f32(14080, 768, "elu2")
        kmask = arena[:, 13312:13696].bitcast(BF16)
        ctxo = [ar.bf16(14848, 784, "ctxo0"), ar.bf16(15240, 784, "ctxo1")]

        sc.op("v", lambda v: v.memset(ctxacc[0:64, 0:768], 0.0),
              writes=["ctxacc"])
        sc.op("v", lambda v: v.memset(ksacc[0:64, 0:12], 0.0), writes=["ksacc"])
        sc.op("v", lambda v: v.memset(ctxo[0][0:64, 780:784], 0.0),
              writes=["ctxo0"])
        sc.op("v", lambda v: v.memset(ctxo[1][0:64, 780:784], 0.0),
              writes=["ctxo1"])
        if aa_stop == "none":
            sc.fence()
            return
        for tk in range(NT512 if aa_stop != "prep" else 0):
            ln_apply_tile(tk)
            if aa_stop == "ln":
                continue
            for w, nm in ((0, "aWk"), (1, "aWv")):
                for ob in range(6):
                    dma("y", wkvv[:, w, :, ob * 128:(ob + 1) * 128],
                        scr[nm].ap()[2 + ob].transpose([1, 0, 2]),
                        writes=["wkvg"])
            for t128 in range(4):
                for w, (bankA, bankB) in ((0, (2, 3)), (1, (4, 5))):
                    def mmkv(t, w=w, t128=t128, bankA=bankA, bankB=bankB):
                        last = None
                        for sp, bank, o0, o1 in ((0, bankA, 0, 512),
                                                 (1, bankB, 512, 768)):
                            for c in range(NB):
                                lhsT = xnv[:, c, t128 * 128:(t128 + 1) * 128]
                                last = t.matmul(
                                    ps[bank][:, 0:o1 - o0], lhsT,
                                    wkvv[:, w, c, o0:o1],
                                    start=(c == 0), stop=(c == NB - 1))
                        return last
                    sc.op("t", mmkv, reads=["xn", "wkvg"],
                          writes=[f"psb{bankA}", f"psb{bankB}"])
                for sp, bank, o0, o1 in ((0, 2, 0, 512), (1, 3, 512, 768)):
                    n = o1 - o0
                    sc.op("v", (lambda v, bank=bank, o0=o0, n=n: v.tensor_scalar(
                        elu1[:, o0:o0 + n], ps[bank][:, 0:n], 0.0, None, A.min)),
                        reads=[f"psb{bank}"], writes=["elu1"])
                    sc.op("s", (lambda e, o0=o0, n=n: e.activation(
                        elu2[:, o0:o0 + n], elu1[:, o0:o0 + n], AF.Exp)),
                        reads=["elu1"], writes=["elu2"])
                    sc.op("v", (lambda v, bank=bank, o0=o0, n=n, t128=t128:
                                v.scalar_tensor_tensor(
                                    kTv[:, t128, o0:o0 + n], ps[bank][:, 0:n],
                                    0.0, elu2[:, o0:o0 + n], A.max, A.add)),
                          reads=[f"psb{bank}", "elu2"], writes=["kT"])
                    eng = "v" if sp == 0 else "s"
                    sc.op(eng, (lambda e, bank=bank, o0=o0, n=n, t128=t128,
                                sv=(sp == 1): (e.copy if sv else e.tensor_copy)(
                                    vTv[:, t128, o0:o0 + n],
                                    ps[bank + 2][:, 0:n])),
                          reads=[f"psb{bank + 2}"], writes=["vT"])
                for cpar in range(2 if aa_stop != "kv" else 0):
                    cu = tk * 8 + t128 * 2 + cpar
                    sc.op("v", (lambda v, t128=t128, cpar=cpar: v.tensor_scalar(
                        kmask[:, 0:768], kTv[:, t128, :],
                        pmask[:, cpar:cpar + 1], None, A.mult)),
                        reads=["kT", "const"], writes=["elu1"])
                    def mmctx(t, t128=t128):
                        last = None
                        for gh in range(GLOBAL_HEADS):
                            bank, col = (6, gh * 64) if gh < 8 else \
                                        (7, (gh - 8) * 64)
                            last = t.matmul(
                                ps[bank][0:64, col:col + 64],
                                kmask[:, gh * 64:gh * 64 + 64],
                                vTv[:, t128, gh * 64:gh * 64 + 64],
                                start=True, stop=True)
                        for gh in range(GLOBAL_HEADS):
                            last = t.matmul(
                                ps[7][0:64, 384 + gh:385 + gh],
                                kmask[:, gh * 64:gh * 64 + 64],
                                onesb[:, 0:1],
                                start=True, stop=True)
                        return last
                    sc.op("t", mmctx, reads=["elu1", "vT", "const"],
                          writes=["psb6", "psb7"])
                    if aa_stop == "ctx1":
                        continue
                    cb = ctxo[cu % 2]
                    cbn = f"ctxo{cu % 2}"
                    sc.op("v", (lambda v, cb=cb: v.tensor_copy(
                        cb[0:64, 0:512], ps[6][0:64, 0:512])),
                        reads=["psb6"], writes=[cbn])
                    sc.op("s", (lambda e, cb=cb: e.copy(
                        cb[0:64, 512:768], ps[7][0:64, 0:256])),
                        reads=["psb7"], writes=[cbn])
                    sc.op("s", (lambda e, cb=cb: e.copy(
                        cb[0:64, 768:780], ps[7][0:64, 384:396])),
                        reads=["psb7"], writes=[cbn])
                    sc.op("v", lambda v: v.tensor_tensor(
                        ctxacc[0:64, 0:512], ctxacc[0:64, 0:512],
                        ps[6][0:64, 0:512], A.add),
                        reads=["psb6"], writes=["ctxacc"])
                    sc.op("v", lambda v: v.tensor_tensor(
                        ctxacc[0:64, 512:768], ctxacc[0:64, 512:768],
                        ps[7][0:64, 0:256], A.add),
                        reads=["psb7"], writes=["ctxacc"])
                    sc.op("v", lambda v: v.tensor_tensor(
                        ksacc[0:64, 0:12], ksacc[0:64, 0:12],
                        ps[7][0:64, 384:396], A.add),
                        reads=["psb7"], writes=["ksacc"])
                    if aa_stop != "ctx2":
                        dma("y", scr["ctx"].ap()[cu], cb[0:64, 0:784],
                            reads=[cbn], writes=["scrctx"])

        if aa_stop in ("prep", "1a", "ln", "kv", "ctx1", "ctx2"):
            sc.fence()
            return
        flagC = cview("flags")[0:64, 0:1]
        flagU = cview("flags")[0:64, 1:2]
        sc.op("v", lambda v: v.memset(ccst[0:64, 780:792], 0.0),
              writes=["wkvg"])
        sc.op("v", lambda v: v.tensor_scalar(
            ccst[0:64, 0:768], ctxacc[0:64, 0:768], flagC, None, A.mult),
            reads=["ctxacc", "const"], writes=["wkvg"])
        sc.op("v", lambda v: v.tensor_scalar(
            ccst[0:64, 768:780], ksacc[0:64, 0:12], flagC, None, A.mult),
            reads=["ksacc", "const"], writes=["wkvg"])
        dma("y", cc1_in.ap(), ccst[0:64, 0:792], reads=["wkvg"], writes=["cc1i"])
        if no_coll:
            dma("y", cc1_out.ap(), cc1_in.ap(), reads=["cc1i"], writes=["cc1o"])
        else:
            sc.op("g", lambda g: g.collective_compute(
                "AllReduce", A.add, replica_groups=RGROUPS,
                ins=[cc1_in.ap().opt()], outs=[cc1_out.ap().opt()]),
                reads=["cc1i"], writes=["cc1o"])
        dma("y", ccst[0:64, 0:792], cc1_out.ap(), reads=["cc1o"], writes=["wkvg"])
        sc.op("v", lambda v: v.tensor_scalar(
            ctxacc[0:64, 0:768], ccst[0:64, 0:768], flagU, None, A.mult),
            reads=["wkvg", "const"], writes=["ctxacc"])
        sc.op("v", lambda v: v.tensor_scalar(
            ksacc[0:64, 0:12], ccst[0:64, 768:780], flagU, None, A.mult),
            reads=["wkvg", "const"], writes=["ksacc"])
        sc.fence()
        if aa_stop == "coll":
            return

        ar2 = Arena()
        ar2.f32(17408, 768, "ctxacc")
        ar2.f32(18176, 12, "ksacc")
        xn2 = ar2.bf16(0, 4096, "xn")
        xnv2 = xn2.rearrange("p (c t) -> p c t", c=NB)
        attnb = ar2.bf16(0, 4096, "xn")
        atv = attnb.rearrange("p (c t) -> p c t", c=NB)
        ar2.f32(2048, 2048, "aqt")
        qbf = ar2.bf16(4096, 4096, "qbf")
        qbv = qbf.rearrange("p (c t) -> p c t", c=NB)
        qe = ar2.f32(6144, 1536, "qearea")
        qev = qe.rearrange("p (c t) -> p c t", c=6)
        qsmR = arena[:, 6144:9216].bitcast(BF16)
        qsRv = qsmR.rearrange("p (h t) -> p h t", h=12)
        wob = [ar2.bf16(6144, 1024, "qearea"), ar2.bf16(6656, 1024, "qearea")]
        wovs = [w.rearrange("p (c o) -> p c o", c=NB) for w in wob]
        qsm = ar2.bf16(7680, 3072, "qsm")
        qsv = qsm.rearrange("p (c t) -> p c t", c=6)
        gstrip = arena[:, 2432:3968].bitcast(BF16)
        gsv = gstrip.rearrange("p (h t) -> p h t", h=12)
        qrec = ar2.bf16(9216, 1024, "qrec")
        kloc = [ar2.bf16(9728, 1024, "kloc0"), ar2.bf16(10240, 1024, "kloc1")]
        klv = [k.rearrange("p (c t) -> p c t", c=2) for k in kloc]
        vloc = [ar2.bf16(10752, 1024, "vloc0"), ar2.bf16(11264, 1024, "vloc1")]
        vlv = [v.rearrange("p (c t) -> p c t", c=4) for v in vloc]
        hkv = ar2.bf16(11776, 512, "halokv")
        hkvv = hkv.rearrange("p (c t) -> p c t", c=4)
        wklb = ar2.bf16(12032, 2048, "wklarea")
        wklv = wklb.rearrange("p (o c t) -> p o c t", o=2, c=NB)
        scp = ar2.bf16(12032, 512, "wklarea")
        scpv = scp.rearrange("p (h t) -> p h t", h=4)
        scs = ar2.bf16(12288, 512, "wklarea")
        scsv = scs.rearrange("p (h t) -> p h t", h=4)
        recrow = ar2.f32(12544, 512, "wklarea")
        wvlb = ar2.bf16(13056, 2048, "wvlarea")
        wvlv = wvlb.rearrange("p (c t) -> p c t", c=NB)
        ctxrb = [ar2.bf16(13056, 784, "wvlarea"), ar2.bf16(13448, 784, "wvlarea")]
        wqb = [ar2.bf16(14080, 1024, "wqb0"), ar2.bf16(14592, 1024, "wqb1")]
        wqvs = [w.rearrange("p (c o) -> p c o", c=NB) for w in wqb]
        hfeat = ar2.f32(15104, 1024, "halof")
        hfv = hfeat.rearrange("p (c t) -> p c t", c=NB)
        rtile = ar2.f32(15104, 512, "halof")
        qm = [ar2.bf16(15616, 256, "halof"), ar2.bf16(15744, 256, "halof")]
        qmv = [q.rearrange("p (c t) -> p c t", c=2) for q in qm]
        ltile = ar2.bf16(15872, 512, "halof")
        hst = ar2.f32(16128, 256, "halost")
        hxn = ar2.bf16(3072, 1024, "aqt")
        hxv = hxn.rearrange("p (c t) -> p c t", c=NB)
        ctxs_bf = arena[:, 2048:2432].bitcast(BF16)

        for tk in range(NT512):
            ln_apply_tile(tk)
            for ob in range(2):
                dma("y", wklv[:, ob], scr["aWk"].ap()[ob].transpose([1, 0, 2]),
                    writes=["wklarea"])
                dma("y", wvlv[:, :, ob * 128:(ob + 1) * 128],
                    scr["aWv"].ap()[ob].transpose([1, 0, 2]),
                    writes=["wvlarea"])
            for o in range(NB):
                wvv = wqvs[o % 2]
                wn = f"wqb{o % 2}"
                dma("y", wvv, scr["aWq"].ap()[o].transpose([1, 0, 2]),
                    writes=[wn])
                bank = 2 + o % 2
                sc.op("t", (lambda t, wvv=wvv, bank=bank: [
                    t.matmul(ps[bank][:, 0:512], wvv[:, c, :], xnv2[:, c, :],
                             start=(c == 0), stop=(c == NB - 1))
                    for c in range(NB)][-1]),
                    reads=[wn, "xn"], writes=[f"psb{bank}"])
                eng = "v" if o % 2 == 0 else "s"
                sc.op(eng, (lambda e, bank=bank, o=o, sv=(o % 2 == 1):
                            (e.copy if sv else e.tensor_copy)(
                                qbv[:, o, :], ps[bank][:, 0:512])),
                      reads=[f"psb{bank}"], writes=["qbf"])
            for ob in range(2):
                bank = 2 + ob
                sc.op("t", (lambda t, ob=ob, bank=bank: [
                    t.matmul(ps[bank][:, 0:512], wklv[:, ob, c, :],
                             xnv2[:, c, :], start=(c == 0), stop=(c == NB - 1))
                    for c in range(NB)][-1]),
                    reads=["wklarea", "xn"], writes=[f"psb{bank}"])
                sc.op("v", (lambda v, bank=bank, ob=ob, tk=tk: v.tensor_copy(
                    klv[tk % 2][:, ob, :], ps[bank][:, 0:512])),
                    reads=[f"psb{bank}"], writes=[f"kloc{tk % 2}"])
            for t128 in range(4):
                bank = 2 + t128 % 2
                sc.op("t", (lambda t, t128=t128, bank=bank: [
                    t.matmul(ps[bank][:, 0:256],
                             xnv2[:, c, t128 * 128:(t128 + 1) * 128],
                             wvlv[:, c, 0:256],
                             start=(c == 0), stop=(c == NB - 1))
                    for c in range(NB)][-1]),
                    reads=["wvlarea", "xn"], writes=[f"psb{bank}"])
                sc.op("v", (lambda v, bank=bank, t128=t128, tk=tk: v.tensor_copy(
                    vlv[tk % 2][:, t128, :], ps[bank][:, 0:256])),
                    reads=[f"psb{bank}"], writes=[f"vloc{tk % 2}"])
            if tk == 0:
                dma("y", hfeat[:, 0:1024].rearrange("p (a b) -> p a b", a=2),
                    halo_in.rearrange("(a t) d -> a t d", a=2), slow=True,
                    writes=["halof"])
                sc.op("t", (lambda t: [
                    t.transpose(ps[4][:, c * 128:(c + 1) * 128],
                                hfeat[:, c * 128:(c + 1) * 128], ident)
                    for c in range(4)][-1]),
                    reads=["halof", "const"], writes=["psb4"])
                sc.op("t", (lambda t: [
                    t.transpose(ps[5][:, (c - 4) * 128:(c - 3) * 128],
                                hfeat[:, c * 128:(c + 1) * 128], ident)
                    for c in range(4, 8)][-1]),
                    reads=["halof", "const"], writes=["psb5"])
                sc.op("v", lambda v: v.tensor_copy(hfeat[:, 0:512],
                                                   ps[4][:, 0:512]),
                      reads=["psb4"], writes=["halof"])
                sc.op("v", lambda v: v.tensor_copy(hfeat[:, 512:1024],
                                                   ps[5][:, 0:512]),
                      reads=["psb5"], writes=["halof"])
                sc.op("t", (lambda t: [
                    t.matmul(ps[4][0:1, 0:128], ones[:, 0:1],
                             hfv[:, c, :], start=(c == 0), stop=(c == NB - 1))
                    for c in range(NB)][-1]),
                    reads=["halof", "const"], writes=["psb4"])
                sc.op("s", (lambda e: [
                    e.activation(aqt[:, c * 128:(c + 1) * 128], hfv[:, c, :],
                                 AF.Square) for c in range(NB)][-1]),
                    reads=["halof"], writes=["aqt"])
                sc.op("t", (lambda t: [
                    t.matmul(ps[5][0:1, 0:128], ones[:, 0:1],
                             aqt[:, c * 128:(c + 1) * 128],
                             start=(c == 0), stop=(c == NB - 1))
                    for c in range(NB)][-1]),
                    reads=["aqt", "const"], writes=["psb5"])
                sc.op("v", lambda v: v.tensor_scalar(
                    hst[0:1, 0:128], ps[4][0:1, 0:128], INVD, None, A.mult),
                    reads=["psb4"], writes=["halost"])
                sc.op("v", lambda v: v.tensor_tensor(
                    hst[0:1, 128:256], hst[0:1, 0:128], hst[0:1, 0:128],
                    A.mult), reads=["halost"], writes=["halost"])
                sc.op("v", lambda v: v.scalar_tensor_tensor(
                    hst[0:1, 128:256], ps[5][0:1, 0:128], INVD,
                    hst[0:1, 128:256], A.mult, A.subtract),
                    reads=["psb5", "halost"], writes=["halost"])
                sc.op("s", lambda e: e.activation(
                    hst[0:1, 128:256], hst[0:1, 128:256], AF.Ln, bias=EPS_LN),
                    reads=["halost"], writes=["halost"])
                sc.op("s", lambda e: e.activation(
                    hst[0:1, 128:256], hst[0:1, 128:256], AF.Exp, scale=-0.5),
                    reads=["halost"], writes=["halost"])
                sc.op("v", lambda v: v.scalar_tensor_tensor(
                    hst[0:1, 0:128], hst[0:1, 0:128], -1.0,
                    hst[0:1, 128:256], A.mult, A.mult),
                    reads=["halost"], writes=["halost"])
                sc.op("t", lambda t: t.matmul(
                    ps[4][0:128, 0:128], ones[0:1, 0:128], hst[0:1, 128:256],
                    start=True, stop=True),
                    reads=["halost", "const"], writes=["psb4"])
                sc.op("t", lambda t: t.matmul(
                    ps[5][0:128, 0:128], ones[0:1, 0:128], hst[0:1, 0:128],
                    start=True, stop=True),
                    reads=["halost", "const"], writes=["psb5"])
                sc.op("v", lambda v: v.scalar_tensor_tensor(
                    aqt[:, 0:1024], hfeat[:, 0:1024], 1.0,
                    ps[4][:, 0:128].unsqueeze(1).broadcast_to([128, 8, 128]),
                    A.mult, A.mult), reads=["halof", "psb4"], writes=["aqt"])
                sc.op("v", lambda v: v.scalar_tensor_tensor(
                    hxn[:, 0:1024], aqt[:, 0:1024], 1.0,
                    ps[5][:, 0:128].unsqueeze(1).broadcast_to([128, 8, 128]),
                    A.mult, A.add), reads=["aqt", "psb5"], writes=["aqt"])
                for ob in range(2):
                    bank = 4 + ob
                    sc.op("t", (lambda t, ob=ob, bank=bank: [
                        t.matmul(ps[bank][:, 0:128], wklv[:, ob, c, :],
                                 hxv[:, c, :], start=(c == 0),
                                 stop=(c == NB - 1)) for c in range(NB)][-1]),
                        reads=["wklarea", "aqt"], writes=[f"psb{bank}"])
                    sc.op("v", (lambda v, bank=bank, ob=ob: v.tensor_copy(
                        hkvv[:, ob, :], ps[bank][:, 0:128])),
                        reads=[f"psb{bank}"], writes=["halokv"])
                sc.op("t", (lambda t: [
                    t.matmul(ps[4][:, 0:256], hxv[:, c, :], wvlv[:, c, 0:256],
                             start=(c == 0), stop=(c == NB - 1))
                    for c in range(NB)][-1]),
                    reads=["wvlarea", "aqt"], writes=["psb4"])
                sc.op("v", (lambda v: v.tensor_copy(
                    hkv[:, 256:512], ps[4][:, 0:256])),
                    reads=["psb4"], writes=["halokv"])
            for th in range(2):
                c0 = th * 256
                sc.op("s", (lambda e, c0=c0: e.activation(
                    qe[:, 0:1536], qbv[:, 2:8, c0:c0 + 256], AF.Exp)),
                    reads=["qbf"], writes=["qearea"])
                sc.op("t", (lambda t, c0=c0, th=th: [
                    t.matmul(ps[4][0:12, th * 256:th * 256 + 256],
                             const_sb[:, CR.off["hsum12"] + 12 * cb:
                                      CR.off["hsum12"] + 12 * cb + 12],
                             qev[:, cb, :], start=(cb == 0), stop=(cb == 5))
                    for cb in range(6)][-1]),
                    reads=["qearea", "const"], writes=["psb4"])
                def _recip(v, th=th):
                    with nc.allow_low_precision("bf16 softmax recip"):
                        return v.reciprocal(
                            qrec[0:12, th * 256:th * 256 + 256],
                            ps[4][0:12, th * 256:th * 256 + 256])
                sc.op("v", _recip, reads=["psb4"], writes=["qrec"])
                for cb in range(6):
                    sc.op("t", (lambda t, cb=cb, th=th: t.matmul(
                        ps[5][:, 0:256],
                        const_sb[0:12, CR.off["hexp12"]:CR.off["hexp12"] + 384]
                        .bitcast(BF16)[:, 128 * cb:128 * cb + 128],
                        qrec[0:12, th * 256:th * 256 + 256],
                        start=True, stop=True)),
                        reads=["qrec", "const"], writes=["psb5"])
                    sc.op("v", (lambda v, cb=cb, th=th, c0=c0: v.tensor_tensor(
                        qsv[:, cb, c0:c0 + 256], qev[:, cb, :],
                        ps[5][:, 0:256], A.mult)),
                        reads=["qearea", "psb5"], writes=["qsm"])
            qsR4 = qsmR.rearrange("p (h q t) -> p h q t", h=6, q=2)
            dma("y", qsR4[0:64, :, 0, :], qsm[0:64, 0:3072]
                .rearrange("p (c t) -> p c t", c=6), reads=["qsm"],
                writes=["qearea", "qsm"])
            dma("y", qsR4[0:64, :, 1, :], qsm[64:128, 0:3072]
                .rearrange("p (c t) -> p c t", c=6), reads=["qsm"],
                writes=["qearea", "qsm"])
            for b in range(4):
                first = (tk == 0 and b == 0)
                if b == 0:
                    if tk == 0:
                        kprev = hkvv[:, 0:2, :]
                        vprev = hkv[:, 256:512]
                        pn = ["halokv"]
                    else:
                        kprev = klv[(tk - 1) % 2][:, :, 384:512]
                        vprev = vloc[(tk - 1) % 2][:, 3 * 256:4 * 256]
                        pn = [f"kloc{(tk - 1) % 2}", f"vloc{(tk - 1) % 2}"]
                else:
                    kprev = klv[tk % 2][:, :, (b - 1) * 128:b * 128]
                    vprev = vloc[tk % 2][:, (b - 1) * 256:b * 256]
                    pn = [f"kloc{tk % 2}", f"vloc{tk % 2}"]
                kself = klv[tk % 2][:, :, b * 128:(b + 1) * 128]
                vself = vloc[tk % 2][:, b * 256:(b + 1) * 256]
                sn = [f"kloc{tk % 2}", f"vloc{tk % 2}"]
                for par in range(2):
                    sc.op("v", (lambda v, par=par, b=b: v.tensor_scalar(
                        qmv[par][:, :, :], qbv[:, 0:2, b * 128:(b + 1) * 128],
                        pmask[:, par:par + 1], None, A.mult)),
                        reads=["qbf", "const"], writes=["halof"])
                def mmdots(t, kprev=kprev, kself=kself, b=b):
                    last = None
                    for h in range(4):
                        par, ob = h % 2, h // 2
                        if kprev.ndim == 3:
                            kp = kprev[:, ob, :]
                        else:
                            kp = kprev.rearrange("p (o t) -> p o t", o=2)[:, ob, :]
                        last = t.matmul(
                            ps[2][:, h * 128:(h + 1) * 128], kp,
                            qmv[par][:, ob, :], start=True, stop=True)
                        last = t.matmul(
                            ps[3][:, h * 128:(h + 1) * 128],
                            kself[:, ob, :],
                            qmv[par][:, ob, :], start=True, stop=True)
                    return last
                sc.op("t", mmdots, reads=pn + sn + ["halof"],
                      writes=["psb2", "psb3"])
                if first:
                    mprev = cview("mask_prev0").unsqueeze(1).broadcast_to(
                        [128, 4, 128])
                    sc.op("v", (lambda v, mprev=mprev: v.scalar_tensor_tensor(
                        ps[2][:, 0:512].rearrange("p (h t) -> p h t", h=4),
                        ps[2][:, 0:512].rearrange("p (h t) -> p h t", h=4),
                        0.125, mprev, A.mult, A.add)),
                        reads=["psb2", "const"], writes=["psb2"])
                else:
                    sc.op("v", lambda v: v.tensor_scalar(
                        ps[2][:, 0:512], ps[2][:, 0:512], 0.125, None, A.mult),
                        reads=["psb2"], writes=["psb2"])
                msf = cview("mask_self").unsqueeze(1).broadcast_to([128, 4, 128])
                sc.op("v", (lambda v, msf=msf: v.scalar_tensor_tensor(
                    ps[3][:, 0:512].rearrange("p (h t) -> p h t", h=4),
                    ps[3][:, 0:512].rearrange("p (h t) -> p h t", h=4),
                    0.125, msf, A.mult, A.add)),
                    reads=["psb3", "const"], writes=["psb3"])
                sc.op("s", lambda e: e.activation(
                    scp[:, 0:512], ps[2][:, 0:512], AF.Exp),
                    reads=["psb2"], writes=["wklarea"])
                sc.op("s", lambda e: e.activation(
                    scs[:, 0:512], ps[3][:, 0:512], AF.Exp),
                    reads=["psb3"], writes=["wklarea"])
                sc.op("t", lambda t: [
                    t.matmul(ps[5][0:1, 0:512], onesb[:, 0:1], scp[:, 0:512],
                             start=True, stop=False),
                    t.matmul(ps[5][0:1, 0:512], onesb[:, 0:1], scs[:, 0:512],
                             start=False, stop=True)][-1],
                    reads=["wklarea", "const"], writes=["psb5"])
                sc.op("v", lambda v: v.reciprocal(
                    recrow[0:1, 0:512], ps[5][0:1, 0:512]),
                    reads=["psb5"], writes=["wklarea"])
                def mmav(t, vprev=vprev, vself=vself):
                    last = None
                    for h in range(4):
                        out = ps[4][0:64, h * 128:(h + 1) * 128]
                        last = t.matmul(out, vprev[:, h * 64:(h + 1) * 64],
                                        scpv[:, h, :], start=True, stop=False)
                        last = t.matmul(out, vself[:, h * 64:(h + 1) * 64],
                                        scsv[:, h, :], start=False, stop=True)
                    return last
                sc.op("t", mmav, reads=pn + sn + ["wklarea"], writes=["psb4"])
                def mmrec(t):
                    last = None
                    for h in range(4):
                        last = t.matmul(
                            ps[5][0:64, h * 128:(h + 1) * 128],
                            ones[0:1, 0:64],
                            recrow[0:1, h * 128:(h + 1) * 128],
                            start=True, stop=True)
                    return last
                sc.op("t", mmrec, reads=["wklarea", "const"], writes=["psb5"])
                sc.op("s", lambda e: e.copy(rtile[0:64, 0:512],
                                            ps[5][0:64, 0:512]),
                      reads=["psb5"], writes=["halof"])
                sc.op("v", (lambda v: v.tensor_tensor(
                    ltile[0:64, 0:512], ps[4][0:64, 0:512],
                    rtile[0:64, 0:512], A.mult)),
                    reads=["psb4", "halof"], writes=["halof"])
                lt4 = ltile[0:64, 0:512].rearrange("p (h q t) -> p h q t",
                                                    h=2, q=2)
                dma("y", atv[0:64, 0:2, b * 128:(b + 1) * 128],
                    lt4[:, :, 0, :], reads=["halof"], writes=["xn"])
                dma("y", atv[64:128, 0:2, b * 128:(b + 1) * 128],
                    lt4[:, :, 1, :], reads=["halof"], writes=["xn"])
            for cl in range(8):
                cu = tk * 8 + cl
                crb = ctxrb[cu % 2]
                crn = "wvlarea"
                dma("y", crb[0:64, 0:784], scr["ctx"].ap()[cu],
                    reads=["scrctx"], writes=[crn])
                sc.op("v", lambda v: v.tensor_scalar(
                    arena[0:64, 18188:18200], ksacc[0:64, 0:12], EPS_CTX,
                    None, A.add), reads=["ksacc"], writes=["kstmp"])
                sc.op("v", lambda v: v.reciprocal(
                    arena[0:64, 18188:18200], arena[0:64, 18188:18200]),
                    reads=["kstmp"], writes=["kstmp"])
                sc.op("v", lambda v: v.tensor_tensor(
                    ctxs_bf[0:64, 0:768].rearrange("p (h e) -> p h e", h=12),
                    ctxacc[0:64, 0:768].rearrange("p (h e) -> p h e", h=12),
                    arena[0:64, 18188:18200].unsqueeze(2)
                    .broadcast_to([64, 12, 64]), A.mult),
                    reads=["ctxacc", "kstmp"], writes=["aqt"])
                def mmglob(t, cl=cl):
                    last = None
                    for gh in range(GLOBAL_HEADS):
                        bank, col = (4, gh * 64) if gh < 8 else \
                                    (5, (gh - 8) * 64)
                        last = t.matmul(
                            ps[bank][0:64, col:col + 64],
                            ctxs_bf[0:64, gh * 64:gh * 64 + 64],
                            qsRv[0:64, gh, cl * 64:(cl + 1) * 64],
                            start=True, stop=True)
                    return last
                sc.op("t", mmglob, reads=["aqt", "qearea", "qsm"],
                      writes=["psb4", "psb5"])
                ch = cl % 4
                sc.op("v", (lambda v, ch=ch: v.tensor_copy(
                    gsv[0:64, 0:8, ch * 64:(ch + 1) * 64],
                    ps[4][0:64, 0:512].rearrange("p (h e) -> p h e", h=8))),
                    reads=["psb4"], writes=["aqt"])
                sc.op("s", (lambda e, ch=ch: e.copy(
                    gsv[0:64, 8:12, ch * 64:(ch + 1) * 64],
                    ps[5][0:64, 0:256].rearrange("p (h e) -> p h e", h=4))),
                    reads=["psb5"], writes=["aqt"])
                sc.op("v", (lambda v, crb=crb: v.tensor_tensor(
                    ksacc[0:64, 0:12], ksacc[0:64, 0:12],
                    crb[0:64, 768:780], A.add)),
                    reads=[crn], writes=["ksacc"])
                sc.op("v", (lambda v, crb=crb: v.tensor_tensor(
                    ctxacc[0:64, 0:768], ctxacc[0:64, 0:768],
                    crb[0:64, 0:768], A.add)),
                    reads=[crn], writes=["ctxacc"])
                if cl % 4 == 3:
                    th0 = (cl // 4) * 256
                    gs4 = gstrip.rearrange("p (h q t) -> p h q t", h=6, q=2)
                    dma("y", atv[0:64, 2:8, th0:th0 + 256], gs4[0:64, :, 0, :],
                        reads=["aqt"], writes=["xn"])
                    dma("y", atv[64:128, 2:8, th0:th0 + 256], gs4[0:64, :, 1, :],
                        reads=["aqt"], writes=["xn"])
            t0 = tk * 512
            for o in range(NB):
                wb = wovs[o % 2]
                wn = "qearea"
                dma("y", wb, scr["aWo"].ap()[o].transpose([1, 0, 2]),
                    writes=[wn])
                bank = 2 + o % 2
                sc.op("t", (lambda t, wb=wb, bank=bank: [
                    t.matmul(ps[bank][:, 0:512], wb[:, c, :], atv[:, c, :],
                             start=(c == 0), stop=(c == NB - 1))
                    for c in range(NB)][-1]),
                    reads=[wn, "xn"], writes=[f"psb{bank}"])
                sc.op("v", (lambda v, bank=bank, o=o, t0=t0:
                            v.scalar_tensor_tensor(
                                xv[:, o, t0:t0 + 512], ps[bank][:, 0:512],
                                abo_sb[:, o:o + 1], xv[:, o, t0:t0 + 512],
                                A.add, A.add)),
                      reads=[f"psb{bank}", "abo"] + xnames(t0, t0 + 512),
                      writes=xnames(t0, t0 + 512))
        sc.fence()

    def emit_attn_b():
        arb = Arena()
        zc = arb.f32(0, 2048, "bz")
        zv = zc.rearrange("p (s f) -> p s f", s=2)
        ngT = zv
        ccs2 = arena[:, 0:1024]
        spc = arb.f32(2048, 2048, "bsp")
        spv = spc.rearrange("p (s f) -> p s f", s=2)
        aqt3 = spc
        xn3 = arb.bf16(4096, 2048, "xn")
        xnv3 = xn3.rearrange("p (c t) -> p c t", c=NB)
        bq = arb.bf16(5120, 2048, "bq")
        bqv = bq.rearrange("p (c t) -> p c t", c=NB)
        bkt = arb.bf16(6144, 2048, "bkt")
        bktv = bkt.rearrange("p (s f) -> p s f", s=2)
        bent = arb.bf16(7168, 2048, "bent")
        bentv = bent.rearrange("p (s f) -> p s f", s=2)
        bepv = bent.rearrange("p (c t) -> p c t", c=NB)
        bkf = arb.bf16(8192, 2048, "bkf")
        bkfv = bkf.rearrange("p (c t) -> p c t", c=NB)
        bvt = arb.bf16(9216, 2048, "bvt")
        bvtv = bvt.rearrange("p (s d) -> p s d", s=2)
        bog = arb.bf16(10240, 4096, "bog")
        bogv = bog.rearrange("p (c t) -> p c t", c=NB)
        ngFs = zc.rearrange("p (s c t) -> p s c t", s=2, c=NB)
        ngtots = arena[:, 16048:16080]
        ngtv = ngtots.rearrange("p (c u) -> p c u", c=NB)
        bqps = [arb.bf16(12288, 512, "bqps0"), arb.bf16(12544, 512, "bqps1")]
        bsA = arb.bf16(12800, 1024, "bsA")
        bw = [arb.bf16(13312, 2048, "bw0"), arb.bf16(14336, 2048, "bw1")]
        bwv = [w.rearrange("p (c o) -> p c o", c=NB) for w in bw]
        bsq = arb.f32(15360, 512, "bsq")
        rmsrow = arena[32:33, 15360:15872]
        EBB_OFF = {(0, 0): 17408, (0, 1): 17664, (1, 0): 17920, (1, 1): 16080}
        for _o in EBB_OFF.values():
            arb.f32(_o, 256, "ebb")

        def ebb(slot, fh):
            o = EBB_OFF[(slot, fh)]
            return arena[:, o:o + 256].bitcast(BF16)[0:2, 0:512]
        bD = arb.f32(15872, 32, "bD")
        bDv = bD.rearrange("p (c u) -> p c u", c=NB)
        bngc = arb.f32(15904, 8, "bngc")
        bngp = arb.f32(15912, 8, "bngp")
        bA = [arb.bf16(15920, 128, "bA0"), arb.bf16(15984, 128, "bA1")]
        pmask = cview("chind2")
        NT256 = T // 256
        emit_ln_stats(1.0)

        def ln_apply(t0, w, dstv, rbslot):
            j, c0 = t0 // 512, t0 % 512
            rb = rowb[rbslot % 2]
            rbn = f"rb{rbslot % 2}"
            src = stats[j:j + 1, :].rearrange("p (q c) -> p q c", q=2)[:, :, c0:c0 + w]
            dma("y", rb[0:1, 0:2 * w].rearrange("p (q c) -> p q c", q=2), src,
                reads=["stats"], writes=[rbn])
            obase = 32 * (1 + rbslot % 2)
            orow = ones[obase:obase + 1, 0:128]
            sc.op("t", (lambda t, rb=rb, orow=orow, w=w: t.matmul(
                ps[6][0:128, 0:w], orow, rb[0:1, 0:w],
                start=True, stop=True)), reads=[rbn, "const"], writes=["psb6"])
            sc.op("t", (lambda t, rb=rb, orow=orow, w=w: t.matmul(
                ps[7][0:128, 0:w], orow, rb[0:1, w:2 * w],
                start=True, stop=True)), reads=[rbn, "const"], writes=["psb7"])
            sc.op("v", (lambda v, t0=t0, w=w: v.scalar_tensor_tensor(
                aqt3[:, 0:8 * w], xv[:, :, t0:t0 + w], 1.0,
                ps[6][:, 0:w].unsqueeze(1).broadcast_to([128, 8, w]),
                A.mult, A.mult)),
                reads=xnames(t0, t0 + w) + ["psb6"], writes=["bsp"])
            sc.op("v", (lambda v, dstv=dstv, w=w: v.scalar_tensor_tensor(
                dstv[:, :, 0:w], aqt3[:, 0:8 * w], 1.0,
                ps[7][:, 0:w].unsqueeze(1).broadcast_to([128, 8, w]),
                A.mult, A.add)),
                reads=["bsp", "psb7"], writes=["xn"])

        def emit_group_out(g0):
            for c in range(NB):
                sc.op("s", (lambda e, c=c: e.activation(
                    bsq[:, 0:512], bogv[:, c, :], AF.Square)),
                    reads=["bog"], writes=["bsq"])
                sc.op("t", (lambda t, c=c: t.matmul(
                    ps[6][0:1, 0:512], ones[:, 0:1], bsq[:, 0:512],
                    start=(c == 0), stop=(c == NB - 1))),
                    reads=["bsq", "const"], writes=["psb6"])
            sc.op("v", lambda v: v.tensor_scalar(
                rmsrow[0:1, 0:512], ps[6][0:1, 0:512], INVD, None, A.mult),
                reads=["psb6"], writes=["bsq"])
            sc.op("s", lambda e: e.activation(
                rmsrow[0:1, 0:512], rmsrow[0:1, 0:512], AF.Ln, bias=EPS_RMS),
                reads=["bsq"], writes=["bsq"])
            sc.op("s", lambda e: e.activation(
                rmsrow[0:1, 0:512], rmsrow[0:1, 0:512], AF.Exp, scale=-0.5),
                reads=["bsq"], writes=["bsq"])
            sc.op("t", lambda t: t.matmul(
                ps[7][0:128, 0:512], ones[32:33, 0:128], rmsrow[0:1, 0:512],
                start=True, stop=True), reads=["bsq", "const"], writes=["psb7"])
            for c in range(NB):
                sc.op("v", (lambda v, c=c: v.scalar_tensor_tensor(
                    bogv[:, c, :], bogv[:, c, :], bgw_sb[:, c:c + 1],
                    ps[7][:, 0:512], A.mult, A.mult)),
                    reads=["psb7", "bgw"], writes=["bog"])
            for o in range(NB):
                wv, wn = wload("bWo", [o])
                bank = 4 + o % 2
                sc.op("t", (lambda t, wv=wv, bank=bank: [
                    t.matmul(ps[bank][:, 0:512], wv[:, c, 0:128], bogv[:, c, :],
                             start=(c == 0), stop=(c == NB - 1))
                    for c in range(NB)][-1]),
                    reads=[wn, "bog"], writes=[f"psb{bank}"])
                sc.op("v", (lambda v, bank=bank, o=o, g0=g0: v.tensor_tensor(
                    xv[:, o, g0:g0 + 512], ps[bank][:, 0:512],
                    xv[:, o, g0:g0 + 512], A.add)),
                    reads=[f"psb{bank}"] + xnames(g0, g0 + 512),
                    writes=xnames(g0, g0 + 512))

        sc.op("v", lambda v: v.memset(S_state[:, :], 0.0), writes=["Sa", "Sb"])
        sc.op("v", lambda v: v.memset(bngc[:, 0:8], 0.0), writes=["bngc"])
        wcnt = [0]

        def wload(nm, obs):
            i = wcnt[0] % 2
            wcnt[0] += 1
            wn = f"bw{i}"
            for k, ob in enumerate(obs):
                dma("y", bwv[i][:, :, k * 128:(k + 1) * 128],
                    scr[nm].ap()[ob].transpose([1, 0, 2]), writes=[wn])
            return bwv[i][:, :, 0:128 * len(obs)], wn

        for tk in range(NT256):
            t0 = tk * 256
            ln_apply(t0, 256, xnv3, tk)
            for nm, which in (("bWf", "z"), ("bWi", "v")):
                for og in range(4):
                    wv, wn = wload(nm, [og * 2, og * 2 + 1])
                    for slot in range(2):
                        bank = 2 + slot
                        def mmz(t, wv=wv, slot=slot, bank=bank):
                            last = None
                            for c in range(NB):
                                last = t.matmul(
                                    ps[bank][:, 0:256],
                                    xnv3[:, c, slot * 128:(slot + 1) * 128],
                                    wv[:, c, :],
                                    start=(c == 0), stop=(c == NB - 1))
                            return last
                        dstv_ = zv if which == "z" else bvtv
                        dn = "bz" if which == "z" else "bvt"
                        sc.op("t", mmz, reads=["xn", wn], writes=[f"psb{bank}"])
                        sc.op("v" if slot == 0 else "s",
                              (lambda e, slot=slot, og=og, bank=bank,
                               dstv_=dstv_, sv=(slot == 1):
                               (e.copy if sv else e.tensor_copy)(
                                  dstv_[:, slot, og * 256:(og + 1) * 256],
                                  ps[bank][:, 0:256])),
                              reads=[f"psb{bank}"], writes=[dn])
            for slot in range(2):
                sc.op("s", (lambda e, slot=slot: e.activation(
                    spv[:, slot, :], zv[:, slot, :], AF.Exp, scale=-1.0)),
                    reads=["bz"], writes=["bsp"])
                sc.op("s", (lambda e, slot=slot: e.activation(
                    spv[:, slot, :], spv[:, slot, :], AF.Ln, bias=1.0)),
                    reads=["bsp"], writes=["bsp"])
                sc.op("s", (lambda e, slot=slot: e.activation(
                    bentv[:, slot, :], spv[:, slot, :], AF.Exp, scale=-1.0)),
                    reads=["bsp"], writes=["bent"])
                sc.op("v", (lambda v, slot=slot: v.tensor_scalar(
                    bktv[:, slot, :], bentv[:, slot, :], -1.0, 1.0,
                    A.mult, A.add)),
                    reads=["bent"], writes=["bkt"])
            kubuf = bsA
            for slot in range(2):
                sc.op("t", (lambda t, slot=slot: [
                    t.matmul(ps[4 + fh][:, 0:512], cview("tri64"),
                             spv[:, slot, fh * 512:(fh + 1) * 512],
                             start=True, stop=True)
                    for fh in range(2)][-1]),
                    reads=["bsp", "const"], writes=["psb4", "psb5"])
                for fh in range(2):
                    sc.op("s", (lambda e, slot=slot, fh=fh: e.activation(
                        bentv[:, slot, fh * 512:(fh + 1) * 512],
                        ps[4 + fh][:, 0:512], AF.Exp)),
                        reads=[f"psb{4 + fh}"], writes=["bent"])
                    sc.op("v", (lambda v, slot=slot, fh=fh: v.tensor_copy(
                        ngT[:, slot, fh * 512:(fh + 1) * 512],
                        ps[4 + fh][:, 0:512])),
                        reads=[f"psb{4 + fh}"], writes=["bz"])
                sc.op("v", (lambda v, slot=slot: v.tensor_tensor(
                    bktv[:, slot, :], bktv[:, slot, :], bentv[:, slot, :],
                    A.mult)), reads=["bent"], writes=["bkt"])
                for half in range(2):
                    bank = 4 + half
                    sc.op("t", (lambda t, slot=slot, half=half, bank=bank: [
                        t.transpose(ps[bank][:, i * 64:(i + 1) * 64]
                                    .bitcast(BF16),
                                    bktv[:, slot, (half * 4 + i) * 128:
                                         (half * 4 + i + 1) * 128], identb)
                        for i in range(4)][-1]),
                        reads=["bkt", "const"], writes=[f"psb{bank}"])
                    sc.op("v", (lambda v, slot=slot, half=half, bank=bank:
                                v.tensor_copy(
                                    bkfv[:, half * 4:half * 4 + 4,
                                         slot * 128:(slot + 1) * 128],
                                    ps[bank][:, 0:256].bitcast(BF16)
                                    .rearrange("p (c t) -> p c t", c=4))),
                          reads=[f"psb{bank}"], writes=["bkf"])
                for fh in range(2):
                    sc.op("t", (lambda t, slot=slot, fh=fh: t.matmul(
                        ps[6 + fh][0:2, 0:512], pmask[:, 0:2],
                        spv[:, slot, fh * 512:(fh + 1) * 512],
                        start=True, stop=True)),
                        reads=["bsp", "const"], writes=[f"psb{6 + fh}"])
                    sc.op("s", (lambda e, slot=slot, fh=fh: e.activation(
                        ebb(slot, fh), ps[6 + fh][0:2, 0:512], AF.Exp,
                        scale=-1.0)),
                        reads=[f"psb{6 + fh}"], writes=["ebb"])

            for slot in range(2):
                for half in range(2):
                    bank = 4 + half
                    sc.op("t", (lambda t, slot=slot, half=half, bank=bank: [
                        t.transpose(ps[bank][:, i * 128:(i + 1) * 128],
                                    ngT[:, slot, (half * 4 + i) * 128:
                                        (half * 4 + i + 1) * 128], ident)
                        for i in range(4)][-1]),
                        reads=["bz", "const"], writes=[f"psb{bank}"])
                    sc.op("v", (lambda v, slot=slot, half=half, bank=bank:
                                v.tensor_copy(
                                    ngFs[:, slot, half * 4:half * 4 + 4, :],
                                    ps[bank][:, 0:512]
                                    .rearrange("p (c t) -> p c t", c=4))),
                          reads=[f"psb{bank}"], writes=["bz"])
            ngF5 = zc.rearrange("p (s c q t) -> p s c q t", s=2, c=NB, q=2)
            sc.op("v", lambda v: v.tensor_copy(
                ngtots.rearrange("p (c s q) -> p c s q", c=NB, s=2),
                ngF5[:, :, :, :, 63:64].transpose([0, 2, 1, 3, 4])
                .rearrange("p c s q one -> p c s (q one)")),
                reads=["bz"], writes=["bngt"])
            sc.op("s", lambda e: e.activation(
                bD[:, 0:32], ngtots[:, 0:32], AF.Exp, scale=-1.0),
                reads=["bngt"], writes=["bD"])
            sc.op("s", lambda e: e.activation(
                bent[:, 0:2048].rearrange("p (c s t) -> p s c t", s=2, c=NB),
                ngFs[:, :, :, :], AF.Exp, scale=-1.0,
                bias=float(SCALAR_CONSTS["lnscale_b"])),
                reads=["bz"], writes=["bent"])
            for o in range(NB):
                wv, wn = wload("bWq", [o])
                bank = 2 + o % 2
                sc.op("t", (lambda t, wv=wv, bank=bank: [
                    t.matmul(ps[bank][:, 0:256], wv[:, c, 0:128],
                             xnv3[:, c, :], start=(c == 0), stop=(c == NB - 1))
                    for c in range(NB)][-1]),
                    reads=[wn, "xn"], writes=[f"psb{bank}"])
                sc.op("s", (lambda e, bank=bank, o=o: e.activation(
                    bqv[:, o, :], ps[bank][:, 0:256], AF.Silu)),
                    reads=[f"psb{bank}"], writes=["bq"])
            sc.op("v", lambda v: v.tensor_tensor(
                bq[:, 0:2048], bq[:, 0:2048], bent[:, 0:2048], A.mult),
                reads=["bent"], writes=["bq"])
            if tk < 2:
                for cl in range(4):
                    cu = tk * 4 + cl
                    sc.op("s", lambda e: e.activation(
                        bngp[:, 0:8], bngc[:, 0:8], AF.Exp, scale=-1.0),
                        reads=["bngc"], writes=["bngp"])
                    qp = bqps[cu % 2]
                    qn = f"bqps{cu % 2}"
                    sc.op("v", (lambda v, cl=cl, qp=qp: v.tensor_tensor(
                        qp[:, 0:512].rearrange("p (c t) -> p c t", c=NB),
                        bqv[:, :, cl * 64:(cl + 1) * 64],
                        bngp[:, 0:8].unsqueeze(2).broadcast_to([128, 8, 64]),
                        A.mult)), reads=["bq", "bngp"], writes=[qn])
                    dma("y", scr["qp0"].ap()[cu], qp[:, 0:512], reads=[qn], writes=["scrqp0"])
                    sc.op("v", (lambda v, cl=cl: v.tensor_tensor(
                        bngc[:, 0:8], bngc[:, 0:8],
                        ngtv[:, :, cl:cl + 1], A.add)),
                        reads=["bngt"], writes=["bngc"])
            for slot in range(2):
                obank = [2 * slot, 2 * slot + 1]
                for h in range(HEADS_B):
                    pa = 4 + h % 2
                    sc.op("t", (lambda t, slot=slot, h=h, pa=pa: t.matmul(
                        ps[pa][:, 0:128], bkfv[:, h, slot * 128:(slot + 1) * 128],
                        bqv[:, h, slot * 128:(slot + 1) * 128],
                        start=True, stop=True)),
                        reads=["bkf", "bq"], writes=[f"psb{pa}"])
                    sc.op("v", (lambda t_, h=h, pa=pa: t_.tensor_tensor(
                        bA[h % 2][:, 0:128], ps[pa][:, 0:128],
                        cview("amask").bitcast(BF16), A.mult)),
                        reads=[f"psb{pa}", "const"], writes=[f"bA{h % 2}"])
                    ob = obank[h // 4]
                    sc.op("t", (lambda t, slot=slot, h=h, ob=ob: t.matmul(
                        ps[ob][:, (h % 4) * 128:(h % 4) * 128 + 128],
                        bvtv[:, slot, h * 128:(h + 1) * 128],
                        bA[h % 2][:, 0:128], start=(h % 4 == 0), stop=False)),
                        reads=[f"bA{h % 2}", "bvt"], writes=[f"psb{ob}"])
                for cpar in range(2):
                    cl = slot * 2 + cpar
                    cu = tk * 4 + cl
                    sb = cpar * 64
                    for hg in range(2):
                        def mminter(t, slot=slot, cpar=cpar, sb=sb, hg=hg,
                                    ob=obank[hg]):
                            last = None
                            for h in range(hg * 4, hg * 4 + 4):
                                last = t.matmul(
                                    ps[ob][:, (h % 4) * 128 + sb:
                                           (h % 4) * 128 + sb + 64],
                                    S_state[:, h * 128:(h + 1) * 128],
                                    bqv[:, h, slot * 128 + sb:
                                        slot * 128 + sb + 64],
                                    start=False,
                                    stop=(cpar == 1 and h % 4 == 3))
                            return last
                        sc.op("t", mminter, reads=[f"S{'ab'[hg]}", "bq"],
                              writes=[f"psb{obank[hg]}"])
                    for fh in range(2):
                        sc.op("t", (lambda t, slot=slot, cpar=cpar, fh=fh:
                                    t.matmul(
                                        ps[6 + fh][0:128, 0:512],
                                        cview("pind2T").bitcast(BF16)
                                        [0:2, cpar * 128:cpar * 128 + 128],
                                        ebb(slot, fh), start=True, stop=True)),
                              reads=["ebb", "const"], writes=[f"psb{6 + fh}"])
                        sc.op("v", (lambda v, slot=slot, fh=fh:
                                    v.tensor_tensor(
                                        kubuf[:, fh * 512:(fh + 1) * 512],
                                        bktv[:, slot, fh * 512:(fh + 1) * 512],
                                        ps[6 + fh][:, 0:512], A.mult)),
                              reads=[f"psb{6 + fh}", "bkt"],
                              writes=[f"ku{fh}"])
                    for hg in range(2):
                        def mmu(t, slot=slot, hg=hg):
                            last = None
                            for h in range(hg * 4, hg * 4 + 4):
                                last = t.matmul(
                                    ps[6 + hg][:, (h % 4) * 128:
                                               (h % 4) * 128 + 128],
                                    kubuf[:, h * 128:(h + 1) * 128],
                                    bvtv[:, slot, h * 128:(h + 1) * 128],
                                    start=True, stop=True)
                            return last
                        sc.op("t", mmu, reads=[f"ku{hg}", "bvt"],
                              writes=[f"psb{6 + hg}"])
                        def scan(e, cl=cl, hg=hg):
                            last = None
                            for h in range(hg * 4, hg * 4 + 4):
                                last = e.scalar_tensor_tensor(
                                    S_state[:, h * 128:(h + 1) * 128],
                                    S_state[:, h * 128:(h + 1) * 128],
                                    bD[:, h * 4 + cl:h * 4 + cl + 1],
                                    ps[6 + hg][:, (h % 4) * 128:
                                               (h % 4) * 128 + 128],
                                    A.mult, A.add)
                            return last
                        sc.op("v", scan,
                              reads=[f"psb{6 + hg}", "bD"],
                              writes=[f"S{'ab'[hg]}"])
                for bi in range(2):
                    ob = obank[bi]
                    sc.op("v" if bi == 0 else "s",
                          (lambda e, ob=ob, bi=bi, slot=slot, tk=tk,
                           sv=(bi == 1): (e.copy if sv else e.tensor_copy)(
                              bogv[:, bi * 4:bi * 4 + 4,
                                   (tk % 2) * 256 + slot * 128:
                                   (tk % 2) * 256 + slot * 128 + 128],
                              ps[ob][:, 0:512]
                              .rearrange("p (c t) -> p c t", c=4))),
                          reads=[f"psb{ob}"], writes=["bog"])
            if tk % 2 == 1:
                g0 = (tk - 1) * 256
                if g0 == 0:
                    dma("y", scr["og0"].ap()[:, :], bog[:, 0:4096], reads=["bog"], writes=["scrog0"])
                else:
                    emit_group_out(g0)
        flagC = cview("flags")[:, 0:1]
        flagU = cview("flags")[:, 1:2]
        sc.op("v", lambda v: v.tensor_scalar(
            ccs2[:, 0:1024], S_state[:, 0:1024], flagC, None, A.mult),
            reads=["Sa", "Sb", "const"], writes=["bz"])
        dma("y", cc2_in.ap(), ccs2[:, 0:1024], reads=["bz"], writes=["cc2i"])
        if no_coll:
            dma("y", cc2_out.ap(), cc2_in.ap(), reads=["cc2i"], writes=["cc2o"])
        else:
            sc.op("g", lambda g: g.collective_compute(
                "AllReduce", A.add, replica_groups=RGROUPS,
                ins=[cc2_in.ap().opt()], outs=[cc2_out.ap().opt()]),
                reads=["cc2i"], writes=["cc2o"])
        dma("y", ccs2[:, 0:1024], cc2_out.ap(), reads=["cc2o"], writes=["bz"])
        sc.op("v", lambda v: v.tensor_scalar(
            bsA[:, 0:1024], ccs2[:, 0:1024], flagU, None, A.mult),
            reads=["bz", "const"], writes=["bsA", "ku0", "ku1"])
        qpb = arena[:, 8192:10240].bitcast(BF16)
        qpv = qpb.rearrange("p (u c t) -> p u c t", u=8, c=NB)
        dma("y", bog[:, 0:4096], scr["og0"].ap()[:, :], reads=["scrog0"], writes=["bog"])
        for cu in range(8):
            dma("y", qpb[:, cu * 512:(cu + 1) * 512], scr["qp0"].ap()[cu],
                reads=["scrqp0"], writes=["bkf", "bvt"])
        for h in range(HEADS_B):
            sc.op("t", (lambda t, h=h: t.matmul(
                ps[6][:, 0:512], bsA[:, h * 128:(h + 1) * 128],
                qpv[:, :, h, :], start=True, stop=True)),
                reads=["bsA", "bkf", "bvt"], writes=["psb6"])
            sc.op("v", (lambda v, h=h: v.tensor_tensor(
                bogv[:, h, :], bogv[:, h, :], ps[6][:, 0:512], A.add)),
                reads=["psb6"], writes=["bog"])
        emit_group_out(0)
        sc.fence()

    def emit_ffn():
        h_bf = ABf(0, 16384)
        hvb = h_bf.rearrange("p (c t) -> p c t", c=NH)
        xi = ABf(32, 4096)
        xivb = xi.rearrange("p (c t) -> p c t", c=NB)
        qt = AFr(40, 2048)
        qtmp = [AFr(48, 512), AFr(50, 512)]
        wb1 = [ABf(52, 1024), ABf(54, 1024)]
        wb2 = [ABf(56, 2048), ABf(60, 2048)]
        emit_ln_stats(16.0)
        for tk in range(NT512):
            pS = ps[0 if tk % 2 == 0 else 6]
            pB = ps[1 if tk % 2 == 0 else 7]
            nS = "psb0" if tk % 2 == 0 else "psb6"
            nB = "psb1" if tk % 2 == 0 else "psb7"
            rb = rowb[tk % 2]
            rbn = f"rb{tk % 2}"
            dma("y", rb, stats[tk:tk + 1, 0:1024], reads=["stats"], writes=[rbn])
            obase = 32 * (1 + tk % 2)
            orow = ones[obase:obase + 1, 0:128]
            sc.op("t", (lambda t, pS=pS, rb=rb, orow=orow: t.matmul(
                pS[0:128, 0:512], orow, rb[0:1, 0:512],
                start=True, stop=True)), reads=[rbn, "const"], writes=[nS])
            sc.op("t", (lambda t, pB=pB, rb=rb, orow=orow: t.matmul(
                pB[0:128, 0:512], orow, rb[0:1, 512:1024],
                start=True, stop=True)), reads=[rbn, "const"], writes=[nB])
            t0 = tk * 512
            for th in range(2):
                sl = slice(t0 + th * 256, t0 + th * 256 + 256)
                xsrc = xv[:, :, sl]
                sc.op("v", (lambda v, xsrc=xsrc, pS=pS, th=th: v.scalar_tensor_tensor(
                    qt[:, 0:2048], xsrc, 1.0,
                    pS[:, th * 256:(th + 1) * 256].unsqueeze(1)
                    .broadcast_to([128, 8, 256]),
                    A.mult, A.mult)),
                    reads=xnames(t0, t0 + 512) + [nS], writes=["qtarea"])
                sc.op("v", (lambda v, pB=pB, th=th: v.scalar_tensor_tensor(
                    qt[:, 0:2048], qt[:, 0:2048], 1.0,
                    pB[:, th * 256:(th + 1) * 256].unsqueeze(1)
                    .broadcast_to([128, 8, 256]),
                    A.mult, A.add)),
                    reads=[nB], writes=["qtarea"])
                sc.op("v", lambda v: v.tensor_scalar(
                    qt[:, 0:2048], qt[:, 0:2048], -128.0, 127.0, A.max, A.min),
                    reads=["qtarea"], writes=["qtarea"])
                xdst = xivb[:, :, th * 256:(th + 1) * 256]
                sc.op("v", (lambda v, xdst=xdst: v.tensor_scalar(
                    xdst, qt[:, 0:2048], MAGIC, -MAGIC, A.add, A.add)),
                    reads=["qtarea"], writes=["xi"])
            for o1 in range(NH):
                wb = wb1[o1 % 2]
                wn = f"wb1_{o1 % 2}"
                wvb = wb.rearrange("p (c o) -> p c o", c=NB)
                dma("y", wvb, scr["w1q"].ap()[o1].transpose([1, 0, 2]),
                    writes=[wn])
                bank = 2 + o1 % 2
                bn = f"psb{bank}"
                sc.op("t", (lambda t, wvb=wvb, bank=bank: [
                    t.matmul(ps[bank][:, 0:512], wvb[:, c, :], xivb[:, c, :],
                             start=(c == 0), stop=(c == NB - 1))
                    for c in range(NB)][-1]),
                    reads=[wn, "xi"], writes=[bn], cost=3600.0)
                qm = qtmp[o1 % 2]
                qmn = f"qtmp{o1 % 2}"
                e1 = "v" if o1 % 2 == 0 else "s"
                if e1 == "v":
                    sc.op("v", (lambda v, bank=bank, qm=qm, o1=o1: v.tensor_scalar(
                        qm[:], ps[bank][:, 0:512], b1q16[:, o1:o1 + 1], 0.0,
                        A.add, A.max)), reads=[bn, "b1q16"], writes=[qmn])
                else:
                    sc.op("s", (lambda e, bank=bank, qm=qm, o1=o1: e.activation(
                        qm[:], ps[bank][:, 0:512], AF.Relu,
                        bias=b1q16[:, o1:o1 + 1])), reads=[bn, "b1q16"],
                        writes=[qmn])
                sc.op("v", (lambda v, qm=qm: v.tensor_scalar(
                    qm[:], qm[:], 127.0, MAGIC, A.min, A.add)),
                    reads=[qmn], writes=[qmn])
                e3 = "s" if o1 % 2 == 0 else "v"
                if e3 == "s":
                    sc.op("s", (lambda e, qm=qm, o1=o1: e.activation(
                        hvb[:, o1, :], qm[:], AF.Copy, bias=-MAGIC)),
                        reads=[qmn], writes=[f"h{o1}"])
                else:
                    sc.op("v", (lambda v, qm=qm, o1=o1: v.tensor_scalar(
                        hvb[:, o1, :], qm[:], -MAGIC, None, A.add)),
                        reads=[qmn], writes=[f"h{o1}"])
            for o2 in range(NB):
                bank = 4 + o2 % 2
                bn = f"psb{bank}"
                for hh in range(2):
                    wb = wb2[(o2 * 2 + hh) % 2]
                    wn = f"wb2_{(o2 * 2 + hh) % 2}"
                    wvb = wb.rearrange("p (c o) -> p c o", c=16)
                    dma("y", wvb,
                        scr["w2q"].ap()[o2, hh * 16:(hh + 1) * 16]
                        .transpose([1, 0, 2]), writes=[wn])
                    sc.op("t", (lambda t, wvb=wvb, bank=bank, hh=hh: [
                        t.matmul(ps[bank][:, 0:512], wvb[:, c, :],
                                 hvb[:, hh * 16 + c, :],
                                 start=(hh == 0 and c == 0),
                                 stop=(hh == 1 and c == 15))
                        for c in range(16)][-1]),
                        reads=[wn] + [f"h{hh * 16 + c}" for c in range(16)],
                        writes=[bn], cost=7200.0)
                sc.op("v", (lambda v, bank=bank, o2=o2, t0=t0: v.scalar_tensor_tensor(
                    xv[:, o2, t0:t0 + 512], ps[bank][:, 0:512],
                    b2q[:, o2:o2 + 1], xv[:, o2, t0:t0 + 512],
                    A.add, A.add)),
                    reads=[bn, "b2q"] + xnames(t0, t0 + 512),
                    writes=xnames(t0, t0 + 512))
        sc.fence()

    if "attn_a" in phases:
        emit_attn_a()
    if "ffn1" in phases:
        emit_ffn()
    if "attn_b" in phases:
        emit_attn_b()
    if "ffn2" in phases:
        emit_ffn()

    sbuf_store = [AFr(0, 1024), AFr(4, 1024)]
    store_ops = []
    for i in range(NT128):
        tb = sbuf_store[i % 2]
        tbn = f"stb{i % 2}"
        for h in range(2):
            bank = 2 + h
            sc.op("t", (lambda t, i=i, h=h, bank=bank: [
                t.transpose(ps[bank][:, c * 128:(c + 1) * 128],
                            xv[:, 4 * h + c, i * 128:(i + 1) * 128], ident)
                for c in range(4)][-1]),
                reads=[f"x_{i}", "const"], writes=[f"psb{bank}"])
            sc.op("v" if h == 0 else "s",
                  (lambda e, h=h, bank=bank, tb=tb, s=(h == 1):
                   (e.copy if s else e.tensor_copy)(
                       tb[:, h * 512:(h + 1) * 512], ps[bank][:, 0:512])),
                  reads=[f"psb{bank}"], writes=[tbn])
        store_ops.append(
            dma("y", y_out[i * 128:(i + 1) * 128, :], tb[:], reads=[tbn]))

    sc.op("g", lambda g: g.nop(), extra=store_ops)

    sc.reorder()
    sc.finalize()
    with nc.Block() as block:
        @block.tensor
        def _(t):
            sc.run("t", t)

        @block.vector
        def _(v):
            sc.run("v", v)

        @block.scalar
        def _(s):
            sc.run("s", s)

        @block.gpsimd
        def _(g):
            sc.run("g", g)

        @block.sync
        def _(y):
            sc.run("y", y)

    return nc


_NC_CACHE = {}


def get_nc(T, n_cores=8):
    key = (T, n_cores)
    if key not in _NC_CACHE:
        _NC_CACHE[key] = build_nc(T, n_cores=n_cores)
    return _NC_CACHE[key]


def make_wmap(inputs):
    wmap = {
        "aWq": inputs["L0_attn_a_Wq"], "aWk": inputs["L0_attn_a_Wk"],
        "aWv": inputs["L0_attn_a_Wv"], "aWo": inputs["L0_attn_a_Wo"],
        "abo": np.asarray(inputs["L0_attn_a_bo"], np.float32).reshape(1, D),
        "bWq": inputs["L0_attn_b_Wq"], "bWf": inputs["L0_attn_b_Wf"],
        "bWi": inputs["L0_attn_b_Wi"], "bWo": inputs["L0_attn_b_Wo"],
        "bgw": np.asarray(inputs["L0_attn_b_g_weight"], np.float32).reshape(1, D),
        "fw1": inputs["L0_ffn_w1"],
        "fb1": np.asarray(inputs["L0_ffn_b1"], np.float32).reshape(1, HID),
        "fw2": inputs["L0_ffn_w2"],
        "fb2": np.asarray(inputs["L0_ffn_b2"], np.float32).reshape(1, D),
    }
    return {k: np.ascontiguousarray(np.asarray(v, np.float32))
            for k, v in wmap.items()}


def kernel(**inputs):
    x = np.asarray(inputs["L0_attn_a_x"], np.float32)
    Bz, S, _ = x.shape
    T = S // 2
    nc = get_nc(T, 2 * Bz)
    wmap = make_wmap(inputs)
    consts = [make_consts(False), make_consts(True)]
    zero_halo = np.zeros((WINDOW, D), np.float32)
    in_maps = []
    for b in range(Bz):
        for h in range(2):
            m = dict(wmap)
            m["x_in"] = np.ascontiguousarray(x[b, h * T:(h + 1) * T, :])
            m["halo_in"] = (np.ascontiguousarray(x[b, T - WINDOW:T, :]) if h == 1
                            else zero_halo)
            m["consts"] = consts[h]
            in_maps.append(m)
    res = run_bass_kernel_spmd(nc, in_maps, core_ids=list(range(2 * Bz)))
    out = np.empty((Bz, S, D), np.float32)
    for b in range(Bz):
        for h in range(2):
            out[b, h * T:(h + 1) * T, :] = res.results[2 * b + h]["y_out"]
    return out
